# version 06
# baseline (speedup 1.0000x reference)
import numpy as np
from contextlib import ExitStack

import concourse.bass as bass
import concourse.mybir as mybir
from concourse.bass_utils import run_bass_kernel_spmd
from concourse.aot_env import allow_ici_all_to_all

F32 = mybir.dt.float32
BF16 = mybir.dt.bfloat16
AF = mybir.ActivationFunctionType
ALU = mybir.AluOpType
AX = mybir.AxisListType

NCORE = 8


class Cfg:
    def __init__(self, B=2, S=4096, D=2048, I_MOE=1408, I_SH=5632, E=16):
        self.B, self.S, self.D = B, S, D
        self.I_MOE, self.I_SH, self.E = I_MOE, I_SH, E
        self.N = B * S
        self.T = self.N // NCORE
        self.NT = self.T // 128
        self.CC = D // 128
        self.HA = D // 128
        self.SLOTS_A = B * self.HA // NCORE // B
        self.DHB = 64
        self.QH = D // self.DHB
        self.KVH = NCORE
        self.G = self.QH // self.KVH
        self.DKV = self.KVH * self.DHB
        self.QT = S // 128
        self.IC = I_MOE // 128
        self.ICS = I_SH // 128
        self.NSRC = S // self.T
        self.IBLK = []
        o = 0
        while o < I_MOE:
            w = min(512, I_MOE - o)
            self.IBLK.append((o, w))
            o += w
        self.SSEG = []
        o = 0
        while o < I_SH:
            w = min(1536, I_SH - o)
            self.SSEG.append((o, w))
            o += w


def cdiv(a, b):
    return (a + b - 1) // b



class _TT:
    def __init__(self, eng):
        self.e = eng
        self.last = None

    def matmul(self, *a, **k):
        self.last = self.e.matmul(*a, **k)
        return self.last

    def transpose(self, *a, **k):
        self.last = self.e.transpose(*a, **k)
        return self.last

    def wait_ge(self, *a):
        return self.e.wait_ge(*a)

    def inc(self, sem):
        self.last.then_inc(sem, 1)


_UID = [0]


def _u(s):
    _UID[0] += 1
    return f"{s}_{_UID[0]}"


class VstgCt:
    def __init__(self):
        self.n = 0

    def __getitem__(self, i):
        return self.n

    def __setitem__(self, i, v):
        self.n = v


class Sem:
    def __init__(self, E_, nc, name):
        self.h = E_(nc.semaphore(name))
        self.n = 0


def build(c: Cfg):
    nc = bass.Bass("TRN2", target_bir_lowering=False)
    T, NT, CC, D, S = c.T, c.NT, c.CC, c.D, c.S
    TH = cdiv(T, 512)
    thw = [min(512, T - 512 * i) for i in range(TH)]
    SCALE_A = float(128.0 ** -0.25)
    SCALE_B = float(c.DHB ** -0.25)

    x_in = nc.dram_tensor("x_shard", [T, D], F32, kind="ExternalInput")
    out_t = nc.dram_tensor("out_shard", [T, D], F32, kind="ExternalOutput")
    wqkv = nc.dram_tensor("Wqkv", [D, 3 * D], F32, kind="ExternalInput")
    bqkv = nc.dram_tensor("bqkv", [3 * D], F32, kind="ExternalInput")
    wo_a = nc.dram_tensor("Wo_a", [D, D], F32, kind="ExternalInput")
    bo_a = nc.dram_tensor("bo_a", [D], F32, kind="ExternalInput")
    wq_b = nc.dram_tensor("Wq_b", [D, D], F32, kind="ExternalInput")
    wk_b = nc.dram_tensor("Wk_b", [D, c.DKV], F32, kind="ExternalInput")
    wv_b = nc.dram_tensor("Wv_b", [D, c.DKV], F32, kind="ExternalInput")
    wo_b = nc.dram_tensor("Wo_b", [D, D], F32, kind="ExternalInput")
    bo_b = nc.dram_tensor("bo_b", [D], F32, kind="ExternalInput")
    router_w = nc.dram_tensor("router_w", [c.E, D], F32, kind="ExternalInput")
    w_gate = nc.dram_tensor("w_gate", [c.E, D, c.I_MOE], F32, kind="ExternalInput")
    w_up = nc.dram_tensor("w_up", [c.E, D, c.I_MOE], F32, kind="ExternalInput")
    w_down = nc.dram_tensor("w_down", [c.E, c.I_MOE, D], F32, kind="ExternalInput")
    sh_gate = nc.dram_tensor("sh_gate", [D, c.I_SH], F32, kind="ExternalInput")
    sh_up = nc.dram_tensor("sh_up", [D, c.I_SH], F32, kind="ExternalInput")
    sh_down = nc.dram_tensor("sh_down", [c.I_SH, D], F32, kind="ExternalInput")
    sh_gate_w = nc.dram_tensor("sh_gate_w", [D, 1], F32, kind="ExternalInput")
    t_ident = nc.dram_tensor("t_ident", [128, 128], BF16, kind="ExternalInput")
    t_maskT = nc.dram_tensor("t_maskT", [128, 128], BF16, kind="ExternalInput")
    t_maskA = nc.dram_tensor("t_maskA", [128, 128], BF16, kind="ExternalInput")
    t_cvec = nc.dram_tensor("t_cvec", [c.SLOTS_A, 3, S], BF16, kind="ExternalInput")
    t_slopei = nc.dram_tensor("t_slopei", [128, c.SLOTS_A * c.QT], BF16, kind="ExternalInput")
    t_ctab = nc.dram_tensor("t_ctab", [128, T], BF16, kind="ExternalInput")
    t_stab = nc.dram_tensor("t_stab", [128, T], BF16, kind="ExternalInput")
    t_pswap = nc.dram_tensor("t_pswap", [128, 128], BF16, kind="ExternalInput")
    t_identf = nc.dram_tensor("t_identf", [2 * D // 128, 2 * D // 128], F32, kind="ExternalInput")

    WCTA = c.SLOTS_A * 128
    WCTB = c.G * 64
    qaA_i = nc.dram_tensor("qaA_i", [c.SLOTS_A * 128 * NCORE, T], BF16)
    qaA_o = nc.dram_tensor("qaA_o", [c.SLOTS_A * 128 * NCORE, T], BF16)
    kaA_i = nc.dram_tensor("kaA_i", [c.SLOTS_A * 128 * NCORE, T], BF16)
    kaA_o = nc.dram_tensor("kaA_o", [c.SLOTS_A * 128 * NCORE, T], BF16)
    vaA_i = nc.dram_tensor("vaA_i", [NCORE * T, WCTA], BF16)
    vaA_o = nc.dram_tensor("vaA_o", [NCORE * T, WCTA], BF16)
    caA_i = nc.dram_tensor("caA_i", [NCORE * T, WCTA], BF16)
    caA_o = nc.dram_tensor("caA_o", [NCORE * T, WCTA], BF16)
    qaB_i = nc.dram_tensor("qaB_i", [WCTB * NCORE, T], BF16)
    qaB_o = nc.dram_tensor("qaB_o", [WCTB * NCORE, T], BF16)
    kaB_i = nc.dram_tensor("kaB_i", [64 * NCORE, T], BF16)
    kaB_o = nc.dram_tensor("kaB_o", [64 * NCORE, T], BF16)
    vaB_i = nc.dram_tensor("vaB_i", [NCORE * T, 64], BF16)
    vaB_o = nc.dram_tensor("vaB_o", [NCORE * T, 64], BF16)
    caB_i = nc.dram_tensor("caB_i", [NCORE * T, WCTB], BF16)
    caB_o = nc.dram_tensor("caB_o", [NCORE * T, WCTB], BF16)

    st_ = ExitStack()
    E_ = st_.enter_context

    max_act = max(c.IC * T, max(w // 128 for _, w in c.SSEG) * T)
    SCRB = max(CC * T, 3 * S, max_act + 8 * 512, 2 * S + (S // 128) * 64 + 64 + S)
    SILOFF = SCRB - 8 * 512
    KOFF = 2 * S + (S // 128) * 64
    x_sb = E_(nc.sbuf_tensor("x_sb", [128, NT * D], F32))
    scrA = E_(nc.sbuf_tensor("scrA", [128, CC * T], BF16))
    scrB = E_(nc.sbuf_tensor("scrB", [128, SCRB], BF16))
    w0 = E_(nc.sbuf_tensor("w0", [128, 8192], BF16))
    w1 = E_(nc.sbuf_tensor("w1", [128, 8192], BF16))
    wslots = [w0, w1]
    stg = [E_(nc.sbuf_tensor(f"stg{i}", [128, max(D, T)], BF16)) for i in range(2)]
    vstg = [E_(nc.sbuf_tensor("vstg0", [128, 512], BF16))] * 2
    psb = E_(nc.sbuf_tensor("psb", [128, 2048], BF16))
    ptsb = E_(nc.sbuf_tensor("ptsb", [128, 1024], BF16))
    NSIL = 8
    def silsl(i, wdt):
        return scrB[:, SILOFF + 512 * i:SILOFF + 512 * i + wdt]
    rowscr = E_(nc.sbuf_tensor("rowscr", [1, S], BF16))
    knm = E_(nc.sbuf_tensor("knm", [1, 1], F32))
    rcol = E_(nc.sbuf_tensor("rcol", [128, c.QT], F32))
    negR = E_(nc.sbuf_tensor("negR", [128, c.QT], F32))
    lrec = E_(nc.sbuf_tensor("lrec", [128, 1], F32))
    ln_stats = E_(nc.sbuf_tensor("ln_stats", [128, 6 * (D // 512)], F32))
    ln_mv = E_(nc.sbuf_tensor("ln_mv", [128, 2], F32))
    ln_negm = E_(nc.sbuf_tensor("ln_negm", [128, 1], F32))
    ln_std = E_(nc.sbuf_tensor("ln_std", [128, 1], F32))
    ln_rstd = E_(nc.sbuf_tensor("ln_rstd", [128, 1], F32))
    expv = E_(nc.sbuf_tensor("expv", [128, c.E], F32))
    nmx = E_(nc.sbuf_tensor("nmx", [128, 1], F32))
    mml = E_(nc.sbuf_tensor("mml", [128, 6], F32))
    ge01 = E_(nc.sbuf_tensor("ge01", [128, c.E], F32))
    mskv = E_(nc.sbuf_tensor("mskv", [128, c.E], F32))
    comb = E_(nc.sbuf_tensor("comb", [128, NT * c.E], F32))
    sg_sb = E_(nc.sbuf_tensor("sg_sb", [128, NT], F32))
    sglog = E_(nc.sbuf_tensor("sglog", [128, NT], F32))
    EEP = (c.E + 2) // 2 * 2
    routerT = E_(nc.sbuf_tensor("routerT", [128, EEP * CC], BF16))
    bias_rows = E_(nc.sbuf_tensor("bias_rows", [65, 2 * D], BF16))
    bqkv_fm = E_(nc.sbuf_tensor("bqkv_fm", [128, 2 * CC], F32))
    tmpb = E_(nc.sbuf_tensor("tmpb", [65, 128], F32))
    bqkv_raw = E_(nc.sbuf_tensor("bqkv_raw", [2 * CC, 128], F32))
    identf = E_(nc.sbuf_tensor("identf", [2 * CC, 2 * CC], F32))
    ctab = E_(nc.sbuf_tensor("ctab", [128, T], BF16))
    stab = E_(nc.sbuf_tensor("stab", [128, T], BF16))
    ident = E_(nc.sbuf_tensor("ident", [128, 128], BF16))
    maskT = E_(nc.sbuf_tensor("maskT", [128, 128], BF16))
    maskA = E_(nc.sbuf_tensor("maskA", [128, 128], BF16))
    slopei = E_(nc.sbuf_tensor("slopei", [128, c.SLOTS_A * c.QT], BF16))
    cvec_sb = E_(nc.sbuf_tensor("cvec_sb", [3, S], BF16))
    pswap = E_(nc.sbuf_tensor("pswap", [128, 128], BF16))
    ones1 = E_(nc.sbuf_tensor("ones1", [1, 32], BF16))
    ones65 = E_(nc.sbuf_tensor("ones65", [65, 128], BF16))
    ones3 = E_(nc.sbuf_tensor("ones3", [3, 128], BF16))

    TQ = Sem(E_, nc, "TQ")
    VQ = Sem(E_, nc, "VQ")
    SQ = Sem(E_, nc, "SQ")
    PQ = Sem(E_, nc, "PQ")
    CCS = Sem(E_, nc, "CCS")
    SX = Sem(E_, nc, "SX")
    SCT = Sem(E_, nc, "SCT")
    GW = [Sem(E_, nc, f"GW{i}") for i in range(len(wslots))]
    SST = [Sem(E_, nc, f"SST{i}") for i in range(2)]
    _svt = Sem(E_, nc, "SVT0")
    SVT = [_svt, _svt]
    SQB = Sem(E_, nc, "SQB")
    SKB = Sem(E_, nc, "SKB")
    SVB = Sem(E_, nc, "SVB")
    SCX = Sem(E_, nc, "SCX")
    SCV = Sem(E_, nc, "SCV")
    SOUT = Sem(E_, nc, "SOUT")
    GB = Sem(E_, nc, "GB")
    GB2 = Sem(E_, nc, "GB2")

    block = E_(nc.Block())

    ps_free = {}
    wfree = [None, None]
    slot_rr = [0]
    stg_ct = [0, 0]
    vstg_ct = VstgCt()
    psb_free = [(TQ, 0), (TQ, 0)]
    ptsb_free = [(TQ, 0), (TQ, 0)]
    scrB_rel = [0]
    scrB_vrel = [0]
    cvec_rel = [0]
    ctx_region_rel = [0]

    SCBUF = []

    def sc(f):
        SCBUF.append(f)

    def sc_flush():
        if not SCBUF:
            return
        ops = SCBUF[:]
        SCBUF.clear()

        @block.scalar
        def _(s_):
            for f in ops:
                f(s_)

    def init_free(ps):
        ps_free[id(ps)] = (TQ, 0)

    def mark_free(ps, sem):
        ps_free[id(ps)] = (sem, sem.n)

    def next_slot():
        s = slot_rr[0]
        slot_rr[0] = (s + 1) % len(wslots)
        return s

    def wview(slot_i, a, b):
        return bass.AP(wslots[slot_i], 0, [[8192, 128], [b, a], [1, b]])

    def wload(slot_i, in_ap, a, b):
        sem = GW[slot_i]
        fr = wfree[slot_i]

        @block.gpsimd
        def _(g):
            if fr is not None:
                g.wait_ge(fr[0].h, fr[1])
            g.dma_start(out=wview(slot_i, a, b), in_=in_ap).then_inc(sem.h, 16)

        sem.n += 16
        return sem.n

    def release_slot(slot_i, sem):
        wfree[slot_i] = (sem, sem.n)

    @block.sync
    def _(sy):
        sy.dma_start(out=bass.AP(x_sb, 0, [[NT * D, 128], [D, NT], [1, D]]),
                     in_=x_in.ap().rearrange("(n p) d -> p n d", p=128)).then_inc(SX.h, 16)
        for dst, src in [
            (ident[:, :], t_ident.ap()),
            (maskT[:, :], t_maskT.ap()),
            (maskA[:, :], t_maskA.ap()),
            (slopei[:, :], t_slopei.ap()),
            (ctab[:, :], t_ctab.ap()),
            (stab[:, :], t_stab.ap()),
            (pswap[:, :], t_pswap.ap()),
            (identf[:, :], t_identf.ap()),
            (bqkv_raw[:, :], bqkv.ap().rearrange("(r p) -> r p", p=128)[0:2 * CC, :]),
        ]:
            sy.dma_start(out=dst, in_=src).then_inc(SCT.h, 16)
    SX.n += 16
    SCT.n += 16 * 9

    bsrc = {0: bqkv.ap().rearrange("(t d) -> t d", t=3)[2:3, :],
            32: bo_a.ap().rearrange("(t d) -> t d", t=1),
            64: bo_b.ap().rearrange("(t d) -> t d", t=1)}
    lo_rel = [0]

    @block.gpsimd
    def _(g):
        for r, ap_ in bsrc.items():
            g.dma_start(out=bias_rows[r:r + 1, 0:D], in_=ap_).then_inc(GB.h, 16)
        g.dma_start(out=stg[0][0:c.E, 0:D], in_=router_w.ap()).then_inc(GB.h, 16)
        g.dma_start(out=stg[0][c.E:c.E + 1, 0:D],
                    in_=sh_gate_w.ap().rearrange("d o -> o d")).then_inc(GB.h, 16)
        g.wait_ge(SCT.h, SCT.n)
    GB.n += 16 * 5

    for ch in range(D // 128):
        @block.gpsimd
        def _(g):
            if lo_rel[0]:
                g.wait_ge(VQ.h, lo_rel[0])
            for r, ap_ in bsrc.items():
                g.dma_start(out=tmpb[r:r + 1, 0:128],
                            in_=ap_[:, 128 * ch:128 * (ch + 1)]).then_inc(GB2.h, 16)
        GB2.n += 16 * 3
        gb_now = GB2.n

        @block.vector
        def _(v):
            v.wait_ge(GB2.h, gb_now)
            v.wait_ge(GB.h, 16 * 5)
            for r in (0, 32, 64):
                v.wait_ge(VQ.h, VQ.n)
                v.tensor_tensor(tmpb[r:r + 1, 0:128], tmpb[r:r + 1, 0:128],
                                bias_rows[r:r + 1, 128 * ch:128 * (ch + 1)],
                                ALU.subtract).then_inc(VQ.h, 1)
                VQ.n += 1
                v.wait_ge(VQ.h, VQ.n)
                v.tensor_copy(bias_rows[r:r + 1, D + 128 * ch:D + 128 * ch + 128],
                              tmpb[r:r + 1, 0:128]).then_inc(VQ.h, 1)
                VQ.n += 1
        lo_rel[0] = VQ.n

    @block.vector
    def _(v):
        v.memset(ones1[:, :], 1.0)
        v.memset(ones65[:, :], 1.0)
        v.memset(ones3[:, :], 1.0).then_inc(VQ.h, 1)
    VQ.n += 1
    memset_v = VQ.n

    with nc.psum_tensor(_u("ps_bq"), [128, 2 * CC], F32) as ps_bq:
        @block.tensor
        def _(t):
            t = _TT(t)
            t.wait_ge(SCT.h, SCT.n)
            t.matmul(ps_bq[:, :], bqkv_raw[:, :], identf[:, :], start=True, stop=True)
            t.inc(TQ.h)
        TQ.n += 1
        tq_bq = TQ.n

        @block.vector
        def _(v):
            v.wait_ge(TQ.h, tq_bq)
            v.tensor_copy(bqkv_fm[:, :], ps_bq[:, :]).then_inc(VQ.h, 1)
        VQ.n += 1
    bias_split_v = max(VQ.n, lo_rel[0])

    EE = c.E + 1
    with nc.psum_tensor(_u("ps_rt"), [128, EEP * CC], BF16) as ps_rt:
        @block.tensor
        def _(t):
            t = _TT(t)
            t.wait_ge(GB.h, 16 * 5)
            t.wait_ge(SCT.h, SCT.n)
            t.wait_ge(VQ.h, VQ.n)
            for cc in range(CC):
                t.transpose(ps_rt[:, cc * EEP:cc * EEP + EE],
                            stg[0][0:EE, 128 * cc:128 * (cc + 1)],
                            ident[0:EE, 0:EE])
            t.inc(TQ.h)
        TQ.n += 1
        tq_rt = TQ.n

        @block.vector
        def _(v):
            v.wait_ge(TQ.h, tq_rt)
            v.tensor_copy(bass.AP(routerT, 0, [[EEP * CC, 128], [EEP, CC], [1, EE]]),
                          bass.AP(ps_rt, 0, [[EEP * CC, 128], [EEP, CC], [1, EE]])).then_inc(VQ.h, 1)
        VQ.n += 1
    startup_v = VQ.n

    ln_state = {"stg_trel": [0, 0], "pl_vrel": [0, 0]}

    def ln_phase():
        nch = D // 512
        tq0 = TQ.n

        with nc.psum_tensor(_u("ps_ln0"), [128, 1024], BF16) as pl0, \
             nc.psum_tensor(_u("ps_ln1"), [128, 1024], BF16) as pl1:
            pls = [pl0, pl1]
            for tile in range(NT):
                sl = tile % 2

                @block.vector
                def _(v):
                    if tile == 0:
                        v.wait_ge(SX.h, 16)
                        v.wait_ge(TQ.h, tq0)
                        v.wait_ge(VQ.h, VQ.n)
                    v.wait_ge(TQ.h, ln_state["stg_trel"][sl])
                    for ch in range(nch):
                        v.bn_stats(ln_stats[:, 6 * ch:6 * ch + 6],
                                   x_sb[:, tile * D + 512 * ch:tile * D + 512 * (ch + 1)]
                                   ).then_inc(VQ.h, 1)
                    v.wait_ge(VQ.h, VQ.n + nch)
                    v.bn_aggr(ln_mv[:, :], ln_stats[:, :]).then_inc(VQ.h, 1)
                    v.wait_ge(VQ.h, VQ.n + nch + 1)
                    v.tensor_scalar(ln_negm[:, :], ln_mv[:, 0:1], -1.0, None,
                                    ALU.mult).then_inc(VQ.h, 1)
                    v.tensor_scalar(ln_std[:, :], ln_mv[:, 1:2], 1e-5, None,
                                    ALU.add).then_inc(VQ.h, 1)
                VQ.n += nch + 3
                vq_stats = VQ.n

                def f(s_, vq_stats=vq_stats):
                    s_.wait_ge(VQ.h, vq_stats)
                    s_.activation(out=ln_rstd[:, :], in_=ln_std[:, :],
                                  func=AF.Sqrt).then_inc(SQ.h, 1)
                sc(f)
                SQ.n += 1
                sq_std = SQ.n

                @block.vector
                def _(v):
                    v.wait_ge(SQ.h, sq_std)
                    v.wait_ge(SST[sl].h, 16 * stg_ct[sl])
                    v.reciprocal(ln_std[:, :], ln_rstd[:, :]).then_inc(VQ.h, 1)
                    v.wait_ge(VQ.h, VQ.n + 1)
                    v.tensor_scalar(stg[sl][:, 0:D], x_sb[:, tile * D:(tile + 1) * D],
                                    ln_negm[:, :], ln_std[:, :], ALU.add,
                                    ALU.mult).then_inc(VQ.h, 1)
                VQ.n += 2
                vq_apply = VQ.n

                nb = CC // 8
                for bb in range(nb):
                    pi = (tile * nb + bb) % 2
                    pb = pls[pi]

                    @block.tensor
                    def _(t):
                        t = _TT(t)
                        t.wait_ge(VQ.h, max(vq_apply, ln_state["pl_vrel"][pi]))
                        for k in range(8):
                            cc = 8 * bb + k
                            t.transpose(pb[:, 128 * k:128 * (k + 1)],
                                        stg[sl][:, 128 * cc:128 * (cc + 1)], ident[:, :])
                        t.inc(TQ.h)
                    TQ.n += 1
                    tq_tr = TQ.n

                    @block.vector
                    def _(v):
                        v.wait_ge(TQ.h, tq_tr)
                        outap = bass.AP(scrA, 8 * bb * T + tile * 128,
                                        [[CC * T, 128], [T, 8], [1, 128]])
                        v.tensor_copy(outap,
                                      bass.AP(pb, 0, [[1024, 128], [128, 8], [1, 128]])
                                      ).then_inc(VQ.h, 1)
                    VQ.n += 1
                    ln_state["pl_vrel"][pi] = VQ.n
                ln_state["stg_trel"][sl] = TQ.n
        return VQ.n

    def attn_sublayer(isA):
        h_v = ln_phase()

        if isA:
            qa_i, ka_i, va_i = qaA_i, kaA_i, vaA_i
            qa_o, ka_o, va_o = qaA_o, kaA_o, vaA_o
            ca_i, ca_o = caA_i, caA_o
            wct = WCTA
        else:
            qa_i, ka_i, va_i = qaB_i, kaB_i, vaB_i
            qa_o, ka_o, va_o = qaB_o, kaB_o, vaB_o
            ca_i, ca_o = caB_i, caB_o
            wct = WCTB

        with nc.psum_tensor(_u("ps_p0"), [128, 512], F32) as pp0, \
             nc.psum_tensor(_u("ps_p1"), [128, 512], F32) as pp1, \
             nc.psum_tensor(_u("ps_p2"), [128, 512], F32) as pp2, \
             nc.psum_tensor(_u("ps_p3"), [128, 512], F32) as pp3, \
             nc.psum_tensor(_u("ps_w0"), [128, 512], F32) as pw0, \
             nc.psum_tensor(_u("ps_w1"), [128, 512], F32) as pw1:
            pps = [pp0, pp1, pp2, pp3]
            pws = [pw0, pw1]
            for p in pps + pws:
                init_free(p)

            def qk_proj(weight, col_off, ncols, out_dram, bias_off, scale, rope):
                for blk in range(ncols // 512):
                    slot_i = next_slot()
                    wtok = wload(slot_i,
                                 weight.ap()[:, col_off + 512 * blk:col_off + 512 * (blk + 1)]
                                 .rearrange("(cc p) f -> p cc f", p=128),
                                 CC, 512)
                    for fo in range(4):
                        fglob = blk * 4 + fo
                        sl = fglob % 2
                        pr = (pps[0], pps[1]) if fo % 2 == 0 else (pps[2], pps[3])

                        @block.tensor
                        def _(t):
                            t = _TT(t)
                            t.wait_ge(GW[slot_i].h, wtok)
                            t.wait_ge(VQ.h, max(h_v, bias_split_v))
                            for th in range(TH):
                                fr = ps_free[id(pr[th])]
                                t.wait_ge(fr[0].h, fr[1])
                            wap = wview(slot_i, CC, 512)
                            for cc in range(CC):
                                for th in range(TH):
                                    t.matmul(pr[th][:, 0:thw[th]],
                                             wap[:, cc, 128 * fo:128 * (fo + 1)],
                                             scrA[:, cc * T + 512 * th:cc * T + 512 * th + thw[th]],
                                             start=(cc == 0), stop=(cc == CC - 1))
                            t.inc(TQ.h)
                        TQ.n += 1
                        tq_mm = TQ.n

                        for th in range(TH):
                            ps = pr[th]

                            @block.vector
                            def _(v):
                                v.wait_ge(TQ.h, tq_mm)
                                v.wait_ge(SST[sl].h, 16 * stg_ct[sl])
                                if bias_off is not None:
                                    v.tensor_scalar(stg[sl][:, 512 * th:512 * th + thw[th]],
                                                    ps[:, 0:thw[th]],
                                                    bqkv_fm[:, bias_off + fglob:bias_off + fglob + 1],
                                                    scale, ALU.add, ALU.mult).then_inc(VQ.h, 1)
                                else:
                                    v.tensor_scalar(stg[sl][:, 512 * th:512 * th + thw[th]],
                                                    ps[:, 0:thw[th]], scale, None,
                                                    ALU.mult).then_inc(VQ.h, 1)
                            VQ.n += 1
                            mark_free(ps, VQ)
                        sq_evac = VQ.n
                        dma_wait_sem, dma_wait_tok = VQ, sq_evac

                        if rope:
                            vq_t0 = []
                            for th in range(TH):
                                pw = pws[th % 2]

                                @block.tensor
                                def _(t):
                                    t = _TT(t)
                                    t.wait_ge(VQ.h, sq_evac)
                                    fr = ps_free[id(pw)]
                                    t.wait_ge(fr[0].h, fr[1])
                                    t.matmul(pw[:, 0:thw[th]], pswap[:, :],
                                             stg[sl][:, 512 * th:512 * th + thw[th]],
                                             start=True, stop=True)
                                    t.inc(TQ.h)
                                TQ.n += 1
                                tq_sw = TQ.n

                                @block.vector
                                def _(v):
                                    v.wait_ge(TQ.h, tq_sw)
                                    v.wait_ge(VQ.h, VQ.n)
                                    v.scalar_tensor_tensor(
                                        ptsb[:, 512 * th:512 * th + thw[th]],
                                        pw[:, 0:thw[th]], 1.0,
                                        stab[:, 512 * th:512 * th + thw[th]],
                                        ALU.mult, ALU.mult).then_inc(VQ.h, 1)
                                VQ.n += 1
                                mark_free(pw, VQ)
                            vq_ropeT = VQ.n

                            @block.gpsimd
                            def _(g):
                                g.wait_ge(VQ.h, sq_evac)
                                g.wait_ge(VQ.h, VQ.n)
                                g.wait_ge(TQ.h, TQ.n)
                                g.tensor_mul(psb[:, 0:T], stg[sl][:, 0:T],
                                             ctab[:, 0:T]).then_inc(PQ.h, 1)
                            PQ.n += 1
                            pq_u = PQ.n
                            tq_swaps = TQ.n

                            @block.vector
                            def _(v):
                                v.wait_ge(PQ.h, pq_u)
                                v.wait_ge(VQ.h, vq_ropeT)
                                v.wait_ge(TQ.h, tq_swaps)
                                v.tensor_tensor(stg[sl][:, 0:T], psb[:, 0:T],
                                                ptsb[:, 0:T], ALU.add).then_inc(VQ.h, 1)
                            VQ.n += 1
                            dma_wait_sem, dma_wait_tok = VQ, VQ.n

                        @block.sync
                        def _(sy, sem=dma_wait_sem, tok=dma_wait_tok):
                            sy.wait_ge(sem.h, tok)
                            sy.dma_start(out=out_dram.ap()[128 * fglob:128 * (fglob + 1), :],
                                         in_=stg[sl][:, 0:T]).then_inc(SST[sl].h, 16)
                        SST[sl].n += 16
                        stg_ct[sl] += 1
                    release_slot(slot_i, TQ)

            if isA:
                qk_proj(wqkv, 0, D, qa_i, 0, SCALE_A, False)
                qk_proj(wqkv, D, D, ka_i, CC, SCALE_A, False)
            else:
                qk_proj(wq_b, 0, D, qa_i, None, SCALE_B, True)
                qk_proj(wk_b, 0, c.DKV, ka_i, None, SCALE_B, True)
                psb_free[0] = psb_free[1] = (VQ, VQ.n)
                ptsb_free[0] = ptsb_free[1] = (VQ, VQ.n)

            vcols = D if isA else c.DKV
            wpd = WCTA if isA else 64
            for blk in range(vcols // 512):
                slot_i = next_slot()
                wsrc = (wqkv.ap()[:, 2 * D + 512 * blk:2 * D + 512 * (blk + 1)] if isA
                        else wv_b.ap()[:, 512 * blk:512 * (blk + 1)])
                wtok = wload(slot_i, wsrc.rearrange("(cc p) f -> p cc f", p=128), CC, 512)
                for tp in range(cdiv(NT, 2)):
                    tiles = [t_ for t_ in (2 * tp, 2 * tp + 1) if t_ < NT]
                    pr = (pps[0], pps[1]) if tp % 2 == 0 else (pps[2], pps[3])

                    @block.tensor
                    def _(t):
                        t = _TT(t)
                        t.wait_ge(GW[slot_i].h, wtok)
                        t.wait_ge(VQ.h, max(h_v, bias_split_v))
                        for ti in range(len(tiles)):
                            fr = ps_free[id(pr[ti])]
                            t.wait_ge(fr[0].h, fr[1])
                        wap = wview(slot_i, CC, 512)
                        for cc in range(CC):
                            for ti, tile in enumerate(tiles):
                                t.matmul(pr[ti][:, 0:512],
                                         scrA[:, cc * T + tile * 128:cc * T + (tile + 1) * 128],
                                         wap[:, cc, 0:512],
                                         start=(cc == 0), stop=(cc == CC - 1) and not isA)
                        if isA:
                            for ti in range(len(tiles)):
                                t.matmul(pr[ti][:, 0:512], ones65[0:1, :],
                                         bias_rows[0:1, 512 * blk:512 * (blk + 1)],
                                         start=False, stop=False)
                                t.matmul(pr[ti][:, 0:512], ones65[0:1, :],
                                         bias_rows[0:1, D + 512 * blk:D + 512 * (blk + 1)],
                                         start=False, stop=True)
                        t.inc(TQ.h)
                    TQ.n += 1
                    tq_mm = TQ.n

                    for ti, tile in enumerate(tiles):
                        ps = pr[ti]
                        vsl = 0

                        @block.vector
                        def _(v):
                            v.wait_ge(TQ.h, tq_mm)
                            v.wait_ge(SVT[vsl].h, 16 * vstg_ct[vsl])
                            v.tensor_copy(vstg[vsl][:, 0:512], ps[:, 0:512]).then_inc(VQ.h, 1)
                        VQ.n += 1
                        mark_free(ps, VQ)
                        sq_ev = VQ.n
                        nd = 512 // wpd

                        @block.sync
                        def _(sy):
                            sy.wait_ge(VQ.h, sq_ev)
                            for dd in range(nd):
                                d = (512 * blk) // wpd + dd
                                sy.dma_start(
                                    out=va_i.ap()[T * d + 128 * tile:T * d + 128 * (tile + 1), :],
                                    in_=vstg[vsl][:, wpd * dd:wpd * (dd + 1)]).then_inc(SVT[vsl].h, 16)
                        SVT[vsl].n += 16 * nd
                        vstg_ct[vsl] += nd
                release_slot(slot_i, TQ)

        @block.gpsimd
        def _(g):
            for s in (SST[0], SST[1], SVT[0], SVT[1]):
                g.wait_ge(s.h, s.n)
            with allow_ici_all_to_all():
                for i_, o_ in [(qa_i, qa_o), (ka_i, ka_o), (va_i, va_o)]:
                    g.collective_compute("AllToAll", ALU.bypass,
                                         replica_groups=[list(range(NCORE))],
                                         ins=[i_.ap()], outs=[o_.ap()]).then_inc(CCS.h, 1)
        CCS.n += 3
        cc_qkv = CCS.n

        hd = 128 if isA else 64
        nqh = 1 if isA else c.G
        units = ([(b, j) for b in range(c.B) for j in range(c.SLOTS_A)] if isA
                 else [(b, 0) for b in range(c.B)])

        with nc.psum_tensor(_u("ps_sc"), [128, 2048], F32) as ss2, \
             nc.psum_tensor(_u("ps_t0"), [128, 512], BF16) as pt0, \
             nc.psum_tensor(_u("ps_t1"), [128, 512], BF16) as pt1, \
             nc.psum_tensor(_u("ps_c0"), [128, hd + 1], F32) as cx0, \
             nc.psum_tensor(_u("ps_c1"), [128, hd + 1], F32) as cx1:
            pts, cxs = [pt0, pt1], [cx0, cx1]
            for p in pts + cxs:
                init_free(p)
            ss_free = [(TQ, 0), (TQ, 0)]
            sctr = [0]
            tctr = [0]

            for ui, (b, j) in enumerate(units):
                @block.sync
                def _(sy):
                    sy.wait_ge(CCS.h, cc_qkv)
                    sy.wait_ge(TQ.h, max(scrB_rel[0], cvec_rel[0]))
                    sy.wait_ge(VQ.h, scrB_vrel[0])
                    ns = c.NSRC
                    if isA:
                        sy.dma_start(out=cvec_sb[:, :],
                                     in_=t_cvec.ap()[j]).then_inc(SCV.h, 16)
                        pass
                    if isA:
                        inq = bass.AP(qa_o, (c.SLOTS_A * ns * b + j) * 128 * T,
                                      [[T, 128], [c.SLOTS_A * 128 * T, ns], [1, T]])
                        sy.dma_start(out=bass.AP(scrB, 0, [[SCRB, 128], [T, ns], [1, T]]),
                                     in_=inq).then_inc(SQB.h, 16)
                        ink = bass.AP(ka_o, (c.SLOTS_A * ns * b + j) * 128 * T,
                                      [[T, 128], [c.SLOTS_A * 128 * T, ns], [1, T]])
                        sy.dma_start(out=bass.AP(scrB, S, [[SCRB, 128], [T, ns], [1, T]]),
                                     in_=ink).then_inc(SKB.h, 16)
                        inv = bass.AP(va_o, (b * S) * WCTA + j * 128,
                                      [[WCTA, 128], [128 * WCTA, S // 128], [1, 128]])
                        sy.dma_start(out=bass.AP(scrB, 2 * S, [[SCRB, 128], [128, S // 128], [1, 128]]),
                                     in_=inv).then_inc(SVB.h, 16)
                    else:
                        for g2 in range(c.G // 2):
                            inq = bass.AP(qa_o, (WCTB * ns * b + g2 * 128) * T,
                                          [[T, 128], [WCTB * T, ns], [1, T]])
                            sy.dma_start(out=bass.AP(scrB, g2 * S, [[SCRB, 128], [T, ns], [1, T]]),
                                         in_=inq).then_inc(SQB.h, 16)
                        ink = bass.AP(ka_o, (64 * ns * b) * T,
                                      [[T, 64], [64 * T, ns], [1, T]])
                        sy.dma_start(out=bass.AP(scrB, KOFF, [[SCRB, 64], [T, ns], [1, T]]),
                                     in_=ink).then_inc(SKB.h, 16)
                        sy.dma_start(out=bass.AP(scrB, 64 * SCRB + KOFF, [[SCRB, 64], [T, ns], [1, T]]),
                                     in_=ink).then_inc(SKB.h, 16)
                        inv = bass.AP(va_o, (b * S) * 64,
                                      [[64, 128], [128 * 64, S // 128], [1, 64]])
                        sy.dma_start(out=bass.AP(scrB, 2 * S, [[SCRB, 128], [64, S // 128], [1, 64]]),
                                     in_=inv).then_inc(SVB.h, 16)
                if isA:
                    SCV.n += 16
                SQB.n += 16 * (1 if isA else c.G // 2)
                SKB.n += 16 if isA else 32
                SVB.n += 16
                qtok, ktok, vtok = SQB.n, SKB.n, SVB.n
                cvtok = SCV.n
                def norm_row(row_sb, src_fn, pbase, pcnt, ld_sem, ld_tok, kscale):
                    for ch in range(S // 512):
                        def f(s_, ch=ch, ld_sem=ld_sem, ld_tok=ld_tok, tqn=TQ.n,
                              pbase=pbase, pcnt=pcnt, src_fn=src_fn,
                              pfr=psb_free[0]):
                            s_.wait_ge(ld_sem.h, ld_tok)
                            s_.wait_ge(TQ.h, tqn)
                            if pfr[1]:
                                s_.wait_ge(pfr[0].h, pfr[1])
                            s_.activation(out=psb[pbase:pbase + pcnt, 0:512],
                                          in_=src_fn(ch), func=AF.Square).then_inc(SQ.h, 1)
                        sc(f)
                        SQ.n += 1
                        sq_sq = SQ.n

                        @block.tensor
                        def _(t):
                            t = _TT(t)
                            t.wait_ge(SQ.h, sq_sq)
                            t.wait_ge(VQ.h, VQ.n)
                            fr = ss_free[0]
                            t.wait_ge(fr[0].h, fr[1])
                            t.matmul(ss2[0:1, 0:512], maskT[pbase:pbase + pcnt, 127:128],
                                     psb[pbase:pbase + pcnt, 0:512],
                                     start=True, stop=True)
                            t.inc(TQ.h)
                        TQ.n += 1
                        psb_free[0] = (TQ, TQ.n)

                        @block.vector
                        def _(v):
                            v.wait_ge(TQ.h, TQ.n)
                            v.wait_ge(VQ.h, VQ.n)
                            if kscale:
                                v.tensor_scalar(row_sb[0:1, 512 * ch:512 * (ch + 1)],
                                                ss2[0:1, 0:512], knm[0:1, 0:1], None,
                                                ALU.mult).then_inc(VQ.h, 1)
                            else:
                                v.tensor_copy(row_sb[0:1, 512 * ch:512 * (ch + 1)],
                                              ss2[0:1, 0:512]).then_inc(VQ.h, 1)
                        VQ.n += 1
                        ss_free[0] = (VQ, VQ.n)

                if isA:
                    norm_row(rowscr, lambda ch: scrB[:, S + 512 * ch:S + 512 * (ch + 1)],
                             0, 128, SKB, ktok, False)
                else:
                    norm_row(rowscr, lambda ch: scrB[0:64, KOFF + 512 * ch:KOFF + 512 * (ch + 1)],
                             0, 64, SKB, ktok, False)

                @block.vector
                def _(v):
                    v.wait_ge(VQ.h, VQ.n)
                    v.reduce_max(knm[:, :], rowscr[0:1, 0:S], axis=AX.X).then_inc(VQ.h, 1)
                VQ.n += 1

                for qh in range(nqh):
                    if isA:
                        norm_row(rowscr, lambda ch: scrB[:, 512 * ch:512 * (ch + 1)],
                                 0, 128, SQB, qtok, True)
                    else:
                        g2, qb = qh // 2, 64 * (qh % 2)
                        norm_row(rowscr,
                                 lambda ch, g2=g2, qb=qb: scrB[qb:qb + 64,
                                                               g2 * S + 512 * ch:g2 * S + 512 * (ch + 1)],
                                 qb, 64, SQB, qtok, True)
                    vq_qnk = VQ.n

                    ptR = pts[0]

                    @block.tensor
                    def _(t):
                        t = _TT(t)
                        t.wait_ge(VQ.h, vq_qnk)
                        fr = ps_free[id(ptR)]
                        t.wait_ge(fr[0].h, fr[1])
                        for qt in range(c.QT):
                            t.transpose(ptR[:, 2 * qt:2 * qt + 1],
                                        rowscr[0:1, 128 * qt:128 * (qt + 1)],
                                        ones1[0:1, 0:1])
                        t.inc(TQ.h)
                    TQ.n += 1
                    tq_R = TQ.n

                    def f(s_, tq_R=tq_R, vqn=VQ.n, ptR=ptR):
                        s_.wait_ge(TQ.h, tq_R)
                        s_.wait_ge(VQ.h, vqn)
                        s_.activation(out=rcol[:, :],
                                      in_=bass.AP(ptR, 0, [[512, 128], [2, c.QT]]),
                                      func=AF.Sqrt).then_inc(SQ.h, 1)
                    sc(f)
                    SQ.n += 1
                    mark_free(ptR, SQ)
                    sq_R = SQ.n

                    @block.vector
                    def _(v):
                        v.wait_ge(SQ.h, sq_R)
                        v.wait_ge(TQ.h, TQ.n)
                        if isA:
                            v.scalar_tensor_tensor(negR[:, :], rcol[:, :], -1.0,
                                                   slopei[:, j * c.QT:(j + 1) * c.QT],
                                                   ALU.mult, ALU.subtract).then_inc(VQ.h, 1)
                        else:
                            v.tensor_scalar(negR[:, :], rcol[:, :], -1.0, None,
                                            ALU.mult).then_inc(VQ.h, 1)
                    VQ.n += 1
                    vq_negR = VQ.n

                    blocks = []
                    for qt in range(c.QT):
                        nk = (qt + 1) * 128
                        nb = cdiv(nk, 1024)
                        for kb in range(nb):
                            blocks.append({"qt": qt, "kb": kb,
                                           "w": min(1024, nk - 1024 * kb),
                                           "diag": kb == nb - 1})
                    NB = len(blocks)

                    def emit_score(B):
                        sl = sctr[0] % 2
                        sctr[0] += 1
                        B["sl"] = sl
                        soff = 1024 * sl
                        B["soff"] = soff
                        qt, kb, w = B["qt"], B["kb"], B["w"]

                        @block.tensor
                        def _(t):
                            t = _TT(t)
                            fr = ss_free[sl]
                            t.wait_ge(fr[0].h, fr[1])
                            t.wait_ge(VQ.h, vq_negR)
                            t.wait_ge(SVB.h, vtok)
                            if isA:
                                t.wait_ge(SCV.h, cvtok)
                            nh2 = cdiv(w, 512)
                            for h2 in range(nh2):
                                wh = min(512, w - 512 * h2)
                                kb512 = 1024 * kb + 512 * h2
                                if isA:
                                    t.matmul(ss2[:, soff + 512 * h2:soff + 512 * h2 + wh],
                                             scrB[:, 128 * qt:128 * (qt + 1)],
                                             scrB[:, S + kb512:S + kb512 + wh],
                                             start=True, stop=False)
                                else:
                                    g2, qb = qh // 2, 64 * (qh % 2)
                                    t.matmul(ss2[:, soff + 512 * h2:soff + 512 * h2 + wh],
                                             scrB[qb:qb + 64,
                                                  g2 * S + 128 * qt:g2 * S + 128 * (qt + 1)],
                                             scrB[qb:qb + 64, KOFF + kb512:KOFF + kb512 + wh],
                                             start=True, stop=not isA)
                            if isA:
                                for h2 in range(nh2):
                                    wh = min(512, w - 512 * h2)
                                    kb512 = 1024 * kb + 512 * h2
                                    t.matmul(ss2[:, soff + 512 * h2:soff + 512 * h2 + wh],
                                             ones3[:, :],
                                             cvec_sb[:, kb512:kb512 + wh],
                                             start=False, stop=True)
                            t.inc(TQ.h)
                        TQ.n += 1
                        tq_s = TQ.n

                        vq_msk = 0
                        if isA and B["diag"]:
                            @block.vector
                            def _(v):
                                v.wait_ge(TQ.h, tq_s)
                                v.tensor_tensor(ss2[:, soff + w - 128:soff + w],
                                                ss2[:, soff + w - 128:soff + w],
                                                maskA[:, :], ALU.add).then_inc(VQ.h, 1)
                            VQ.n += 1
                            vq_msk = VQ.n

                        def f(s_, tq_s=tq_s, vq_msk=vq_msk, fr=psb_free[sl],
                              soff=soff, w=w, qt=qt):
                            s_.wait_ge(TQ.h, tq_s)
                            if vq_msk:
                                s_.wait_ge(VQ.h, vq_msk)
                            if fr[1]:
                                s_.wait_ge(fr[0].h, fr[1])
                            s_.activation(out=psb[:, soff:soff + w],
                                          in_=ss2[:, soff:soff + w], func=AF.Exp,
                                          bias=negR[:, qt:qt + 1], scale=1.0
                                          ).then_inc(SQ.h, 1)
                        sc(f)
                        SQ.n += 1
                        ss_free[sl] = (SQ, SQ.n)
                        B["sq_exp"] = SQ.n

                    def emit_trev(B):
                        qt, kb, w, soff, sl = B["qt"], B["kb"], B["w"], B["soff"], B["sl"]
                        B["groups"] = []
                        for g5 in range(cdiv(w, 512)):
                            wg = min(512, w - 512 * g5)
                            ptl = tctr[0] % 2
                            ptp = pts[ptl]
                            tctr[0] += 1

                            @block.tensor
                            def _(t):
                                t = _TT(t)
                                t.wait_ge(SQ.h, B["sq_exp"])
                                fr = ps_free[id(ptp)]
                                t.wait_ge(fr[0].h, fr[1])
                                for k in range(wg // 128):
                                    t.transpose(ptp[:, 128 * k:128 * (k + 1)],
                                                psb[:, soff + 512 * g5 + 128 * k:
                                                    soff + 512 * g5 + 128 * (k + 1)],
                                                ident[:, :])
                                t.inc(TQ.h)
                            TQ.n += 1
                            if g5 == cdiv(w, 512) - 1:
                                psb_free[sl] = (TQ, TQ.n)
                            tq_tr = TQ.n

                            @block.vector
                            def _(v):
                                v.wait_ge(TQ.h, tq_tr)
                                fr = ptsb_free[ptl]
                                if fr[1]:
                                    v.wait_ge(fr[0].h, fr[1])
                                v.tensor_copy(ptsb[:, 512 * ptl:512 * ptl + wg],
                                              ptp[:, 0:wg]).then_inc(VQ.h, 1)
                                ksub_hi = (1024 * kb + 512 * g5 + wg) // 128 - 1
                                if ksub_hi == qt and not isA:
                                    v.wait_ge(VQ.h, VQ.n + 1)
                                    dcol = 512 * ptl + wg - 128
                                    v.tensor_tensor(ptsb[:, dcol:dcol + 128],
                                                    ptsb[:, dcol:dcol + 128],
                                                    maskT[:, :], ALU.mult).then_inc(VQ.h, 1)
                            nvops = 2 if ((1024 * kb + 512 * g5 + wg) // 128 - 1 == qt
                                          and not isA) else 1
                            VQ.n += nvops
                            mark_free(ptp, VQ)
                            B["groups"].append((g5, wg, ptl, VQ.n))

                    def emit_ctx(B):
                        qt, kb = B["qt"], B["kb"]
                        cxp = cxs[qt % 2]
                        first = kb == 0
                        for (g5, wg, ptl, vq_pt) in B["groups"]:
                            @block.tensor
                            def _(t, first=first):
                                t = _TT(t)
                                t.wait_ge(VQ.h, vq_pt)
                                if first:
                                    fr = ps_free[id(cxp)]
                                    t.wait_ge(fr[0].h, fr[1])
                                for k in range(wg // 128):
                                    ksub = 8 * kb + 4 * g5 + k
                                    lhs = ptsb[:, 512 * ptl + 128 * k:512 * ptl + 128 * (k + 1)]
                                    t.matmul(cxp[:, 0:hd], lhs,
                                             scrB[:, 2 * S + hd * ksub:2 * S + hd * (ksub + 1)],
                                             start=first and k == 0, stop=False)
                                    t.matmul(cxp[:, hd:hd + 1], lhs, maskT[:, 127:128],
                                             start=False, stop=(ksub == qt))
                                t.inc(TQ.h)
                            TQ.n += 1
                            ptsb_free[ptl] = (TQ, TQ.n)
                            first = False
                        if B["diag"]:
                            emit_final(qt, cxp)

                    def emit_final(qt, cxp):
                        tq_ctx = TQ.n

                        @block.vector
                        def _(v):
                            v.wait_ge(TQ.h, tq_ctx)
                            if ctx_region_rel[0]:
                                v.wait_ge(SCX.h, ctx_region_rel[0])
                            v.reciprocal(lrec[:, :], cxp[:, hd:hd + 1]).then_inc(VQ.h, 1)
                            v.wait_ge(VQ.h, VQ.n + 1)
                            col = qt * wct + (j * 128 if isA else 64 * qh)
                            v.tensor_scalar(scrA[:, col:col + hd], cxp[:, 0:hd],
                                            lrec[:, :], None, ALU.mult).then_inc(VQ.h, 1)
                        VQ.n += 2
                        mark_free(cxp, VQ)

                    for i in range(NB + 2):
                        if i >= 2:
                            emit_ctx(blocks[i - 2])
                        if i < NB:
                            emit_score(blocks[i])
                        if i >= 1 and i - 1 < NB:
                            emit_trev(blocks[i - 1])

                last_of_batch = (j == c.SLOTS_A - 1) if isA else True
                if last_of_batch:
                    vq_unit = VQ.n

                    @block.sync
                    def _(sy):
                        sy.wait_ge(VQ.h, vq_unit)
                        outap = bass.AP(ca_i, (c.NSRC * b * T) * wct,
                                        [[wct, 128], [128 * wct, S // 128], [1, wct]])
                        inap = bass.AP(scrA, 0, [[CC * T, 128], [wct, S // 128], [1, wct]])
                        sy.dma_start(out=outap, in_=inap).then_inc(SCX.h, 16)
                    SCX.n += 16
                    ctx_region_rel[0] = SCX.n
                scrB_rel[0] = TQ.n
                cvec_rel[0] = TQ.n
            sc_flush()

        @block.gpsimd
        def _(g):
            g.wait_ge(SCX.h, SCX.n)
            with allow_ici_all_to_all():
                g.collective_compute("AllToAll", ALU.bypass,
                                     replica_groups=[list(range(NCORE))],
                                     ins=[ca_i.ap()], outs=[ca_o.ap()]).then_inc(CCS.h, 1)
        CCS.n += 1
        cc_ctx = CCS.n

        @block.sync
        def _(sy):
            sy.wait_ge(CCS.h, cc_ctx)
            sy.wait_ge(TQ.h, scrB_rel[0])
            sy.wait_ge(VQ.h, scrB_vrel[0])
            for tile in range(NT):
                inap = bass.AP(ca_o, tile * 128 * wct,
                               [[wct, 128], [T * wct, NCORE], [1, wct]])
                outap = bass.AP(scrB, tile * D,
                                [[SCRB, 128], [wct, NCORE], [1, wct]])
                sy.dma_start(out=outap, in_=inap).then_inc(SCX.h, 16)
        SCX.n += 16 * NT
        ctx_ld = SCX.n

        with nc.psum_tensor(_u("ps_ct0"), [128, 1024], BF16) as pc0, \
             nc.psum_tensor(_u("ps_ct1"), [128, 1024], BF16) as pc1:
            pcs = [pc0, pc1]
            rel = [0, 0]
            for tile in range(NT):
                for bb in range(CC // 8):
                    pi = (tile * (CC // 8) + bb) % 2
                    pb = pcs[pi]

                    @block.tensor
                    def _(t):
                        t = _TT(t)
                        t.wait_ge(SCX.h, ctx_ld)
                        t.wait_ge(VQ.h, VQ.n)
                        t.wait_ge(SQ.h, SQ.n)
                        if rel[pi]:
                            t.wait_ge(VQ.h, rel[pi])
                        for k in range(8):
                            cc = 8 * bb + k
                            t.transpose(pb[:, 128 * k:128 * (k + 1)],
                                        scrB[:, tile * D + 128 * cc:tile * D + 128 * (cc + 1)],
                                        ident[:, :])
                        t.inc(TQ.h)
                    TQ.n += 1
                    tq_tr = TQ.n

                    @block.vector
                    def _(v):
                        v.wait_ge(TQ.h, tq_tr)
                        outap = bass.AP(scrA, 8 * bb * T + tile * 128,
                                        [[CC * T, 128], [T, 8], [1, 128]])
                        v.tensor_copy(outap,
                                      bass.AP(pb, 0, [[1024, 128], [128, 8], [1, 128]])
                                      ).then_inc(VQ.h, 1)
                    VQ.n += 1
                    rel[pi] = VQ.n
        ctxfm_v = VQ.n
        scrB_rel[0] = TQ.n

        wo = wo_a if isA else wo_b
        brow = 32 if isA else 64
        with nc.psum_tensor(_u("ps_o0"), [128, 512], F32) as po0, \
             nc.psum_tensor(_u("ps_o1"), [128, 512], F32) as po1, \
             nc.psum_tensor(_u("ps_o2"), [128, 512], F32) as po2, \
             nc.psum_tensor(_u("ps_o3"), [128, 512], F32) as po3:
            pos_ = [po0, po1, po2, po3]
            for p in pos_:
                init_free(p)
            for blk in range(D // 512):
                slot_i = next_slot()
                wtok = wload(slot_i,
                             wo.ap()[:, 512 * blk:512 * (blk + 1)].rearrange("(cc p) f -> p cc f", p=128),
                             CC, 512)
                for tp in range(cdiv(NT, 2)):
                    tiles = [t_ for t_ in (2 * tp, 2 * tp + 1) if t_ < NT]
                    pr = (pos_[0], pos_[1]) if tp % 2 == 0 else (pos_[2], pos_[3])

                    @block.tensor
                    def _(t):
                        t = _TT(t)
                        t.wait_ge(GW[slot_i].h, wtok)
                        t.wait_ge(VQ.h, ctxfm_v)
                        for ti in range(len(tiles)):
                            fr = ps_free[id(pr[ti])]
                            t.wait_ge(fr[0].h, fr[1])
                        wap = wview(slot_i, CC, 512)
                        for cc in range(CC):
                            for ti, tile in enumerate(tiles):
                                t.matmul(pr[ti][:, 0:512],
                                         scrA[:, cc * T + tile * 128:cc * T + (tile + 1) * 128],
                                         wap[:, cc, 0:512], start=(cc == 0), stop=False)
                        for ti in range(len(tiles)):
                            t.matmul(pr[ti][:, 0:512], ones65[brow:brow + 1, :],
                                     bias_rows[brow:brow + 1, 512 * blk:512 * (blk + 1)],
                                     start=False, stop=False)
                            t.matmul(pr[ti][:, 0:512], ones65[brow:brow + 1, :],
                                     bias_rows[brow:brow + 1, D + 512 * blk:D + 512 * (blk + 1)],
                                     start=False, stop=True)
                        t.inc(TQ.h)
                    TQ.n += 1
                    tq_mm = TQ.n

                    for ti, tile in enumerate(tiles):
                        ps = pr[ti]

                        @block.vector
                        def _(v):
                            v.wait_ge(TQ.h, tq_mm)
                            v.tensor_tensor(x_sb[:, tile * D + 512 * blk:tile * D + 512 * (blk + 1)],
                                            x_sb[:, tile * D + 512 * blk:tile * D + 512 * (blk + 1)],
                                            ps[:, 0:512], ALU.add).then_inc(VQ.h, 1)
                        VQ.n += 1
                        mark_free(ps, VQ)
                release_slot(slot_i, TQ)

    def ffn_sublayer():
        h_v = ln_phase()

        with nc.psum_tensor(_u("ps_r0"), [128, 512], F32) as pr0, \
             nc.psum_tensor(_u("ps_r1"), [128, 512], F32) as pr1:
            prs = [pr0, pr1]
            for p in prs:
                init_free(p)
            for tile in range(NT):
                ps = prs[tile % 2]

                @block.tensor
                def _(t):
                    t = _TT(t)
                    t.wait_ge(VQ.h, max(h_v, startup_v))
                    fr = ps_free[id(ps)]
                    t.wait_ge(fr[0].h, fr[1])
                    EE = c.E + 1
                    for cc in range(CC):
                        t.matmul(ps[:, 0:EE],
                                 scrA[:, cc * T + tile * 128:cc * T + (tile + 1) * 128],
                                 routerT[:, cc * EEP:cc * EEP + EE],
                                 start=(cc == 0), stop=(cc == CC - 1))
                    t.inc(TQ.h)
                TQ.n += 1
                tq_log = TQ.n

                @block.vector
                def _(v):
                    v.wait_ge(TQ.h, tq_log)
                    v.reduce_max(nmx[:, :], ps[:, 0:c.E], axis=AX.X).then_inc(VQ.h, 1)
                    v.wait_ge(VQ.h, VQ.n + 1)
                    v.tensor_scalar(nmx[:, :], nmx[:, :], -1.0, None, ALU.mult).then_inc(VQ.h, 1)
                VQ.n += 2
                vq_mx = VQ.n

                def f(s_, vq_mx=vq_mx, ps=ps):
                    s_.wait_ge(VQ.h, vq_mx)
                    s_.activation(out=expv[:, :], in_=ps[:, 0:c.E], func=AF.Exp,
                                  bias=nmx[:, :], scale=1.0).then_inc(SQ.h, 1)
                sc(f)
                SQ.n += 1
                sq_e = SQ.n

                @block.vector
                def _(v):
                    v.wait_ge(SQ.h, sq_e)
                    v.tensor_copy(sglog[:, tile:tile + 1], ps[:, c.E:c.E + 1]).then_inc(VQ.h, 1)
                VQ.n += 1
                mark_free(ps, VQ)

                @block.vector
                def _(v):
                    v.wait_ge(SQ.h, sq_e)
                    n0 = VQ.n
                    v.tensor_copy(mskv[:, :], expv[:, :]).then_inc(VQ.h, 1)
                    for k in range(4):
                        v.wait_ge(VQ.h, n0 + 1 + 3 * k)
                        v.reduce_max(mml[:, k:k + 1], mskv[:, :], axis=AX.X).then_inc(VQ.h, 1)
                        if k < 3:
                            v.wait_ge(VQ.h, n0 + 2 + 3 * k)
                            v.tensor_scalar(ge01[:, :], mskv[:, :], mml[:, k:k + 1], -1e9,
                                            ALU.is_ge, ALU.mult).then_inc(VQ.h, 1)
                            v.wait_ge(VQ.h, n0 + 3 + 3 * k)
                            v.tensor_tensor(mskv[:, :], mskv[:, :], ge01[:, :],
                                            ALU.add).then_inc(VQ.h, 1)
                    v.wait_ge(VQ.h, n0 + 11)
                    v.tensor_tensor(mml[:, 4:5], mml[:, 0:1], mml[:, 1:2], ALU.add).then_inc(VQ.h, 1)
                    v.wait_ge(VQ.h, n0 + 12)
                    v.tensor_tensor(mml[:, 4:5], mml[:, 4:5], mml[:, 2:3], ALU.add).then_inc(VQ.h, 1)
                    v.wait_ge(VQ.h, n0 + 13)
                    v.tensor_tensor(mml[:, 4:5], mml[:, 4:5], mml[:, 3:4], ALU.add).then_inc(VQ.h, 1)
                    v.wait_ge(VQ.h, n0 + 14)
                    v.reciprocal(mml[:, 5:6], mml[:, 4:5]).then_inc(VQ.h, 1)
                    v.tensor_scalar(ge01[:, :], expv[:, :], mml[:, 3:4], None,
                                    ALU.is_ge).then_inc(VQ.h, 1)
                    v.wait_ge(VQ.h, n0 + 16)
                    v.tensor_tensor(mskv[:, :], expv[:, :], ge01[:, :], ALU.mult).then_inc(VQ.h, 1)
                    v.wait_ge(VQ.h, n0 + 17)
                    v.tensor_scalar(comb[:, tile * c.E:(tile + 1) * c.E], mskv[:, :],
                                    mml[:, 5:6], None, ALU.mult).then_inc(VQ.h, 1)
                VQ.n += 18
            sc_flush()
        vq_sg = VQ.n

        def f(s_, vq_sg=vq_sg):
            s_.wait_ge(VQ.h, vq_sg)
            s_.activation(out=sg_sb[:, :], in_=sglog[:, :],
                          func=AF.Sigmoid).then_inc(SQ.h, 1)
        sc(f)
        SQ.n += 1

        comb_v = VQ.n
        ffn_entry_s = SQ.n

        with nc.psum_tensor(_u("ps_m0"), [128, 512], F32) as pm0, \
             nc.psum_tensor(_u("ps_m1"), [128, 512], F32) as pm1, \
             nc.psum_tensor(_u("ps_m2"), [128, 512], F32) as pm2, \
             nc.psum_tensor(_u("ps_m3"), [128, 512], F32) as pm3, \
             nc.psum_tensor(_u("ps_d0"), [128, 512], F32) as pd0, \
             nc.psum_tensor(_u("ps_d1"), [128, 512], F32) as pd1:
            pms, pds = [pm0, pm1, pm2, pm3], [pd0, pd1]
            for p in pms + pds:
                init_free(p)
            sil_free = {i: (VQ, 0) for i in range(NSIL)}

            def gu_phase(gsrc, usrc, iblks, rebase):
                pending = []
                gct, uct = [0], [0]
                iclct = [0]
                for (ioff, iw) in iblks:
                    nic = iw // 128
                    for which in range(2):
                        slot_i = next_slot()
                        src_ = (gsrc if which == 0 else usrc)(ioff, iw)
                        wtok = wload(slot_i, src_, CC, iw)
                        for icl in range(nic):
                            ic = ioff // 128 + icl
                            pr = (pms[0], pms[1]) if iclct[0] % 2 == 0 else (pms[2], pms[3])
                            iclct[0] += 1

                            @block.tensor
                            def _(t):
                                t = _TT(t)
                                t.wait_ge(GW[slot_i].h, wtok)
                                t.wait_ge(VQ.h, max(h_v, comb_v))
                                t.wait_ge(SQ.h, ffn_entry_s)
                                for th in range(TH):
                                    fr = ps_free[id(pr[th])]
                                    t.wait_ge(fr[0].h, fr[1])
                                wap = wview(slot_i, CC, iw)
                                for cc in range(CC):
                                    for th in range(TH):
                                        t.matmul(pr[th][:, 0:thw[th]],
                                                 wap[:, cc, 128 * icl:128 * (icl + 1)],
                                                 scrA[:, cc * T + 512 * th:cc * T + 512 * th + thw[th]],
                                                 start=(cc == 0), stop=(cc == CC - 1))
                                t.inc(TQ.h)
                            TQ.n += 1
                            tq_g = TQ.n

                            for th in range(TH):
                                ps = pr[th]
                                if which == 0:
                                    sl = gct[0] % NSIL

                                    def f(s_, tqw=max(tq_g, scrB_rel[0]),
                                          fr=sil_free[sl], sl=sl, th=th, ps=ps):
                                        s_.wait_ge(TQ.h, tqw)
                                        if fr[1]:
                                            s_.wait_ge(fr[0].h, fr[1])
                                        s_.activation(out=silsl(sl, thw[th]),
                                                      in_=ps[:, 0:thw[th]],
                                                      func=AF.Silu).then_inc(SQ.h, 1)
                                    sc(f)
                                    SQ.n += 1
                                    mark_free(ps, SQ)
                                    pending.append((ic, th, sl, SQ.n))
                                    gct[0] += 1
                                else:
                                    ic0, th0, sl, sq_sil = pending.pop(0)
                                    col = (ic0 - rebase) * T + 512 * th0

                                    @block.vector
                                    def _(v):
                                        v.wait_ge(TQ.h, max(tq_g, scrB_rel[0]))
                                        v.wait_ge(SQ.h, sq_sil)
                                        v.scalar_tensor_tensor(
                                            scrB[:, col:col + thw[th0]],
                                            ps[:, 0:thw[th0]], 1.0, silsl(sl, thw[th0]),
                                            ALU.mult, ALU.mult).then_inc(VQ.h, 1)
                                    VQ.n += 1
                                    mark_free(ps, VQ)
                                    sil_free[sl] = (VQ, VQ.n)
                                    uct[0] += 1
                        release_slot(slot_i, TQ)
                return VQ.n

            def down_phase(dsrc, ncc, rebase, scal_fn):
                for fc in range(D // 512):
                    slot_i = next_slot()
                    wtok = wload(slot_i, dsrc(fc), ncc, 512)
                    for tp in range(cdiv(NT, 2)):
                        tiles = [t_ for t_ in (2 * tp, 2 * tp + 1) if t_ < NT]
                        pr = (pds[0], pds[1]) if tp % 2 == 0 else (pms[0], pms[1])

                        @block.tensor
                        def _(t):
                            t = _TT(t)
                            t.wait_ge(GW[slot_i].h, wtok)
                            t.wait_ge(VQ.h, act_v[0])
                            for ti in range(len(tiles)):
                                fr = ps_free[id(pr[ti])]
                                t.wait_ge(fr[0].h, fr[1])
                            wap = wview(slot_i, ncc, 512)
                            for icl in range(ncc):
                                for ti, tile in enumerate(tiles):
                                    t.matmul(pr[ti][:, 0:512],
                                             scrB[:, icl * T + tile * 128:icl * T + (tile + 1) * 128],
                                             wap[:, icl, 0:512],
                                             start=(icl == 0), stop=(icl == ncc - 1))
                            t.inc(TQ.h)
                        TQ.n += 1
                        tq_d = TQ.n

                        for ti, tile in enumerate(tiles):
                            ps = pr[ti]
                            scal = scal_fn(tile)

                            @block.vector
                            def _(v):
                                v.wait_ge(TQ.h, tq_d)
                                v.wait_ge(VQ.h, comb_v)
                                v.scalar_tensor_tensor(
                                    x_sb[:, tile * D + 512 * fc:tile * D + 512 * (fc + 1)],
                                    ps[:, 0:512], scal,
                                    x_sb[:, tile * D + 512 * fc:tile * D + 512 * (fc + 1)],
                                    ALU.mult, ALU.add).then_inc(VQ.h, 1)
                            VQ.n += 1
                            mark_free(ps, VQ)
                    release_slot(slot_i, TQ)

            act_v = [0]
            for e in range(c.E):
                act_v[0] = gu_phase(
                    lambda ioff, iw, e=e: w_gate.ap()[e, :, ioff:ioff + iw]
                    .rearrange("(cc p) i -> p cc i", p=128),
                    lambda ioff, iw, e=e: w_up.ap()[e, :, ioff:ioff + iw]
                    .rearrange("(cc p) i -> p cc i", p=128),
                    c.IBLK, 0)
                down_phase(
                    lambda fc, e=e: w_down.ap()[e, :, 512 * fc:512 * (fc + 1)]
                    .rearrange("(ic p) f -> p ic f", p=128),
                    c.IC, 0,
                    lambda tile, e=e: comb[:, tile * c.E + e:tile * c.E + e + 1])
                scrB_rel[0] = TQ.n
                scrB_vrel[0] = VQ.n

            for (soff, sw) in c.SSEG:
                iblk_seg = []
                o = soff
                while o < soff + sw:
                    w_ = min(512, soff + sw - o)
                    iblk_seg.append((o, w_))
                    o += w_
                act_v[0] = gu_phase(
                    lambda ioff, iw: sh_gate.ap()[:, ioff:ioff + iw]
                    .rearrange("(cc p) i -> p cc i", p=128),
                    lambda ioff, iw: sh_up.ap()[:, ioff:ioff + iw]
                    .rearrange("(cc p) i -> p cc i", p=128),
                    iblk_seg, soff // 128)
                down_phase(
                    lambda fc, soff=soff, sw=sw: sh_down.ap()[soff:soff + sw, 512 * fc:512 * (fc + 1)]
                    .rearrange("(ic p) f -> p ic f", p=128),
                    sw // 128, soff // 128,
                    lambda tile: sg_sb[:, tile:tile + 1])
                scrB_rel[0] = TQ.n
                scrB_vrel[0] = VQ.n
            sc_flush()

    attn_sublayer(True)
    ffn_sublayer()
    attn_sublayer(False)
    ffn_sublayer()

    sc_flush()

    @block.sync
    def _(sy):
        sy.wait_ge(VQ.h, VQ.n)
        sy.dma_start(out=out_t.ap().rearrange("(n p) d -> p n d", p=128),
                     in_=bass.AP(x_sb, 0, [[NT * D, 128], [D, NT], [1, D]])
                     ).then_inc(SOUT.h, 16)
        sy.wait_ge(SOUT.h, 16)

    st_.close()
    return nc


def host_tables(c: Cfg, core):
    import ml_dtypes
    bf = ml_dtypes.bfloat16
    t = {}
    t["t_ident"] = np.eye(128, dtype=np.float32).astype(bf)
    t["t_identf"] = np.eye(2 * c.D // 128, dtype=np.float32)
    k_ = np.arange(128)[:, None]
    q_ = np.arange(128)[None, :]
    t["t_maskT"] = (k_ <= q_).astype(np.float32).astype(bf)
    t["t_maskA"] = np.where(q_.T >= k_.T, 0.0, -1e9).astype(np.float32).astype(bf)
    slopes = np.array([1.0 / 2 ** (k / 2) for k in range(c.HA)], np.float64) \
        if c.HA > 8 else np.array([1.0 / 2 ** k for k in range(c.HA)], np.float64)
    my_heads = [c.SLOTS_A * core + jj for jj in range(c.SLOTS_A)]
    jj_ = np.arange(c.S)
    cj = np.stack([slopes[h] * jj_.astype(np.float64) for h in my_heads])
    c0 = cj.astype(bf).astype(np.float64)
    c1 = (cj - c0).astype(bf).astype(np.float64)
    c2 = (cj - c0 - c1).astype(bf)
    t["t_cvec"] = np.stack([c0.astype(bf), c1.astype(bf), c2], axis=1)
    qt = np.arange(c.QT)
    pp = np.arange(128)
    si = np.stack([slopes[h] * (128 * qt[None, :] + pp[:, None]).astype(np.float64)
                   for h in my_heads], axis=0)
    t["t_slopei"] = np.ascontiguousarray(
        si.transpose(1, 0, 2).reshape(128, c.SLOTS_A * c.QT)).astype(np.float32).astype(bf)
    s0 = (core % c.NSRC) * c.T
    pos = np.arange(s0, s0 + c.T, dtype=np.float64)
    invf = 1.0 / (10000.0 ** (np.arange(0, c.DHB, 2, dtype=np.float64) / c.DHB))
    p = np.arange(128)
    ang = pos[None, :] * invf[p % (c.DHB // 2)][:, None]
    sgn = np.where((p % c.DHB) < (c.DHB // 2), -1.0, 1.0)
    t["t_ctab"] = np.cos(ang).astype(np.float32).astype(bf)
    t["t_stab"] = (np.sin(ang) * sgn[:, None]).astype(np.float32).astype(bf)
    P = np.zeros((128, 128), np.float32)
    r = np.arange(128)
    src = (r // c.DHB) * c.DHB + ((r + c.DHB // 2) % c.DHB)
    P[src, r] = 1.0
    t["t_pswap"] = P.astype(bf)
    return t


_CFG = Cfg()


def run(c: Cfg, x, weights):
    nc = build(c)
    in_maps = []
    for core in range(NCORE):
        b, s0 = core // c.NSRC, (core % c.NSRC) * c.T
        m = {"x_shard": np.ascontiguousarray(x[b, s0:s0 + c.T, :])}
        m.update(weights)
        m.update(host_tables(c, core))
        in_maps.append(m)
    res = run_bass_kernel_spmd(nc, in_maps, core_ids=list(range(NCORE)))
    out = np.empty((c.B, c.S, c.D), np.float32)
    for core in range(NCORE):
        b, s0 = core // c.NSRC, (core % c.NSRC) * c.T
        out[b, s0:s0 + c.T, :] = res.results[core]["out_shard"]
    return out


def kernel(L0_attn_a_x, L0_attn_a_Wqkv, L0_attn_a_bqkv, L0_attn_a_Wo, L0_attn_a_bo,
           L0_attn_b_Wq, L0_attn_b_Wk, L0_attn_b_Wv, L0_attn_b_Wo, L0_attn_b_b_out,
           L0_ffn_router_w, L0_ffn_w_gate, L0_ffn_w_up, L0_ffn_w_down,
           L0_ffn_sh_gate, L0_ffn_sh_up, L0_ffn_sh_down, L0_ffn_sh_gate_w):
    c = _CFG
    x = np.ascontiguousarray(np.asarray(L0_attn_a_x, np.float32))
    weights = {
        "Wqkv": L0_attn_a_Wqkv, "bqkv": L0_attn_a_bqkv, "Wo_a": L0_attn_a_Wo,
        "bo_a": L0_attn_a_bo, "Wq_b": L0_attn_b_Wq, "Wk_b": L0_attn_b_Wk,
        "Wv_b": L0_attn_b_Wv, "Wo_b": L0_attn_b_Wo, "bo_b": L0_attn_b_b_out,
        "router_w": L0_ffn_router_w, "w_gate": L0_ffn_w_gate, "w_up": L0_ffn_w_up,
        "w_down": L0_ffn_w_down, "sh_gate": L0_ffn_sh_gate, "sh_up": L0_ffn_sh_up,
        "sh_down": L0_ffn_sh_down, "sh_gate_w": L0_ffn_sh_gate_w,
    }
    weights = {k: np.ascontiguousarray(np.asarray(v, np.float32)) for k, v in weights.items()}
    return run(c, x, weights)


# version 12
# speedup vs baseline: 1.1968x; 1.1968x over previous
import numpy as np
from contextlib import ExitStack

import concourse.bass as bass
import concourse.mybir as mybir
from concourse.bass_utils import run_bass_kernel_spmd
from concourse.aot_env import allow_ici_all_to_all

F32 = mybir.dt.float32
BF16 = mybir.dt.bfloat16
AF = mybir.ActivationFunctionType
ALU = mybir.AluOpType
AX = mybir.AxisListType

NCORE = 8


class Cfg:
    def __init__(self, B=2, S=4096, D=2048, I_MOE=1408, I_SH=5632, E=16):
        self.B, self.S, self.D = B, S, D
        self.I_MOE, self.I_SH, self.E = I_MOE, I_SH, E
        self.N = B * S
        self.T = self.N // NCORE
        self.NT = self.T // 128
        self.CC = D // 128
        self.HA = D // 128
        self.SLOTS_A = B * self.HA // NCORE // B
        self.DHB = 64
        self.QH = D // self.DHB
        self.KVH = NCORE
        self.G = self.QH // self.KVH
        self.DKV = self.KVH * self.DHB
        self.QT = S // 128
        self.CT = 64
        self.CAP = self.T // 128 * self.CT
        self.IC = I_MOE // 128
        self.ICS = I_SH // 128
        self.NSRC = S // self.T
        self.IBLK = []
        o = 0
        while o < I_MOE:
            w = min(512, I_MOE - o)
            self.IBLK.append((o, w))
            o += w
        self.SSEG = []
        o = 0
        while o < I_SH:
            w = min(1536, I_SH - o)
            self.SSEG.append((o, w))
            o += w


def cdiv(a, b):
    return (a + b - 1) // b



class _TT:
    def __init__(self, eng):
        self.e = eng
        self.last = None

    def matmul(self, *a, **k):
        self.last = self.e.matmul(*a, **k)
        return self.last

    def transpose(self, *a, **k):
        self.last = self.e.transpose(*a, **k)
        return self.last

    def wait_ge(self, *a):
        return self.e.wait_ge(*a)

    def inc(self, sem):
        self.last.then_inc(sem, 1)


_UID = [0]


def _u(s):
    _UID[0] += 1
    return f"{s}_{_UID[0]}"


class VstgCt:
    def __init__(self):
        self.n = 0

    def __getitem__(self, i):
        return self.n

    def __setitem__(self, i, v):
        self.n = v


class Sem:
    def __init__(self, E_, nc, name):
        self.h = E_(nc.semaphore(name))
        self.n = 0


def build(c: Cfg):
    nc = bass.Bass("TRN2", target_bir_lowering=False)
    T, NT, CC, D, S = c.T, c.NT, c.CC, c.D, c.S
    TH = cdiv(T, 512)
    thw = [min(512, T - 512 * i) for i in range(TH)]
    SCALE_A = float(128.0 ** -0.25)
    SCALE_B = float(c.DHB ** -0.25)

    x_in = nc.dram_tensor("x_shard", [T, D], F32, kind="ExternalInput")
    out_t = nc.dram_tensor("out_shard", [T, D], F32, kind="ExternalOutput")
    wqkv = nc.dram_tensor("Wqkv", [D, 3 * D], F32, kind="ExternalInput")
    bqkv = nc.dram_tensor("bqkv", [3 * D], F32, kind="ExternalInput")
    wo_a = nc.dram_tensor("Wo_a", [D, D], F32, kind="ExternalInput")
    bo_a = nc.dram_tensor("bo_a", [D], F32, kind="ExternalInput")
    wq_b = nc.dram_tensor("Wq_b", [D, D], F32, kind="ExternalInput")
    wk_b = nc.dram_tensor("Wk_b", [D, c.DKV], F32, kind="ExternalInput")
    wv_b = nc.dram_tensor("Wv_b", [D, c.DKV], F32, kind="ExternalInput")
    wo_b = nc.dram_tensor("Wo_b", [D, D], F32, kind="ExternalInput")
    bo_b = nc.dram_tensor("bo_b", [D], F32, kind="ExternalInput")
    router_w = nc.dram_tensor("router_w", [c.E, D], F32, kind="ExternalInput")
    w_gate = nc.dram_tensor("w_gate", [c.E, D, c.I_MOE], F32, kind="ExternalInput")
    w_up = nc.dram_tensor("w_up", [c.E, D, c.I_MOE], F32, kind="ExternalInput")
    w_down = nc.dram_tensor("w_down", [c.E, c.I_MOE, D], F32, kind="ExternalInput")
    sh_gate = nc.dram_tensor("sh_gate", [D, c.I_SH], F32, kind="ExternalInput")
    sh_up = nc.dram_tensor("sh_up", [D, c.I_SH], F32, kind="ExternalInput")
    sh_down = nc.dram_tensor("sh_down", [c.I_SH, D], F32, kind="ExternalInput")
    sh_gate_w = nc.dram_tensor("sh_gate_w", [D, 1], F32, kind="ExternalInput")
    t_ident = nc.dram_tensor("t_ident", [128, 128], BF16, kind="ExternalInput")
    t_maskT = nc.dram_tensor("t_maskT", [128, 128], BF16, kind="ExternalInput")
    t_maskA = nc.dram_tensor("t_maskA", [128, 128], BF16, kind="ExternalInput")
    t_cvec = nc.dram_tensor("t_cvec", [c.SLOTS_A, 3, S], BF16, kind="ExternalInput")
    t_slopei = nc.dram_tensor("t_slopei", [128, c.SLOTS_A * c.QT], BF16, kind="ExternalInput")
    t_ctab = nc.dram_tensor("t_ctab", [128, T], BF16, kind="ExternalInput")
    t_stab = nc.dram_tensor("t_stab", [128, T], BF16, kind="ExternalInput")
    t_pswap = nc.dram_tensor("t_pswap", [128, 128], BF16, kind="ExternalInput")
    t_identf = nc.dram_tensor("t_identf", [2 * D // 128, 2 * D // 128], F32, kind="ExternalInput")
    t_utri = nc.dram_tensor("t_utri", [128, 128], BF16, kind="ExternalInput")
    t_ecrow = nc.dram_tensor("t_ecrow", [1, NT * 16], BF16, kind="ExternalInput")

    WCTA = c.SLOTS_A * 128
    WCTB = c.G * 64
    qaA_i = nc.dram_tensor("qaA_i", [c.SLOTS_A * 128 * NCORE, T], BF16)
    qaA_o = nc.dram_tensor("qaA_o", [c.SLOTS_A * 128 * NCORE, T], BF16)
    kaA_i = nc.dram_tensor("kaA_i", [c.SLOTS_A * 128 * NCORE, T], BF16)
    kaA_o = nc.dram_tensor("kaA_o", [c.SLOTS_A * 128 * NCORE, T], BF16)
    vaA_i = nc.dram_tensor("vaA_i", [NCORE * T, WCTA], BF16)
    vaA_o = nc.dram_tensor("vaA_o", [NCORE * T, WCTA], BF16)
    caA_i = nc.dram_tensor("caA_i", [NCORE * T, WCTA], BF16)
    caA_o = nc.dram_tensor("caA_o", [NCORE * T, WCTA], BF16)
    qaB_i = nc.dram_tensor("qaB_i", [WCTB * NCORE, T], BF16)
    qaB_o = nc.dram_tensor("qaB_o", [WCTB * NCORE, T], BF16)
    kaB_i = nc.dram_tensor("kaB_i", [64 * NCORE, T], BF16)
    kaB_o = nc.dram_tensor("kaB_o", [64 * NCORE, T], BF16)
    vaB_i = nc.dram_tensor("vaB_i", [NCORE * T, 64], BF16)
    vaB_o = nc.dram_tensor("vaB_o", [NCORE * T, 64], BF16)
    caB_i = nc.dram_tensor("caB_i", [NCORE * T, WCTB], BF16)
    caB_o = nc.dram_tensor("caB_o", [NCORE * T, WCTB], BF16)
    ECAP = c.E * c.CAP
    NTRASH = 4 * NT
    h_dram = nc.dram_tensor("h_dram", [T, D], BF16)
    hgath = nc.dram_tensor("hgath", [ECAP + NTRASH, D], BF16)
    comb_sl = nc.dram_tensor("comb_sl", [ECAP + NTRASH, c.E], F32)
    y_dram = nc.dram_tensor("y_dram", [ECAP + NTRASH, D], BF16)

    st_ = ExitStack()
    E_ = st_.enter_context

    max_act = max(c.IC * T, max(w // 128 for _, w in c.SSEG) * T)
    SCRB = max(CC * T, 3 * S, max_act + 8 * 512, 2 * S + (S // 128) * 64 + 64 + S)
    SILOFF = SCRB - 8 * 512
    KOFF = 2 * S + (S // 128) * 64
    x_sb = E_(nc.sbuf_tensor("x_sb", [128, NT * D], F32))
    scrA = E_(nc.sbuf_tensor("scrA", [128, CC * T], BF16))
    scrB = E_(nc.sbuf_tensor("scrB", [128, SCRB], BF16))
    w0 = E_(nc.sbuf_tensor("w0", [128, 8192], BF16))
    w1 = E_(nc.sbuf_tensor("w1", [128, 8192], BF16))
    wslots = [w0, w1]
    stg = [E_(nc.sbuf_tensor(f"stg{i}", [128, max(D, T)], BF16)) for i in range(2)]
    vstg = [E_(nc.sbuf_tensor("vstg0", [128, 512], BF16))] * 2
    psb = E_(nc.sbuf_tensor("psb", [128, 2048], BF16))
    ptsb = E_(nc.sbuf_tensor("ptsb", [128, 1024], BF16))
    NSIL = 8
    def silsl(i, wdt):
        return scrB[:, SILOFF + 512 * i:SILOFF + 512 * i + wdt]
    rowscr = E_(nc.sbuf_tensor("rowscr", [1, S], BF16))
    knm = E_(nc.sbuf_tensor("knm", [1, 1], F32))
    rcol = E_(nc.sbuf_tensor("rcol", [128, c.QT], F32))
    negR = E_(nc.sbuf_tensor("negR", [128, c.QT], F32))
    lrec = E_(nc.sbuf_tensor("lrec", [128, 1], F32))
    ln_stats = E_(nc.sbuf_tensor("ln_stats", [128, 6 * (D // 512)], F32))
    ln_mv = E_(nc.sbuf_tensor("ln_mv", [128, 2], F32))
    ln_negm = E_(nc.sbuf_tensor("ln_negm", [128, 1], F32))
    ln_std = E_(nc.sbuf_tensor("ln_std", [128, 1], F32))
    ln_rstd = E_(nc.sbuf_tensor("ln_rstd", [128, 1], F32))
    expv = E_(nc.sbuf_tensor("expv", [128, c.E], F32))
    nmx = E_(nc.sbuf_tensor("nmx", [128, 1], F32))
    mml = E_(nc.sbuf_tensor("mml", [128, 6], F32))
    ge01 = E_(nc.sbuf_tensor("ge01", [128, c.E], BF16))
    mskv = E_(nc.sbuf_tensor("mskv", [128, c.E], F32))
    gidx_f = E_(nc.sbuf_tensor("gidx_f", [128, c.E], F32))
    ecrow_sb = E_(nc.sbuf_tensor("ecrow_sb", [1, NT * 16], BF16))
    utri = E_(nc.sbuf_tensor("utri", [128, 128], BF16))
    cslot = E_(nc.sbuf_tensor("cslot", [128, ((c.CAP + 127) // 128) * c.E], F32))
    gmn = E_(nc.sbuf_tensor("gmn", [128, 2], F32))
    gmnu = E_(nc.sbuf_tensor("gmnu", [128, NT * 4], mybir.dt.uint32))
    comb = E_(nc.sbuf_tensor("comb", [128, NT * c.E], F32))
    sg_sb = E_(nc.sbuf_tensor("sg_sb", [128, NT], F32))
    sglog = E_(nc.sbuf_tensor("sglog", [128, NT], F32))
    EEP = (c.E + 2) // 2 * 2
    routerT = E_(nc.sbuf_tensor("routerT", [128, EEP * CC], BF16))
    bias_rows = E_(nc.sbuf_tensor("bias_rows", [65, 2 * D], BF16))
    bqkv_fm = E_(nc.sbuf_tensor("bqkv_fm", [128, 2 * CC], F32))
    tmpb = E_(nc.sbuf_tensor("tmpb", [65, 128], F32))
    bqkv_raw = tmpb
    identf = E_(nc.sbuf_tensor("identf", [2 * CC, 2 * CC], F32))
    ctab = E_(nc.sbuf_tensor("ctab", [128, T], BF16))
    stab = E_(nc.sbuf_tensor("stab", [128, T], BF16))
    ident = E_(nc.sbuf_tensor("ident", [128, 128], BF16))
    maskT = E_(nc.sbuf_tensor("maskT", [128, 128], BF16))
    maskA = E_(nc.sbuf_tensor("maskA", [128, 128], BF16))
    slopei = E_(nc.sbuf_tensor("slopei", [128, c.SLOTS_A * c.QT], BF16))
    pswap = E_(nc.sbuf_tensor("pswap", [128, 128], BF16))
    ones1 = E_(nc.sbuf_tensor("ones1", [1, 32], BF16))
    ones65 = E_(nc.sbuf_tensor("ones65", [65, 128], BF16))
    ones3 = E_(nc.sbuf_tensor("ones3", [3, 128], BF16))

    TQ = Sem(E_, nc, "TQ")
    VQ = Sem(E_, nc, "VQ")
    SQ = Sem(E_, nc, "SQ")
    PQ = Sem(E_, nc, "PQ")
    CCS = Sem(E_, nc, "CCS")
    SX = Sem(E_, nc, "SX")
    SCT = Sem(E_, nc, "SCT")
    GW = [Sem(E_, nc, f"GW{i}") for i in range(len(wslots))]
    SST = [Sem(E_, nc, f"SST{i}") for i in range(2)]
    _svt = Sem(E_, nc, "SVT0")
    SVT = [_svt, _svt]
    SQB = Sem(E_, nc, "SQB")
    SKB = Sem(E_, nc, "SKB")
    SVB = Sem(E_, nc, "SVB")
    SCX = Sem(E_, nc, "SCX")
    SCV = Sem(E_, nc, "SCV")
    SOUT = Sem(E_, nc, "SOUT")
    GB = Sem(E_, nc, "GB")
    GB2 = Sem(E_, nc, "GB2")
    SIX = Sem(E_, nc, "SIX")
    SIT = [Sem(E_, nc, "SIT0"), Sem(E_, nc, "SIT1")]
    SGA = Sem(E_, nc, "SGA")
    SCS = Sem(E_, nc, "SCS")
    SYL = [Sem(E_, nc, f"SYL{i}") for i in range(5)]

    CVOFF = 3 * S
    assert CVOFF + S <= SCRB

    block = E_(nc.Block())

    ps_free = {}
    wfree = [None, None]
    slot_rr = [0]
    stg_ct = [0, 0]
    vstg_ct = VstgCt()
    psb_free = [(TQ, 0), (TQ, 0)]
    ptsb_free = [(TQ, 0), (TQ, 0)]
    scrB_rel = [0]
    scrB_vrel = [0]
    cvec_rel = [0]
    six_rel = [0]
    sit_rel = [0, 0]
    sga_rel = [0]
    hg_rel = [0]
    scs_rel = [0]
    y_sga_gate = [0]
    ctx_region_rel = [0]

    SCBUF = []

    def sc(f):
        SCBUF.append(f)

    def sc_flush():
        if not SCBUF:
            return
        ops = SCBUF[:]
        SCBUF.clear()

        @block.scalar
        def _(s_):
            for f in ops:
                f(s_)

    def init_free(ps):
        ps_free[id(ps)] = (TQ, 0)

    def mark_free(ps, sem):
        ps_free[id(ps)] = (sem, sem.n)

    def next_slot():
        s = slot_rr[0]
        slot_rr[0] = (s + 1) % len(wslots)
        return s

    def wview(slot_i, a, b):
        return bass.AP(wslots[slot_i], 0, [[8192, 128], [b, a], [1, b]])

    def wload(slot_i, in_ap, a, b):
        sem = GW[slot_i]
        fr = wfree[slot_i]

        @block.gpsimd
        def _(g):
            if fr is not None:
                g.wait_ge(fr[0].h, fr[1])
            g.dma_start(out=wview(slot_i, a, b), in_=in_ap).then_inc(sem.h, 16)

        sem.n += 16
        return sem.n

    def release_slot(slot_i, sem):
        wfree[slot_i] = (sem, sem.n)

    @block.sync
    def _(sy):
        sy.dma_start(out=bass.AP(x_sb, 0, [[NT * D, 128], [D, NT], [1, D]]),
                     in_=x_in.ap().rearrange("(n p) d -> p n d", p=128)).then_inc(SX.h, 16)
        for dst, src in [
            (ident[:, :], t_ident.ap()),
            (maskT[:, :], t_maskT.ap()),
            (maskA[:, :], t_maskA.ap()),
            (slopei[:, :], t_slopei.ap()),
            (ctab[:, :], t_ctab.ap()),
            (stab[:, :], t_stab.ap()),
            (pswap[:, :], t_pswap.ap()),
            (identf[:, :], t_identf.ap()),
            (bqkv_raw[0:2 * CC, :], bqkv.ap().rearrange("(r p) -> r p", p=128)[0:2 * CC, :]),
            (utri[:, :], t_utri.ap()),
            (ecrow_sb[:, :], t_ecrow.ap()),
        ]:
            sy.dma_start(out=dst, in_=src).then_inc(SCT.h, 16)
    SX.n += 16
    SCT.n += 16 * 11

    bsrc = {0: bqkv.ap().rearrange("(t d) -> t d", t=3)[2:3, :],
            32: bo_a.ap().rearrange("(t d) -> t d", t=1),
            64: bo_b.ap().rearrange("(t d) -> t d", t=1)}
    lo_rel = [0]
    bq_tq = [0]

    @block.gpsimd
    def _(g):
        for r, ap_ in bsrc.items():
            g.dma_start(out=bias_rows[r:r + 1, 0:D], in_=ap_).then_inc(GB.h, 16)
        g.dma_start(out=stg[0][0:c.E, 0:D], in_=router_w.ap()).then_inc(GB.h, 16)
        g.dma_start(out=stg[0][c.E:c.E + 1, 0:D],
                    in_=sh_gate_w.ap().rearrange("d o -> o d")).then_inc(GB.h, 16)
        g.wait_ge(SCT.h, SCT.n)
    GB.n += 16 * 5

    with nc.psum_tensor(_u("ps_bq"), [128, 2 * CC], F32) as ps_bq:
        @block.tensor
        def _(t):
            t = _TT(t)
            t.wait_ge(SCT.h, SCT.n)
            t.matmul(ps_bq[:, :], bqkv_raw[0:2 * CC, :], identf[:, :], start=True, stop=True)
            t.inc(TQ.h)
        TQ.n += 1
        tq_bq = TQ.n
        bq_tq[0] = TQ.n

        @block.vector
        def _(v):
            v.wait_ge(TQ.h, tq_bq)
            v.tensor_copy(bqkv_fm[:, :], ps_bq[:, :]).then_inc(VQ.h, 1)
        VQ.n += 1
    bqkv_fm_v = VQ.n

    for ch in range(D // 128):
        @block.gpsimd
        def _(g):
            if lo_rel[0]:
                g.wait_ge(VQ.h, lo_rel[0])
            g.wait_ge(TQ.h, bq_tq[0])
            for r, ap_ in bsrc.items():
                g.dma_start(out=tmpb[r:r + 1, 0:128],
                            in_=ap_[:, 128 * ch:128 * (ch + 1)]).then_inc(GB2.h, 16)
        GB2.n += 16 * 3
        gb_now = GB2.n

        @block.vector
        def _(v):
            v.wait_ge(GB2.h, gb_now)
            v.wait_ge(GB.h, 16 * 5)
            for r in (0, 32, 64):
                v.wait_ge(VQ.h, VQ.n)
                v.tensor_tensor(tmpb[r:r + 1, 0:128], tmpb[r:r + 1, 0:128],
                                bias_rows[r:r + 1, 128 * ch:128 * (ch + 1)],
                                ALU.subtract).then_inc(VQ.h, 1)
                VQ.n += 1
                v.wait_ge(VQ.h, VQ.n)
                v.tensor_copy(bias_rows[r:r + 1, D + 128 * ch:D + 128 * ch + 128],
                              tmpb[r:r + 1, 0:128]).then_inc(VQ.h, 1)
                VQ.n += 1
        lo_rel[0] = VQ.n

    @block.vector
    def _(v):
        v.memset(ones1[:, :], 1.0)
        v.memset(ones65[:, :], 1.0)
        v.memset(stg[1][:, 0:D], 0.0)
        v.memset(expv[:, :], 0.0)
        v.memset(ones3[:, :], 1.0).then_inc(VQ.h, 1)
    VQ.n += 1
    memset_v = VQ.n

    @block.sync
    def _(sy):
        sy.wait_ge(VQ.h, memset_v)
        NROW = c.E * c.CAP + 4 * NT
        for k0 in range(0, NROW, 128):
            w_ = min(128, NROW - k0)
            sy.dma_start(out=hgath.ap()[k0:k0 + w_, :],
                         in_=stg[1][0:w_, 0:D]).then_inc(SIX.h, 16)
            sy.dma_start(out=comb_sl.ap()[k0:k0 + w_, :],
                         in_=expv[0:w_, :]).then_inc(SIX.h, 16)
            SIX.n += 32
        sy.dma_start(out=y_dram.ap()[c.E * c.CAP:NROW, :],
                     in_=stg[1][0:4 * NT, 0:D]).then_inc(SIX.h, 16)
        SIX.n += 16
    six_rel[0] = SIX.n
    hg_rel[0] = SIX.n
    bias_split_v = max(bqkv_fm_v, lo_rel[0])

    EE = c.E + 1
    with nc.psum_tensor(_u("ps_rt"), [128, EEP * CC], BF16) as ps_rt:
        @block.tensor
        def _(t):
            t = _TT(t)
            t.wait_ge(GB.h, 16 * 5)
            t.wait_ge(SCT.h, SCT.n)
            t.wait_ge(VQ.h, VQ.n)
            for cc in range(CC):
                t.transpose(ps_rt[:, cc * EEP:cc * EEP + EE],
                            stg[0][0:EE, 128 * cc:128 * (cc + 1)],
                            ident[0:EE, 0:EE])
            t.inc(TQ.h)
        TQ.n += 1
        tq_rt = TQ.n

        @block.vector
        def _(v):
            v.wait_ge(TQ.h, tq_rt)
            v.tensor_copy(bass.AP(routerT, 0, [[EEP * CC, 128], [EEP, CC], [1, EE]]),
                          bass.AP(ps_rt, 0, [[EEP * CC, 128], [EEP, CC], [1, EE]])).then_inc(VQ.h, 1)
        VQ.n += 1
    startup_v = VQ.n

    ln_state = {"stg_trel": [0, 0], "pl_vrel": [0, 0]}

    def ln_phase(dump_h=False):
        nch = D // 512
        tq0 = TQ.n

        with nc.psum_tensor(_u("ps_ln0"), [128, 1024], BF16) as pl0, \
             nc.psum_tensor(_u("ps_ln1"), [128, 1024], BF16) as pl1:
            pls = [pl0, pl1]
            for tile in range(NT):
                sl = tile % 2

                @block.vector
                def _(v):
                    if tile == 0:
                        v.wait_ge(SX.h, 16)
                        v.wait_ge(TQ.h, tq0)
                        v.wait_ge(VQ.h, VQ.n)
                        v.wait_ge(SIX.h, six_rel[0])
                        v.wait_ge(SIT[0].h, sit_rel[0])
                        v.wait_ge(SIT[1].h, sit_rel[1])
                        v.wait_ge(SGA.h, sga_rel[0])
                    v.wait_ge(TQ.h, ln_state["stg_trel"][sl])
                    for ch in range(nch):
                        v.bn_stats(ln_stats[:, 6 * ch:6 * ch + 6],
                                   x_sb[:, tile * D + 512 * ch:tile * D + 512 * (ch + 1)]
                                   ).then_inc(VQ.h, 1)
                    v.wait_ge(VQ.h, VQ.n + nch)
                    v.bn_aggr(ln_mv[:, :], ln_stats[:, :]).then_inc(VQ.h, 1)
                    v.wait_ge(VQ.h, VQ.n + nch + 1)
                    v.tensor_scalar(ln_negm[:, :], ln_mv[:, 0:1], -1.0, None,
                                    ALU.mult).then_inc(VQ.h, 1)
                    v.tensor_scalar(ln_std[:, :], ln_mv[:, 1:2], 1e-5, None,
                                    ALU.add).then_inc(VQ.h, 1)
                VQ.n += nch + 3
                vq_stats = VQ.n

                def f(s_, vq_stats=vq_stats):
                    s_.wait_ge(VQ.h, vq_stats)
                    s_.activation(out=ln_rstd[:, :], in_=ln_std[:, :],
                                  func=AF.Sqrt).then_inc(SQ.h, 1)
                sc(f)
                SQ.n += 1
                sq_std = SQ.n

                @block.vector
                def _(v):
                    v.wait_ge(SQ.h, sq_std)
                    v.wait_ge(SST[sl].h, 16 * stg_ct[sl])
                    v.reciprocal(ln_std[:, :], ln_rstd[:, :]).then_inc(VQ.h, 1)
                    v.wait_ge(VQ.h, VQ.n + 1)
                    v.tensor_scalar(stg[sl][:, 0:D], x_sb[:, tile * D:(tile + 1) * D],
                                    ln_negm[:, :], ln_std[:, :], ALU.add,
                                    ALU.mult).then_inc(VQ.h, 1)
                VQ.n += 2
                vq_apply = VQ.n

                if dump_h:
                    @block.sync
                    def _(sy):
                        sy.wait_ge(VQ.h, vq_apply)
                        sy.dma_start(out=h_dram.ap()[128 * tile:128 * (tile + 1), :],
                                     in_=stg[sl][:, 0:D]).then_inc(SST[sl].h, 16)
                    SST[sl].n += 16
                    stg_ct[sl] += 1

                nb = CC // 8
                for bb in range(nb):
                    pi = (tile * nb + bb) % 2
                    pb = pls[pi]

                    @block.tensor
                    def _(t):
                        t = _TT(t)
                        t.wait_ge(VQ.h, max(vq_apply, ln_state["pl_vrel"][pi]))
                        for k in range(8):
                            cc = 8 * bb + k
                            t.transpose(pb[:, 128 * k:128 * (k + 1)],
                                        stg[sl][:, 128 * cc:128 * (cc + 1)], ident[:, :])
                        t.inc(TQ.h)
                    TQ.n += 1
                    tq_tr = TQ.n

                    @block.vector
                    def _(v):
                        v.wait_ge(TQ.h, tq_tr)
                        outap = bass.AP(scrA, 8 * bb * T + tile * 128,
                                        [[CC * T, 128], [T, 8], [1, 128]])
                        v.tensor_copy(outap,
                                      bass.AP(pb, 0, [[1024, 128], [128, 8], [1, 128]])
                                      ).then_inc(VQ.h, 1)
                    VQ.n += 1
                    ln_state["pl_vrel"][pi] = VQ.n
                ln_state["stg_trel"][sl] = TQ.n
        return VQ.n

    def attn_sublayer(isA):
        h_v = ln_phase()

        if isA:
            qa_i, ka_i, va_i = qaA_i, kaA_i, vaA_i
            qa_o, ka_o, va_o = qaA_o, kaA_o, vaA_o
            ca_i, ca_o = caA_i, caA_o
            wct = WCTA
        else:
            qa_i, ka_i, va_i = qaB_i, kaB_i, vaB_i
            qa_o, ka_o, va_o = qaB_o, kaB_o, vaB_o
            ca_i, ca_o = caB_i, caB_o
            wct = WCTB

        with nc.psum_tensor(_u("ps_p0"), [128, 512], F32) as pp0, \
             nc.psum_tensor(_u("ps_p1"), [128, 512], F32) as pp1, \
             nc.psum_tensor(_u("ps_p2"), [128, 512], F32) as pp2, \
             nc.psum_tensor(_u("ps_p3"), [128, 512], F32) as pp3, \
             nc.psum_tensor(_u("ps_w0"), [128, 512], F32) as pw0, \
             nc.psum_tensor(_u("ps_w1"), [128, 512], F32) as pw1:
            pps = [pp0, pp1, pp2, pp3]
            pws = [pw0, pw1]
            for p in pps + pws:
                init_free(p)

            def qk_proj(weight, col_off, ncols, out_dram, bias_off, scale, rope):
                for blk in range(ncols // 512):
                    slot_i = next_slot()
                    wtok = wload(slot_i,
                                 weight.ap()[:, col_off + 512 * blk:col_off + 512 * (blk + 1)]
                                 .rearrange("(cc p) f -> p cc f", p=128),
                                 CC, 512)
                    for fo in range(4):
                        fglob = blk * 4 + fo
                        sl = fglob % 2
                        pr = (pps[0], pps[1]) if fo % 2 == 0 else (pps[2], pps[3])

                        @block.tensor
                        def _(t):
                            t = _TT(t)
                            t.wait_ge(GW[slot_i].h, wtok)
                            t.wait_ge(VQ.h, max(h_v, bias_split_v))
                            for th in range(TH):
                                fr = ps_free[id(pr[th])]
                                t.wait_ge(fr[0].h, fr[1])
                            wap = wview(slot_i, CC, 512)
                            for cc in range(CC):
                                for th in range(TH):
                                    t.matmul(pr[th][:, 0:thw[th]],
                                             wap[:, cc, 128 * fo:128 * (fo + 1)],
                                             scrA[:, cc * T + 512 * th:cc * T + 512 * th + thw[th]],
                                             start=(cc == 0), stop=(cc == CC - 1))
                            t.inc(TQ.h)
                        TQ.n += 1
                        tq_mm = TQ.n

                        for th in range(TH):
                            ps = pr[th]

                            @block.vector
                            def _(v):
                                v.wait_ge(TQ.h, tq_mm)
                                v.wait_ge(SST[sl].h, 16 * stg_ct[sl])
                                if bias_off is not None:
                                    v.tensor_scalar(stg[sl][:, 512 * th:512 * th + thw[th]],
                                                    ps[:, 0:thw[th]],
                                                    bqkv_fm[:, bias_off + fglob:bias_off + fglob + 1],
                                                    scale, ALU.add, ALU.mult).then_inc(VQ.h, 1)
                                else:
                                    v.tensor_scalar(stg[sl][:, 512 * th:512 * th + thw[th]],
                                                    ps[:, 0:thw[th]], scale, None,
                                                    ALU.mult).then_inc(VQ.h, 1)
                            VQ.n += 1
                            mark_free(ps, VQ)
                        sq_evac = VQ.n
                        dma_wait_sem, dma_wait_tok = VQ, sq_evac

                        if rope:
                            vq_t0 = []
                            for th in range(TH):
                                pw = pws[th % 2]

                                @block.tensor
                                def _(t):
                                    t = _TT(t)
                                    t.wait_ge(VQ.h, sq_evac)
                                    fr = ps_free[id(pw)]
                                    t.wait_ge(fr[0].h, fr[1])
                                    t.matmul(pw[:, 0:thw[th]], pswap[:, :],
                                             stg[sl][:, 512 * th:512 * th + thw[th]],
                                             start=True, stop=True)
                                    t.inc(TQ.h)
                                TQ.n += 1
                                tq_sw = TQ.n

                                @block.vector
                                def _(v):
                                    v.wait_ge(TQ.h, tq_sw)
                                    v.wait_ge(VQ.h, VQ.n)
                                    v.scalar_tensor_tensor(
                                        ptsb[:, 512 * th:512 * th + thw[th]],
                                        pw[:, 0:thw[th]], 1.0,
                                        stab[:, 512 * th:512 * th + thw[th]],
                                        ALU.mult, ALU.mult).then_inc(VQ.h, 1)
                                VQ.n += 1
                                mark_free(pw, VQ)
                            vq_ropeT = VQ.n

                            @block.gpsimd
                            def _(g):
                                g.wait_ge(VQ.h, sq_evac)
                                g.wait_ge(VQ.h, VQ.n)
                                g.wait_ge(TQ.h, TQ.n)
                                g.tensor_mul(psb[:, 0:T], stg[sl][:, 0:T],
                                             ctab[:, 0:T]).then_inc(PQ.h, 1)
                            PQ.n += 1
                            pq_u = PQ.n
                            tq_swaps = TQ.n

                            @block.vector
                            def _(v):
                                v.wait_ge(PQ.h, pq_u)
                                v.wait_ge(VQ.h, vq_ropeT)
                                v.wait_ge(TQ.h, tq_swaps)
                                v.tensor_tensor(stg[sl][:, 0:T], psb[:, 0:T],
                                                ptsb[:, 0:T], ALU.add).then_inc(VQ.h, 1)
                            VQ.n += 1
                            dma_wait_sem, dma_wait_tok = VQ, VQ.n

                        @block.sync
                        def _(sy, sem=dma_wait_sem, tok=dma_wait_tok):
                            sy.wait_ge(sem.h, tok)
                            sy.dma_start(out=out_dram.ap()[128 * fglob:128 * (fglob + 1), :],
                                         in_=stg[sl][:, 0:T]).then_inc(SST[sl].h, 16)
                        SST[sl].n += 16
                        stg_ct[sl] += 1
                    release_slot(slot_i, TQ)

            if isA:
                qk_proj(wqkv, 0, D, qa_i, 0, SCALE_A, False)
                qk_proj(wqkv, D, D, ka_i, CC, SCALE_A, False)
            else:
                qk_proj(wq_b, 0, D, qa_i, None, SCALE_B, True)
                qk_proj(wk_b, 0, c.DKV, ka_i, None, SCALE_B, True)
                psb_free[0] = psb_free[1] = (VQ, VQ.n)
                ptsb_free[0] = ptsb_free[1] = (VQ, VQ.n)

            vcols = D if isA else c.DKV
            wpd = WCTA if isA else 64
            for blk in range(vcols // 512):
                slot_i = next_slot()
                wsrc = (wqkv.ap()[:, 2 * D + 512 * blk:2 * D + 512 * (blk + 1)] if isA
                        else wv_b.ap()[:, 512 * blk:512 * (blk + 1)])
                wtok = wload(slot_i, wsrc.rearrange("(cc p) f -> p cc f", p=128), CC, 512)
                for tp in range(cdiv(NT, 2)):
                    tiles = [t_ for t_ in (2 * tp, 2 * tp + 1) if t_ < NT]
                    pr = (pps[0], pps[1]) if tp % 2 == 0 else (pps[2], pps[3])

                    @block.tensor
                    def _(t):
                        t = _TT(t)
                        t.wait_ge(GW[slot_i].h, wtok)
                        t.wait_ge(VQ.h, max(h_v, bias_split_v))
                        for ti in range(len(tiles)):
                            fr = ps_free[id(pr[ti])]
                            t.wait_ge(fr[0].h, fr[1])
                        wap = wview(slot_i, CC, 512)
                        for cc in range(CC):
                            for ti, tile in enumerate(tiles):
                                t.matmul(pr[ti][:, 0:512],
                                         scrA[:, cc * T + tile * 128:cc * T + (tile + 1) * 128],
                                         wap[:, cc, 0:512],
                                         start=(cc == 0), stop=(cc == CC - 1) and not isA)
                        if isA:
                            for ti in range(len(tiles)):
                                t.matmul(pr[ti][:, 0:512], ones65[0:1, :],
                                         bias_rows[0:1, 512 * blk:512 * (blk + 1)],
                                         start=False, stop=False)
                                t.matmul(pr[ti][:, 0:512], ones65[0:1, :],
                                         bias_rows[0:1, D + 512 * blk:D + 512 * (blk + 1)],
                                         start=False, stop=True)
                        t.inc(TQ.h)
                    TQ.n += 1
                    tq_mm = TQ.n

                    for ti, tile in enumerate(tiles):
                        ps = pr[ti]
                        vsl = 0

                        @block.vector
                        def _(v):
                            v.wait_ge(TQ.h, tq_mm)
                            v.wait_ge(SVT[vsl].h, 16 * vstg_ct[vsl])
                            v.tensor_copy(vstg[vsl][:, 0:512], ps[:, 0:512]).then_inc(VQ.h, 1)
                        VQ.n += 1
                        mark_free(ps, VQ)
                        sq_ev = VQ.n
                        nd = 512 // wpd

                        @block.sync
                        def _(sy):
                            sy.wait_ge(VQ.h, sq_ev)
                            for dd in range(nd):
                                d = (512 * blk) // wpd + dd
                                sy.dma_start(
                                    out=va_i.ap()[T * d + 128 * tile:T * d + 128 * (tile + 1), :],
                                    in_=vstg[vsl][:, wpd * dd:wpd * (dd + 1)]).then_inc(SVT[vsl].h, 16)
                        SVT[vsl].n += 16 * nd
                        vstg_ct[vsl] += nd
                release_slot(slot_i, TQ)

        @block.gpsimd
        def _(g):
            for s in (SST[0], SST[1], SVT[0], SVT[1]):
                g.wait_ge(s.h, s.n)
            with allow_ici_all_to_all():
                for i_, o_ in [(qa_i, qa_o), (ka_i, ka_o), (va_i, va_o)]:
                    g.collective_compute("AllToAll", ALU.bypass,
                                         replica_groups=[list(range(NCORE))],
                                         ins=[i_.ap()], outs=[o_.ap()]).then_inc(CCS.h, 1)
        CCS.n += 3
        cc_qkv = CCS.n

        hd = 128 if isA else 64
        nqh = 1 if isA else c.G
        units = ([(b, j) for b in range(c.B) for j in range(c.SLOTS_A)] if isA
                 else [(b, 0) for b in range(c.B)])

        with nc.psum_tensor(_u("ps_sc"), [128, 2048], F32) as ss2, \
             nc.psum_tensor(_u("ps_t0"), [128, 512], BF16) as pt0, \
             nc.psum_tensor(_u("ps_t1"), [128, 512], BF16) as pt1, \
             nc.psum_tensor(_u("ps_c0"), [128, hd + 1], F32) as cx0, \
             nc.psum_tensor(_u("ps_c1"), [128, hd + 1], F32) as cx1:
            pts, cxs = [pt0, pt1], [cx0, cx1]
            for p in pts + cxs:
                init_free(p)
            ss_free = [(TQ, 0), (TQ, 0)]
            sctr = [0]
            tctr = [0]

            for ui, (b, j) in enumerate(units):
                @block.sync
                def _(sy):
                    sy.wait_ge(CCS.h, cc_qkv)
                    sy.wait_ge(TQ.h, max(scrB_rel[0], cvec_rel[0]))
                    sy.wait_ge(VQ.h, scrB_vrel[0])
                    for s_ in SYL:
                        sy.wait_ge(s_.h, s_.n)
                    ns = c.NSRC
                    if isA:
                        sy.dma_start(out=scrB[0:3, CVOFF:CVOFF + S],
                                     in_=t_cvec.ap()[j]).then_inc(SCV.h, 16)
                    if isA:
                        inq = bass.AP(qa_o, (c.SLOTS_A * ns * b + j) * 128 * T,
                                      [[T, 128], [c.SLOTS_A * 128 * T, ns], [1, T]])
                        sy.dma_start(out=bass.AP(scrB, 0, [[SCRB, 128], [T, ns], [1, T]]),
                                     in_=inq).then_inc(SQB.h, 16)
                        ink = bass.AP(ka_o, (c.SLOTS_A * ns * b + j) * 128 * T,
                                      [[T, 128], [c.SLOTS_A * 128 * T, ns], [1, T]])
                        sy.dma_start(out=bass.AP(scrB, S, [[SCRB, 128], [T, ns], [1, T]]),
                                     in_=ink).then_inc(SKB.h, 16)
                        inv = bass.AP(va_o, (b * S) * WCTA + j * 128,
                                      [[WCTA, 128], [128 * WCTA, S // 128], [1, 128]])
                        sy.dma_start(out=bass.AP(scrB, 2 * S, [[SCRB, 128], [128, S // 128], [1, 128]]),
                                     in_=inv).then_inc(SVB.h, 16)
                    else:
                        for g2 in range(c.G // 2):
                            inq = bass.AP(qa_o, (WCTB * ns * b + g2 * 128) * T,
                                          [[T, 128], [WCTB * T, ns], [1, T]])
                            sy.dma_start(out=bass.AP(scrB, g2 * S, [[SCRB, 128], [T, ns], [1, T]]),
                                         in_=inq).then_inc(SQB.h, 16)
                        ink = bass.AP(ka_o, (64 * ns * b) * T,
                                      [[T, 64], [64 * T, ns], [1, T]])
                        sy.dma_start(out=bass.AP(scrB, KOFF, [[SCRB, 64], [T, ns], [1, T]]),
                                     in_=ink).then_inc(SKB.h, 16)
                        sy.dma_start(out=bass.AP(scrB, 64 * SCRB + KOFF, [[SCRB, 64], [T, ns], [1, T]]),
                                     in_=ink).then_inc(SKB.h, 16)
                        inv = bass.AP(va_o, (b * S) * 64,
                                      [[64, 128], [128 * 64, S // 128], [1, 64]])
                        sy.dma_start(out=bass.AP(scrB, 2 * S, [[SCRB, 128], [64, S // 128], [1, 64]]),
                                     in_=inv).then_inc(SVB.h, 16)
                if isA:
                    SCV.n += 16
                SQB.n += 16 * (1 if isA else c.G // 2)
                SKB.n += 16 if isA else 32
                SVB.n += 16
                qtok, ktok, vtok = SQB.n, SKB.n, SVB.n
                cvtok = SCV.n
                def norm_row(row_sb, src_fn, pbase, pcnt, ld_sem, ld_tok, kscale):
                    for ch in range(S // 512):
                        def f(s_, ch=ch, ld_sem=ld_sem, ld_tok=ld_tok, tqn=TQ.n,
                              pbase=pbase, pcnt=pcnt, src_fn=src_fn,
                              pfr=psb_free[0]):
                            s_.wait_ge(ld_sem.h, ld_tok)
                            s_.wait_ge(TQ.h, tqn)
                            if pfr[1]:
                                s_.wait_ge(pfr[0].h, pfr[1])
                            s_.activation(out=psb[pbase:pbase + pcnt, 0:512],
                                          in_=src_fn(ch), func=AF.Square).then_inc(SQ.h, 1)
                        sc(f)
                        SQ.n += 1
                        sq_sq = SQ.n

                        @block.tensor
                        def _(t):
                            t = _TT(t)
                            t.wait_ge(SQ.h, sq_sq)
                            t.wait_ge(VQ.h, VQ.n)
                            fr = ss_free[0]
                            t.wait_ge(fr[0].h, fr[1])
                            t.matmul(ss2[0:1, 0:512], maskT[pbase:pbase + pcnt, 127:128],
                                     psb[pbase:pbase + pcnt, 0:512],
                                     start=True, stop=True)
                            t.inc(TQ.h)
                        TQ.n += 1
                        psb_free[0] = (TQ, TQ.n)

                        @block.vector
                        def _(v):
                            v.wait_ge(TQ.h, TQ.n)
                            v.wait_ge(VQ.h, VQ.n)
                            if kscale:
                                v.tensor_scalar(row_sb[0:1, 512 * ch:512 * (ch + 1)],
                                                ss2[0:1, 0:512], knm[0:1, 0:1], None,
                                                ALU.mult).then_inc(VQ.h, 1)
                            else:
                                v.tensor_copy(row_sb[0:1, 512 * ch:512 * (ch + 1)],
                                              ss2[0:1, 0:512]).then_inc(VQ.h, 1)
                        VQ.n += 1
                        ss_free[0] = (VQ, VQ.n)

                if isA:
                    norm_row(rowscr, lambda ch: scrB[:, S + 512 * ch:S + 512 * (ch + 1)],
                             0, 128, SKB, ktok, False)
                else:
                    norm_row(rowscr, lambda ch: scrB[0:64, KOFF + 512 * ch:KOFF + 512 * (ch + 1)],
                             0, 64, SKB, ktok, False)

                @block.vector
                def _(v):
                    v.wait_ge(VQ.h, VQ.n)
                    v.reduce_max(knm[:, :], rowscr[0:1, 0:S], axis=AX.X).then_inc(VQ.h, 1)
                VQ.n += 1

                for qh in range(nqh):
                    if isA:
                        norm_row(rowscr, lambda ch: scrB[:, 512 * ch:512 * (ch + 1)],
                                 0, 128, SQB, qtok, True)
                    else:
                        g2, qb = qh // 2, 64 * (qh % 2)
                        norm_row(rowscr,
                                 lambda ch, g2=g2, qb=qb: scrB[qb:qb + 64,
                                                               g2 * S + 512 * ch:g2 * S + 512 * (ch + 1)],
                                 qb, 64, SQB, qtok, True)
                    vq_qnk = VQ.n

                    ptR = pts[0]

                    @block.tensor
                    def _(t):
                        t = _TT(t)
                        t.wait_ge(VQ.h, vq_qnk)
                        fr = ps_free[id(ptR)]
                        t.wait_ge(fr[0].h, fr[1])
                        for qt in range(c.QT):
                            t.transpose(ptR[:, 2 * qt:2 * qt + 1],
                                        rowscr[0:1, 128 * qt:128 * (qt + 1)],
                                        ones1[0:1, 0:1])
                        t.inc(TQ.h)
                    TQ.n += 1
                    tq_R = TQ.n

                    def f(s_, tq_R=tq_R, vqn=VQ.n, ptR=ptR):
                        s_.wait_ge(TQ.h, tq_R)
                        s_.wait_ge(VQ.h, vqn)
                        s_.activation(out=rcol[:, :],
                                      in_=bass.AP(ptR, 0, [[512, 128], [2, c.QT]]),
                                      func=AF.Sqrt).then_inc(SQ.h, 1)
                    sc(f)
                    SQ.n += 1
                    mark_free(ptR, SQ)
                    sq_R = SQ.n

                    @block.vector
                    def _(v):
                        v.wait_ge(SQ.h, sq_R)
                        v.wait_ge(TQ.h, TQ.n)
                        if isA:
                            v.scalar_tensor_tensor(negR[:, :], rcol[:, :], -1.0,
                                                   slopei[:, j * c.QT:(j + 1) * c.QT],
                                                   ALU.mult, ALU.subtract).then_inc(VQ.h, 1)
                        else:
                            v.tensor_scalar(negR[:, :], rcol[:, :], -1.0, None,
                                            ALU.mult).then_inc(VQ.h, 1)
                    VQ.n += 1
                    vq_negR = VQ.n

                    blocks = []
                    for qt in range(c.QT):
                        nk = (qt + 1) * 128
                        nb = cdiv(nk, 1024)
                        for kb in range(nb):
                            blocks.append({"qt": qt, "kb": kb,
                                           "w": min(1024, nk - 1024 * kb),
                                           "diag": kb == nb - 1})
                    NB = len(blocks)

                    def emit_score(B):
                        sl = sctr[0] % 2
                        sctr[0] += 1
                        B["sl"] = sl
                        soff = 1024 * sl
                        B["soff"] = soff
                        qt, kb, w = B["qt"], B["kb"], B["w"]

                        @block.tensor
                        def _(t):
                            t = _TT(t)
                            fr = ss_free[sl]
                            t.wait_ge(fr[0].h, fr[1])
                            t.wait_ge(VQ.h, vq_negR)
                            t.wait_ge(SVB.h, vtok)
                            if isA:
                                t.wait_ge(SCV.h, cvtok)
                            nh2 = cdiv(w, 512)
                            for h2 in range(nh2):
                                wh = min(512, w - 512 * h2)
                                kb512 = 1024 * kb + 512 * h2
                                if isA:
                                    t.matmul(ss2[:, soff + 512 * h2:soff + 512 * h2 + wh],
                                             scrB[:, 128 * qt:128 * (qt + 1)],
                                             scrB[:, S + kb512:S + kb512 + wh],
                                             start=True, stop=False)
                                else:
                                    g2, qb = qh // 2, 64 * (qh % 2)
                                    t.matmul(ss2[:, soff + 512 * h2:soff + 512 * h2 + wh],
                                             scrB[qb:qb + 64,
                                                  g2 * S + 128 * qt:g2 * S + 128 * (qt + 1)],
                                             scrB[qb:qb + 64, KOFF + kb512:KOFF + kb512 + wh],
                                             start=True, stop=not isA)
                            if isA:
                                for h2 in range(nh2):
                                    wh = min(512, w - 512 * h2)
                                    kb512 = 1024 * kb + 512 * h2
                                    t.matmul(ss2[:, soff + 512 * h2:soff + 512 * h2 + wh],
                                             ones3[:, :],
                                             scrB[0:3, CVOFF + kb512:CVOFF + kb512 + wh],
                                             start=False, stop=True)
                            t.inc(TQ.h)
                        TQ.n += 1
                        tq_s = TQ.n

                        vq_msk = 0
                        if isA and B["diag"]:
                            @block.vector
                            def _(v):
                                v.wait_ge(TQ.h, tq_s)
                                v.tensor_tensor(ss2[:, soff + w - 128:soff + w],
                                                ss2[:, soff + w - 128:soff + w],
                                                maskA[:, :], ALU.add).then_inc(VQ.h, 1)
                            VQ.n += 1
                            vq_msk = VQ.n

                        def f(s_, tq_s=tq_s, vq_msk=vq_msk, fr=psb_free[sl],
                              soff=soff, w=w, qt=qt):
                            s_.wait_ge(TQ.h, tq_s)
                            if vq_msk:
                                s_.wait_ge(VQ.h, vq_msk)
                            if fr[1]:
                                s_.wait_ge(fr[0].h, fr[1])
                            s_.activation(out=psb[:, soff:soff + w],
                                          in_=ss2[:, soff:soff + w], func=AF.Exp,
                                          bias=negR[:, qt:qt + 1], scale=1.0
                                          ).then_inc(SQ.h, 1)
                        sc(f)
                        SQ.n += 1
                        ss_free[sl] = (SQ, SQ.n)
                        B["sq_exp"] = SQ.n

                    def emit_trev(B):
                        qt, kb, w, soff, sl = B["qt"], B["kb"], B["w"], B["soff"], B["sl"]
                        B["groups"] = []
                        for g5 in range(cdiv(w, 512)):
                            wg = min(512, w - 512 * g5)
                            ptl = tctr[0] % 2
                            ptp = pts[ptl]
                            tctr[0] += 1

                            @block.tensor
                            def _(t):
                                t = _TT(t)
                                t.wait_ge(SQ.h, B["sq_exp"])
                                fr = ps_free[id(ptp)]
                                t.wait_ge(fr[0].h, fr[1])
                                for k in range(wg // 128):
                                    t.transpose(ptp[:, 128 * k:128 * (k + 1)],
                                                psb[:, soff + 512 * g5 + 128 * k:
                                                    soff + 512 * g5 + 128 * (k + 1)],
                                                ident[:, :])
                                t.inc(TQ.h)
                            TQ.n += 1
                            if g5 == cdiv(w, 512) - 1:
                                psb_free[sl] = (TQ, TQ.n)
                            tq_tr = TQ.n

                            @block.vector
                            def _(v):
                                v.wait_ge(TQ.h, tq_tr)
                                fr = ptsb_free[ptl]
                                if fr[1]:
                                    v.wait_ge(fr[0].h, fr[1])
                                v.tensor_copy(ptsb[:, 512 * ptl:512 * ptl + wg],
                                              ptp[:, 0:wg]).then_inc(VQ.h, 1)
                                ksub_hi = (1024 * kb + 512 * g5 + wg) // 128 - 1
                                if ksub_hi == qt and not isA:
                                    v.wait_ge(VQ.h, VQ.n + 1)
                                    dcol = 512 * ptl + wg - 128
                                    v.tensor_tensor(ptsb[:, dcol:dcol + 128],
                                                    ptsb[:, dcol:dcol + 128],
                                                    maskT[:, :], ALU.mult).then_inc(VQ.h, 1)
                            nvops = 2 if ((1024 * kb + 512 * g5 + wg) // 128 - 1 == qt
                                          and not isA) else 1
                            VQ.n += nvops
                            mark_free(ptp, VQ)
                            B["groups"].append((g5, wg, ptl, VQ.n))

                    def emit_ctx(B):
                        qt, kb = B["qt"], B["kb"]
                        cxp = cxs[qt % 2]
                        first = kb == 0
                        for (g5, wg, ptl, vq_pt) in B["groups"]:
                            @block.tensor
                            def _(t, first=first):
                                t = _TT(t)
                                t.wait_ge(VQ.h, vq_pt)
                                if first:
                                    fr = ps_free[id(cxp)]
                                    t.wait_ge(fr[0].h, fr[1])
                                for k in range(wg // 128):
                                    ksub = 8 * kb + 4 * g5 + k
                                    lhs = ptsb[:, 512 * ptl + 128 * k:512 * ptl + 128 * (k + 1)]
                                    t.matmul(cxp[:, 0:hd], lhs,
                                             scrB[:, 2 * S + hd * ksub:2 * S + hd * (ksub + 1)],
                                             start=first and k == 0, stop=False)
                                    t.matmul(cxp[:, hd:hd + 1], lhs, maskT[:, 127:128],
                                             start=False, stop=(ksub == qt))
                                t.inc(TQ.h)
                            TQ.n += 1
                            ptsb_free[ptl] = (TQ, TQ.n)
                            first = False
                        if B["diag"]:
                            emit_final(qt, cxp)

                    def emit_final(qt, cxp):
                        tq_ctx = TQ.n

                        @block.vector
                        def _(v):
                            v.wait_ge(TQ.h, tq_ctx)
                            if ctx_region_rel[0]:
                                v.wait_ge(SCX.h, ctx_region_rel[0])
                            v.reciprocal(lrec[:, :], cxp[:, hd:hd + 1]).then_inc(VQ.h, 1)
                            v.wait_ge(VQ.h, VQ.n + 1)
                            col = qt * wct + (j * 128 if isA else 64 * qh)
                            v.tensor_scalar(scrA[:, col:col + hd], cxp[:, 0:hd],
                                            lrec[:, :], None, ALU.mult).then_inc(VQ.h, 1)
                        VQ.n += 2
                        mark_free(cxp, VQ)

                    for i in range(NB + 2):
                        if i >= 2:
                            emit_ctx(blocks[i - 2])
                        if i < NB:
                            emit_score(blocks[i])
                        if i >= 1 and i - 1 < NB:
                            emit_trev(blocks[i - 1])

                last_of_batch = (j == c.SLOTS_A - 1) if isA else True
                if last_of_batch:
                    vq_unit = VQ.n

                    @block.sync
                    def _(sy):
                        sy.wait_ge(VQ.h, vq_unit)
                        outap = bass.AP(ca_i, (c.NSRC * b * T) * wct,
                                        [[wct, 128], [128 * wct, S // 128], [1, wct]])
                        inap = bass.AP(scrA, 0, [[CC * T, 128], [wct, S // 128], [1, wct]])
                        sy.dma_start(out=outap, in_=inap).then_inc(SCX.h, 16)
                    SCX.n += 16
                    ctx_region_rel[0] = SCX.n
                scrB_rel[0] = TQ.n
                cvec_rel[0] = TQ.n
            sc_flush()

        @block.gpsimd
        def _(g):
            g.wait_ge(SCX.h, SCX.n)
            with allow_ici_all_to_all():
                g.collective_compute("AllToAll", ALU.bypass,
                                     replica_groups=[list(range(NCORE))],
                                     ins=[ca_i.ap()], outs=[ca_o.ap()]).then_inc(CCS.h, 1)
        CCS.n += 1
        cc_ctx = CCS.n

        @block.sync
        def _(sy):
            sy.wait_ge(CCS.h, cc_ctx)
            sy.wait_ge(TQ.h, scrB_rel[0])
            sy.wait_ge(VQ.h, scrB_vrel[0])
            for tile in range(NT):
                inap = bass.AP(ca_o, tile * 128 * wct,
                               [[wct, 128], [T * wct, NCORE], [1, wct]])
                outap = bass.AP(scrB, tile * D,
                                [[SCRB, 128], [wct, NCORE], [1, wct]])
                sy.dma_start(out=outap, in_=inap).then_inc(SCX.h, 16)
        SCX.n += 16 * NT
        ctx_ld = SCX.n

        with nc.psum_tensor(_u("ps_ct0"), [128, 1024], BF16) as pc0, \
             nc.psum_tensor(_u("ps_ct1"), [128, 1024], BF16) as pc1:
            pcs = [pc0, pc1]
            rel = [0, 0]
            for tile in range(NT):
                for bb in range(CC // 8):
                    pi = (tile * (CC // 8) + bb) % 2
                    pb = pcs[pi]

                    @block.tensor
                    def _(t):
                        t = _TT(t)
                        t.wait_ge(SCX.h, ctx_ld)
                        t.wait_ge(VQ.h, VQ.n)
                        t.wait_ge(SQ.h, SQ.n)
                        if rel[pi]:
                            t.wait_ge(VQ.h, rel[pi])
                        for k in range(8):
                            cc = 8 * bb + k
                            t.transpose(pb[:, 128 * k:128 * (k + 1)],
                                        scrB[:, tile * D + 128 * cc:tile * D + 128 * (cc + 1)],
                                        ident[:, :])
                        t.inc(TQ.h)
                    TQ.n += 1
                    tq_tr = TQ.n

                    @block.vector
                    def _(v):
                        v.wait_ge(TQ.h, tq_tr)
                        outap = bass.AP(scrA, 8 * bb * T + tile * 128,
                                        [[CC * T, 128], [T, 8], [1, 128]])
                        v.tensor_copy(outap,
                                      bass.AP(pb, 0, [[1024, 128], [128, 8], [1, 128]])
                                      ).then_inc(VQ.h, 1)
                    VQ.n += 1
                    rel[pi] = VQ.n
        ctxfm_v = VQ.n
        scrB_rel[0] = TQ.n

        wo = wo_a if isA else wo_b
        brow = 32 if isA else 64
        with nc.psum_tensor(_u("ps_o0"), [128, 512], F32) as po0, \
             nc.psum_tensor(_u("ps_o1"), [128, 512], F32) as po1, \
             nc.psum_tensor(_u("ps_o2"), [128, 512], F32) as po2, \
             nc.psum_tensor(_u("ps_o3"), [128, 512], F32) as po3:
            pos_ = [po0, po1, po2, po3]
            for p in pos_:
                init_free(p)
            for blk in range(D // 512):
                slot_i = next_slot()
                wtok = wload(slot_i,
                             wo.ap()[:, 512 * blk:512 * (blk + 1)].rearrange("(cc p) f -> p cc f", p=128),
                             CC, 512)
                for tp in range(cdiv(NT, 2)):
                    tiles = [t_ for t_ in (2 * tp, 2 * tp + 1) if t_ < NT]
                    pr = (pos_[0], pos_[1]) if tp % 2 == 0 else (pos_[2], pos_[3])

                    @block.tensor
                    def _(t):
                        t = _TT(t)
                        t.wait_ge(GW[slot_i].h, wtok)
                        t.wait_ge(VQ.h, ctxfm_v)
                        for ti in range(len(tiles)):
                            fr = ps_free[id(pr[ti])]
                            t.wait_ge(fr[0].h, fr[1])
                        wap = wview(slot_i, CC, 512)
                        for cc in range(CC):
                            for ti, tile in enumerate(tiles):
                                t.matmul(pr[ti][:, 0:512],
                                         scrA[:, cc * T + tile * 128:cc * T + (tile + 1) * 128],
                                         wap[:, cc, 0:512], start=(cc == 0), stop=False)
                        for ti in range(len(tiles)):
                            t.matmul(pr[ti][:, 0:512], ones65[brow:brow + 1, :],
                                     bias_rows[brow:brow + 1, 512 * blk:512 * (blk + 1)],
                                     start=False, stop=False)
                            t.matmul(pr[ti][:, 0:512], ones65[brow:brow + 1, :],
                                     bias_rows[brow:brow + 1, D + 512 * blk:D + 512 * (blk + 1)],
                                     start=False, stop=True)
                        t.inc(TQ.h)
                    TQ.n += 1
                    tq_mm = TQ.n

                    for ti, tile in enumerate(tiles):
                        ps = pr[ti]

                        @block.vector
                        def _(v):
                            v.wait_ge(TQ.h, tq_mm)
                            v.tensor_tensor(x_sb[:, tile * D + 512 * blk:tile * D + 512 * (blk + 1)],
                                            x_sb[:, tile * D + 512 * blk:tile * D + 512 * (blk + 1)],
                                            ps[:, 0:512], ALU.add).then_inc(VQ.h, 1)
                        VQ.n += 1
                        mark_free(ps, VQ)
                release_slot(slot_i, TQ)

    def ffn_sublayer():
        h_v = ln_phase(dump_h=True)

        ECAPL = c.E * c.CAP
        pf_trel = [0]
        sit_base = [SIT[0].n, SIT[1].n]
        with nc.psum_tensor(_u("ps_r0"), [128, 512], F32) as pr0, \
             nc.psum_tensor(_u("ps_r1"), [128, 512], F32) as pr1, \
             nc.psum_tensor(_u("ps_pf"), [128, 32], F32) as ppf:
            prs = [pr0, pr1]
            for p in prs + [ppf]:
                init_free(p)
            for tile in range(NT):
                ps = prs[tile % 2]

                @block.tensor
                def _(t):
                    t = _TT(t)
                    t.wait_ge(VQ.h, max(h_v, startup_v))
                    fr = ps_free[id(ps)]
                    t.wait_ge(fr[0].h, fr[1])
                    EE = c.E + 1
                    for cc in range(CC):
                        t.matmul(ps[:, 0:EE],
                                 scrA[:, cc * T + tile * 128:cc * T + (tile + 1) * 128],
                                 routerT[:, cc * EEP:cc * EEP + EE],
                                 start=(cc == 0), stop=(cc == CC - 1))
                    t.inc(TQ.h)
                TQ.n += 1
                tq_log = TQ.n

                @block.vector
                def _(v):
                    v.wait_ge(TQ.h, tq_log)
                    v.reduce_max(nmx[:, :], ps[:, 0:c.E], axis=AX.X).then_inc(VQ.h, 1)
                    v.wait_ge(VQ.h, VQ.n + 1)
                    v.tensor_scalar(nmx[:, :], nmx[:, :], -1.0, None, ALU.mult).then_inc(VQ.h, 1)
                VQ.n += 2
                vq_mx = VQ.n

                def f(s_, vq_mx=vq_mx, ps=ps):
                    s_.wait_ge(VQ.h, vq_mx)
                    s_.activation(out=expv[:, :], in_=ps[:, 0:c.E], func=AF.Exp,
                                  bias=nmx[:, :], scale=1.0).then_inc(SQ.h, 1)
                sc(f)
                SQ.n += 1
                sq_e = SQ.n

                @block.vector
                def _(v):
                    v.wait_ge(SQ.h, sq_e)
                    v.tensor_copy(sglog[:, tile:tile + 1], ps[:, c.E:c.E + 1]).then_inc(VQ.h, 1)
                VQ.n += 1
                mark_free(ps, VQ)

                @block.vector
                def _(v):
                    v.wait_ge(SQ.h, sq_e)
                    v.wait_ge(TQ.h, pf_trel[0])
                    n0 = VQ.n
                    v.tensor_copy(mskv[:, :], expv[:, :]).then_inc(VQ.h, 1)
                    for k in range(4):
                        v.wait_ge(VQ.h, n0 + 1 + 3 * k)
                        v.reduce_max(mml[:, k:k + 1], mskv[:, :], axis=AX.X).then_inc(VQ.h, 1)
                        if k < 3:
                            v.wait_ge(VQ.h, n0 + 2 + 3 * k)
                            v.tensor_scalar(ge01[:, :], mskv[:, :], mml[:, k:k + 1], -1e9,
                                            ALU.is_ge, ALU.mult).then_inc(VQ.h, 1)
                            v.wait_ge(VQ.h, n0 + 3 + 3 * k)
                            v.tensor_tensor(mskv[:, :], mskv[:, :], ge01[:, :],
                                            ALU.add).then_inc(VQ.h, 1)
                    v.wait_ge(VQ.h, n0 + 11)
                    v.tensor_tensor(mml[:, 4:5], mml[:, 0:1], mml[:, 1:2], ALU.add).then_inc(VQ.h, 1)
                    v.wait_ge(VQ.h, n0 + 12)
                    v.tensor_tensor(mml[:, 4:5], mml[:, 4:5], mml[:, 2:3], ALU.add).then_inc(VQ.h, 1)
                    v.wait_ge(VQ.h, n0 + 13)
                    v.tensor_tensor(mml[:, 4:5], mml[:, 4:5], mml[:, 3:4], ALU.add).then_inc(VQ.h, 1)
                    v.wait_ge(VQ.h, n0 + 14)
                    v.reciprocal(mml[:, 5:6], mml[:, 4:5]).then_inc(VQ.h, 1)
                    v.tensor_scalar(ge01[:, :], expv[:, :], mml[:, 3:4], None,
                                    ALU.is_ge).then_inc(VQ.h, 1)
                    v.wait_ge(VQ.h, n0 + 16)
                    v.tensor_tensor(mskv[:, :], expv[:, :], ge01[:, :], ALU.mult).then_inc(VQ.h, 1)
                    v.wait_ge(VQ.h, n0 + 17)
                    v.tensor_scalar(comb[:, tile * c.E:(tile + 1) * c.E], mskv[:, :],
                                    mml[:, 5:6], None, ALU.mult).then_inc(VQ.h, 1)
                VQ.n += 18
                vq_ge = VQ.n - 2

                frp = ps_free[id(ppf)]

                @block.tensor
                def _(t):
                    t = _TT(t)
                    t.wait_ge(VQ.h, vq_ge)
                    t.wait_ge(frp[0].h, frp[1])
                    t.matmul(ppf[:, 0:16], utri[:, :], ge01[:, :],
                             start=True, stop=True)
                    t.matmul(ppf[:, 16:32], utri[:, :], ge01[:, :],
                             start=True, stop=False)
                    t.matmul(ppf[:, 16:32], ones65[0:1, :],
                             ecrow_sb[0:1, 16 * tile:16 * tile + 16],
                             start=False, stop=True)
                    t.inc(TQ.h)
                TQ.n += 1
                pf_trel[0] = TQ.n
                pf_tq = TQ.n

                @block.vector
                def _(v):
                    n1 = VQ.n
                    v.wait_ge(TQ.h, pf_tq)
                    v.tensor_scalar(gidx_f[:, :], ppf[:, 0:16], c.CT - 0.5, 1e7,
                                    ALU.is_ge, ALU.mult).then_inc(VQ.h, 1)
                    v.wait_ge(VQ.h, n1 + 1)
                    v.tensor_tensor(gidx_f[:, :], gidx_f[:, :], ppf[:, 16:32],
                                    ALU.add).then_inc(VQ.h, 1)
                    v.wait_ge(VQ.h, n1 + 2)
                    v.scalar_tensor_tensor(gidx_f[:, :], ge01[:, :], -1e7,
                                           gidx_f[:, :], ALU.mult,
                                           ALU.add).then_inc(VQ.h, 1)
                    v.wait_ge(VQ.h, n1 + 3)
                    v.tensor_scalar(gidx_f[:, :], gidx_f[:, :], 1e7, -1.0,
                                    ALU.add, ALU.mult).then_inc(VQ.h, 1)
                    for r in range(4):
                        v.wait_ge(VQ.h, n1 + 5 * r + 4)
                        v.reduce_max(gmn[:, 0:1], gidx_f[:, :],
                                     axis=AX.X).then_inc(VQ.h, 1)
                        v.wait_ge(VQ.h, n1 + 5 * r + 5)
                        v.tensor_scalar(mskv[:, :], gidx_f[:, :], gmn[:, 0:1],
                                        -3e7, ALU.is_ge,
                                        ALU.mult).then_inc(VQ.h, 1)
                        v.wait_ge(VQ.h, n1 + 5 * r + 6)
                        v.tensor_tensor(gidx_f[:, :], gidx_f[:, :], mskv[:, :],
                                        ALU.add).then_inc(VQ.h, 1)
                        v.wait_ge(VQ.h, n1 + 5 * r + 5)
                        v.tensor_scalar(gmn[:, 1:2], gmn[:, 0:1], -1.0,
                                        float(ECAPL + 4 * tile + r), ALU.mult,
                                        ALU.min).then_inc(VQ.h, 1)
                        v.wait_ge(VQ.h, n1 + 5 * r + 8)
                        v.tensor_copy(gmnu[:, 4 * tile + r:4 * tile + r + 1],
                                      gmn[:, 1:2]).then_inc(VQ.h, 1)
                VQ.n += 24
                mark_free(ppf, VQ)
                vq_idx = VQ.n

                sl = tile % 2
                if True:
                    @block.sync
                    def _(sy):
                        sy.wait_ge(TQ.h, ln_state["stg_trel"][sl])
                        sy.wait_ge(SST[sl].h, 16 * stg_ct[sl])
                        if tile >= 2:
                            sy.wait_ge(SIT[sl].h,
                                       sit_base[sl] + 8 * 16 * (tile // 2))
                        sy.dma_start(out=stg[sl][:, 0:D],
                                     in_=h_dram.ap()[128 * tile:128 * (tile + 1), :]
                                     ).then_inc(SST[sl].h, 16)
                    SST[sl].n += 16
                    stg_ct[sl] += 1
                sst_tok = 16 * stg_ct[sl]

                @block.gpsimd
                def _(g):
                    g.wait_ge(VQ.h, vq_idx)
                    g.wait_ge(SST[sl].h, sst_tok)
                    g.wait_ge(SST[1 - sl].h, 16 * stg_ct[1 - sl])
                    g.wait_ge(SIX.h, hg_rel[0])
                    g.wait_ge(SCS.h, scs_rel[0])
                    for r in range(4):
                        g.indirect_dma_start(
                            hgath.ap(),
                            bass.IndirectOffsetOnAxis(
                                ap=gmnu[:, 4 * tile + r:4 * tile + r + 1], axis=0),
                            stg[sl][:, 0:D], None).then_inc(SIT[sl].h, 16)
                        g.indirect_dma_start(
                            comb_sl.ap(),
                            bass.IndirectOffsetOnAxis(
                                ap=gmnu[:, 4 * tile + r:4 * tile + r + 1], axis=0),
                            comb[:, tile * c.E:(tile + 1) * c.E],
                            None).then_inc(SIT[sl].h, 16)
                SIT[sl].n += 8 * 16
            sc_flush()
        vq_sg = VQ.n

        def f(s_, vq_sg=vq_sg):
            s_.wait_ge(VQ.h, vq_sg)
            s_.activation(out=sg_sb[:, :], in_=sglog[:, :],
                          func=AF.Sigmoid).then_inc(SQ.h, 1)
        sc(f)
        SQ.n += 1

        comb_v = VQ.n
        ffn_entry_s = SQ.n

        with nc.psum_tensor(_u("ps_m0"), [128, 512], F32) as pm0, \
             nc.psum_tensor(_u("ps_m1"), [128, 512], F32) as pm1, \
             nc.psum_tensor(_u("ps_m2"), [128, 512], F32) as pm2, \
             nc.psum_tensor(_u("ps_m3"), [128, 512], F32) as pm3, \
             nc.psum_tensor(_u("ps_d0"), [128, 512], F32) as pd0, \
             nc.psum_tensor(_u("ps_d1"), [128, 512], F32) as pd1, \
             nc.psum_tensor(_u("ps_t0"), [128, 1024], BF16) as pt0, \
             nc.psum_tensor(_u("ps_t1"), [128, 1024], BF16) as pt1:
            pms, pds = [pm0, pm1, pm2, pm3], [pd0, pd1]
            pts = [pt0, pt1]
            for p in pms + pds:
                init_free(p)
            sil_free = {i: (VQ, 0) for i in range(NSIL)}

            def gu_phase(gsrc, usrc, iblks, rebase):
                pending = []
                gct, uct = [0], [0]
                iclct = [0]
                for (ioff, iw) in iblks:
                    nic = iw // 128
                    for which in range(2):
                        slot_i = next_slot()
                        src_ = (gsrc if which == 0 else usrc)(ioff, iw)
                        wtok = wload(slot_i, src_, CC, iw)
                        for icl in range(nic):
                            ic = ioff // 128 + icl
                            pr = (pms[0], pms[1]) if iclct[0] % 2 == 0 else (pms[2], pms[3])
                            iclct[0] += 1

                            @block.tensor
                            def _(t):
                                t = _TT(t)
                                t.wait_ge(GW[slot_i].h, wtok)
                                t.wait_ge(VQ.h, max(h_v, comb_v))
                                t.wait_ge(SQ.h, ffn_entry_s)
                                for th in range(TH):
                                    fr = ps_free[id(pr[th])]
                                    t.wait_ge(fr[0].h, fr[1])
                                wap = wview(slot_i, CC, iw)
                                for cc in range(CC):
                                    for th in range(TH):
                                        t.matmul(pr[th][:, 0:thw[th]],
                                                 wap[:, cc, 128 * icl:128 * (icl + 1)],
                                                 scrA[:, cc * T + 512 * th:cc * T + 512 * th + thw[th]],
                                                 start=(cc == 0), stop=(cc == CC - 1))
                                t.inc(TQ.h)
                            TQ.n += 1
                            tq_g = TQ.n

                            for th in range(TH):
                                ps = pr[th]
                                if which == 0:
                                    sl = gct[0] % NSIL

                                    def f(s_, tqw=max(tq_g, scrB_rel[0]),
                                          fr=sil_free[sl], sl=sl, th=th, ps=ps):
                                        s_.wait_ge(TQ.h, tqw)
                                        if fr[1]:
                                            s_.wait_ge(fr[0].h, fr[1])
                                        s_.activation(out=silsl(sl, thw[th]),
                                                      in_=ps[:, 0:thw[th]],
                                                      func=AF.Silu).then_inc(SQ.h, 1)
                                    sc(f)
                                    SQ.n += 1
                                    mark_free(ps, SQ)
                                    pending.append((ic, th, sl, SQ.n))
                                    gct[0] += 1
                                else:
                                    ic0, th0, sl, sq_sil = pending.pop(0)
                                    col = (ic0 - rebase) * T + 512 * th0

                                    @block.vector
                                    def _(v):
                                        v.wait_ge(TQ.h, max(tq_g, scrB_rel[0]))
                                        v.wait_ge(SQ.h, sq_sil)
                                        v.scalar_tensor_tensor(
                                            scrB[:, col:col + thw[th0]],
                                            ps[:, 0:thw[th0]], 1.0, silsl(sl, thw[th0]),
                                            ALU.mult, ALU.mult).then_inc(VQ.h, 1)
                                    VQ.n += 1
                                    mark_free(ps, VQ)
                                    sil_free[sl] = (VQ, VQ.n)
                                    uct[0] += 1
                        release_slot(slot_i, TQ)
                return VQ.n

            def down_phase(dsrc, ncc, rebase, scal_fn):
                for fc in range(D // 512):
                    slot_i = next_slot()
                    wtok = wload(slot_i, dsrc(fc), ncc, 512)
                    for tp in range(cdiv(NT, 2)):
                        tiles = [t_ for t_ in (2 * tp, 2 * tp + 1) if t_ < NT]
                        pr = (pds[0], pds[1]) if tp % 2 == 0 else (pms[0], pms[1])

                        @block.tensor
                        def _(t):
                            t = _TT(t)
                            t.wait_ge(GW[slot_i].h, wtok)
                            t.wait_ge(VQ.h, act_v[0])
                            for ti in range(len(tiles)):
                                fr = ps_free[id(pr[ti])]
                                t.wait_ge(fr[0].h, fr[1])
                            wap = wview(slot_i, ncc, 512)
                            for icl in range(ncc):
                                for ti, tile in enumerate(tiles):
                                    t.matmul(pr[ti][:, 0:512],
                                             scrB[:, icl * T + tile * 128:icl * T + (tile + 1) * 128],
                                             wap[:, icl, 0:512],
                                             start=(icl == 0), stop=(icl == ncc - 1))
                            t.inc(TQ.h)
                        TQ.n += 1
                        tq_d = TQ.n

                        for ti, tile in enumerate(tiles):
                            ps = pr[ti]
                            scal = scal_fn(tile)

                            @block.vector
                            def _(v):
                                v.wait_ge(TQ.h, tq_d)
                                v.wait_ge(VQ.h, comb_v)
                                v.scalar_tensor_tensor(
                                    x_sb[:, tile * D + 512 * fc:tile * D + 512 * (fc + 1)],
                                    ps[:, 0:512], scal,
                                    x_sb[:, tile * D + 512 * fc:tile * D + 512 * (fc + 1)],
                                    ALU.mult, ALU.add).then_inc(VQ.h, 1)
                            VQ.n += 1
                            mark_free(ps, VQ)
                    release_slot(slot_i, TQ)

            act_v = [0]
            CAPL = c.CAP
            CAPT = cdiv(CAPL, 128)
            CW = min(128, CAPL)
            HGOFF = c.IC * CAPL
            NSIL2 = 5
            SIL2OFF = SCRB - NSIL2 * 512
            assert HGOFF + c.E * CAPL <= SIL2OFF, (HGOFF, c.E * CAPL, SIL2OFF)
            sil2_free = {i: (SYL[i], SYL[i].n) for i in range(NSIL2)}
            sil2_ct = [0]
            hgA_trel = [TQ.n, TQ.n]
            hgfm_rel = [0]
            act_trel = [0]
            cslot_vrel = [0]
            pt_vrel = [0, 0]
            sit_full = [SIT[0].n, SIT[1].n]
            for e in range(c.E):
                hg_tok = {}

                def hg_load(k):
                    sl = k % 2
                    cw = min(128, CAPL - 128 * k)

                    @block.sync
                    def _(sy):
                        if e == 0 and k < 2:
                            sy.wait_ge(SIT[0].h, sit_full[0])
                            sy.wait_ge(SIT[1].h, sit_full[1])
                        sy.wait_ge(TQ.h, hgA_trel[sl])
                        sy.wait_ge(SST[sl].h, 16 * stg_ct[sl])
                        sy.dma_start(out=stg[sl][0:cw, 0:D],
                                     in_=hgath.ap()[e * CAPL + 128 * k:
                                                    e * CAPL + 128 * k + cw, :]
                                     ).then_inc(SST[sl].h, 16)
                    SST[sl].n += 16
                    stg_ct[sl] += 1
                    hg_tok[k] = 16 * stg_ct[sl]

                for k in range(min(2, CAPT)):
                    hg_load(k)
                for k in range(CAPT):
                    sl = k % 2
                    cw = min(128, CAPL - 128 * k)
                    for bb in range(CC // 8):
                        pi = (k * (CC // 8) + bb) % 2
                        pt = pts[pi]

                        @block.tensor
                        def _(t):
                            t = _TT(t)
                            t.wait_ge(SST[sl].h, hg_tok[k])
                            t.wait_ge(VQ.h, pt_vrel[pi])
                            for j in range(8):
                                cc = 8 * bb + j
                                t.transpose(pt[:, 128 * j:128 * j + cw],
                                            stg[sl][0:cw, 128 * cc:128 * (cc + 1)],
                                            ident[0:cw, 0:cw])
                            t.inc(TQ.h)
                        TQ.n += 1
                        tq_tr = TQ.n

                        @block.vector
                        def _(v):
                            v.wait_ge(TQ.h, max(tq_tr, hgfm_rel[0]))
                            outap = bass.AP(scrB, HGOFF + 8 * bb * CAPL + 128 * k,
                                            [[SCRB, 128], [CAPL, 8], [1, cw]])
                            v.tensor_copy(outap,
                                          bass.AP(pt, 0, [[1024, 128], [128, 8], [1, cw]])
                                          ).then_inc(VQ.h, 1)
                        VQ.n += 1
                        pt_vrel[pi] = VQ.n
                    hgA_trel[sl] = TQ.n
                    if k + 2 < CAPT:
                        hg_load(k + 2)
                hgfm_v = VQ.n

                @block.sync
                def _(sy):
                    sy.wait_ge(VQ.h, cslot_vrel[0])
                    sy.dma_start(out=bass.AP(cslot, 0,
                                             [[CAPT * c.E, CW], [c.E, CAPT], [1, c.E]]),
                                 in_=comb_sl.ap()[e * CAPL:(e + 1) * CAPL, :]
                                 .rearrange("(s p) f -> p s f", p=CW)
                                 ).then_inc(SCS.h, 16)
                SCS.n += 16
                cslot_tok = SCS.n

                pending = []
                iclct = [0]
                for (ioff, iw) in c.IBLK:
                    nic = iw // 128
                    for which in range(2):
                        slot_i = next_slot()
                        wsrc = (w_gate if which == 0 else w_up)
                        wtok = wload(slot_i,
                                     wsrc.ap()[e, :, ioff:ioff + iw]
                                     .rearrange("(cc p) i -> p cc i", p=128),
                                     CC, iw)
                        for icl in range(nic):
                            ic = ioff // 128 + icl
                            pr = pms[iclct[0] % 4]
                            iclct[0] += 1

                            @block.tensor
                            def _(t):
                                t = _TT(t)
                                t.wait_ge(GW[slot_i].h, wtok)
                                t.wait_ge(VQ.h, hgfm_v)
                                fr = ps_free[id(pr)]
                                t.wait_ge(fr[0].h, fr[1])
                                wap = wview(slot_i, CC, iw)
                                for cc in range(CC):
                                    t.matmul(pr[:, 0:CAPL],
                                             wap[:, cc, 128 * icl:128 * (icl + 1)],
                                             scrB[:, HGOFF + cc * CAPL:
                                                  HGOFF + (cc + 1) * CAPL],
                                             start=(cc == 0), stop=(cc == CC - 1))
                                t.inc(TQ.h)
                            TQ.n += 1
                            tq_g = TQ.n

                            if which == 0:
                                sl2 = sil2_ct[0] % NSIL2
                                sil2_ct[0] += 1

                                def f(s_, tq_g=tq_g, fr2=sil2_free[sl2],
                                      sl2=sl2, pr=pr):
                                    s_.wait_ge(TQ.h, tq_g)
                                    if fr2[1]:
                                        s_.wait_ge(fr2[0].h, fr2[1])
                                    s_.activation(
                                        out=scrB[:, SIL2OFF + 512 * sl2:
                                                 SIL2OFF + 512 * sl2 + CAPL],
                                        in_=pr[:, 0:CAPL],
                                        func=AF.Silu).then_inc(SQ.h, 1)
                                sc(f)
                                SQ.n += 1
                                mark_free(pr, SQ)
                                pending.append((ic, sl2, SQ.n))
                            else:
                                ic0, sl2, sq_sil = pending.pop(0)

                                @block.vector
                                def _(v):
                                    v.wait_ge(TQ.h, max(tq_g, act_trel[0]))
                                    v.wait_ge(SQ.h, sq_sil)
                                    v.scalar_tensor_tensor(
                                        scrB[:, ic0 * CAPL:(ic0 + 1) * CAPL],
                                        pr[:, 0:CAPL], 1.0,
                                        scrB[:, SIL2OFF + 512 * sl2:
                                             SIL2OFF + 512 * sl2 + CAPL],
                                        ALU.mult, ALU.mult).then_inc(VQ.h, 1)
                                VQ.n += 1
                                mark_free(pr, VQ)
                                sil2_free[sl2] = (VQ, VQ.n)
                        release_slot(slot_i, TQ)
                hgfm_rel[0] = TQ.n
                vq_act = VQ.n

                for fc in range(D // 512):
                    slot_i = next_slot()
                    wtok = wload(slot_i,
                                 w_down.ap()[e, :, 512 * fc:512 * (fc + 1)]
                                 .rearrange("(ic p) f -> p ic f", p=128),
                                 c.IC, 512)
                    for st in range(CAPT):
                        cw = min(128, CAPL - 128 * st)
                        pr = pds[(fc * CAPT + st) % 2]

                        @block.tensor
                        def _(t):
                            t = _TT(t)
                            t.wait_ge(GW[slot_i].h, wtok)
                            t.wait_ge(VQ.h, vq_act)
                            fr = ps_free[id(pr)]
                            t.wait_ge(fr[0].h, fr[1])
                            wap = wview(slot_i, c.IC, 512)
                            for icl in range(c.IC):
                                t.matmul(pr[0:cw, 0:512],
                                         scrB[:, icl * CAPL + 128 * st:
                                              icl * CAPL + 128 * st + cw],
                                         wap[:, icl, 0:512],
                                         start=(icl == 0), stop=(icl == c.IC - 1))
                            t.inc(TQ.h)
                        TQ.n += 1
                        tq_d = TQ.n

                        sl2 = sil2_ct[0] % NSIL2
                        sil2_ct[0] += 1
                        fr2 = sil2_free[sl2]

                        @block.vector
                        def _(v):
                            v.wait_ge(TQ.h, tq_d)
                            v.wait_ge(SCS.h, cslot_tok)
                            if fr2[1]:
                                v.wait_ge(fr2[0].h, fr2[1])
                            v.tensor_scalar(
                                scrB[0:cw, SIL2OFF + 512 * sl2:
                                     SIL2OFF + 512 * sl2 + 512],
                                pr[0:cw, 0:512],
                                cslot[0:cw, st * c.E + e:st * c.E + e + 1], None,
                                ALU.mult).then_inc(VQ.h, 1)
                        VQ.n += 1
                        mark_free(pr, VQ)
                        vq_evac = VQ.n

                        @block.sync
                        def _(sy):
                            sy.wait_ge(VQ.h, vq_evac)
                            sy.wait_ge(SGA.h, y_sga_gate[0])
                            sy.dma_start(
                                out=y_dram.ap()[e * CAPL + 128 * st:
                                                e * CAPL + 128 * st + cw,
                                                512 * fc:512 * (fc + 1)],
                                in_=scrB[0:cw, SIL2OFF + 512 * sl2:
                                         SIL2OFF + 512 * sl2 + 512]
                                ).then_inc(SYL[sl2].h, 16)
                        SYL[sl2].n += 16
                        sil2_free[sl2] = (SYL[sl2], SYL[sl2].n)
                    release_slot(slot_i, TQ)
                act_trel[0] = TQ.n
                cslot_vrel[0] = VQ.n
            y_done = [(SYL[i], SYL[i].n) for i in range(NSIL2)]
            hg_rel[0] = SIX.n
            scs_rel[0] = SCS.n
            scrB_rel[0] = TQ.n
            scrB_vrel[0] = VQ.n
            for i in range(NSIL):
                if i >= 3:
                    sil_free[i] = y_done[i - 3]

            for (soff, sw) in c.SSEG:
                iblk_seg = []
                o = soff
                while o < soff + sw:
                    w_ = min(512, soff + sw - o)
                    iblk_seg.append((o, w_))
                    o += w_
                act_v[0] = gu_phase(
                    lambda ioff, iw: sh_gate.ap()[:, ioff:ioff + iw]
                    .rearrange("(cc p) i -> p cc i", p=128),
                    lambda ioff, iw: sh_up.ap()[:, ioff:ioff + iw]
                    .rearrange("(cc p) i -> p cc i", p=128),
                    iblk_seg, soff // 128)
                down_phase(
                    lambda fc, soff=soff, sw=sw: sh_down.ap()[soff:soff + sw, 512 * fc:512 * (fc + 1)]
                    .rearrange("(ic p) f -> p ic f", p=128),
                    sw // 128, soff // 128,
                    lambda tile: sg_sb[:, tile:tile + 1])
                scrB_rel[0] = TQ.n
                scrB_vrel[0] = VQ.n
            sc_flush()

            vq_shared = VQ.n
            sga_base = SGA.n
            for r in range(4):
                for tile in range(NT):
                    @block.gpsimd
                    def _(g):
                        if tile == 0:
                            if r == 0:
                                g.wait_ge(VQ.h, vq_shared)
                                for sem_, tok_ in y_done:
                                    g.wait_ge(sem_.h, tok_)
                            else:
                                g.wait_ge(SGA.h, sga_base + 16 * NT * r)
                        g.indirect_dma_start(
                            x_sb[:, tile * D:(tile + 1) * D], None,
                            y_dram.ap(),
                            bass.IndirectOffsetOnAxis(
                                ap=gmnu[:, 4 * tile + r:4 * tile + r + 1], axis=0),
                            compute_op=ALU.add, oob_is_err=True).then_inc(SGA.h, 16)
                    SGA.n += 16
            sga_rel[0] = SGA.n
            y_sga_gate[0] = SGA.n
            six_rel[0] = SIX.n
            sit_rel[0] = SIT[0].n
            sit_rel[1] = SIT[1].n

    attn_sublayer(True)
    ffn_sublayer()
    attn_sublayer(False)
    ffn_sublayer()

    sc_flush()

    @block.sync
    def _(sy):
        sy.wait_ge(VQ.h, VQ.n)
        sy.wait_ge(SGA.h, SGA.n)
        sy.wait_ge(SIX.h, SIX.n)
        sy.wait_ge(SIT[0].h, SIT[0].n)
        sy.wait_ge(SIT[1].h, SIT[1].n)
        sy.dma_start(out=out_t.ap().rearrange("(n p) d -> p n d", p=128),
                     in_=bass.AP(x_sb, 0, [[NT * D, 128], [D, NT], [1, D]])
                     ).then_inc(SOUT.h, 16)
        sy.wait_ge(SOUT.h, 16)

    st_.close()
    return nc


def host_tables(c: Cfg, core):
    import ml_dtypes
    bf = ml_dtypes.bfloat16
    t = {}
    t["t_ident"] = np.eye(128, dtype=np.float32).astype(bf)
    t["t_identf"] = np.eye(2 * c.D // 128, dtype=np.float32)
    kk2 = np.arange(128)
    t["t_utri"] = (kk2[:, None] < kk2[None, :]).astype(np.float32).astype(bf)
    NTl = c.T // 128
    ec = np.zeros((1, NTl * 16), np.float64)
    for tl in range(NTl):
        for e in range(16):
            ec[0, 16 * tl + e] = e * c.CAP + tl * c.CT
    t["t_ecrow"] = ec.astype(bf)
    assert np.all(ec.astype(bf).astype(np.float64) == ec), "ecrow must be bf16-exact"
    k_ = np.arange(128)[:, None]
    q_ = np.arange(128)[None, :]
    t["t_maskT"] = (k_ <= q_).astype(np.float32).astype(bf)
    t["t_maskA"] = np.where(q_.T >= k_.T, 0.0, -1e9).astype(np.float32).astype(bf)
    slopes = np.array([1.0 / 2 ** (k / 2) for k in range(c.HA)], np.float64) \
        if c.HA > 8 else np.array([1.0 / 2 ** k for k in range(c.HA)], np.float64)
    my_heads = [c.SLOTS_A * core + jj for jj in range(c.SLOTS_A)]
    jj_ = np.arange(c.S)
    cj = np.stack([slopes[h] * jj_.astype(np.float64) for h in my_heads])
    c0 = cj.astype(bf).astype(np.float64)
    c1 = (cj - c0).astype(bf).astype(np.float64)
    c2 = (cj - c0 - c1).astype(bf)
    t["t_cvec"] = np.stack([c0.astype(bf), c1.astype(bf), c2], axis=1)
    qt = np.arange(c.QT)
    pp = np.arange(128)
    si = np.stack([slopes[h] * (128 * qt[None, :] + pp[:, None]).astype(np.float64)
                   for h in my_heads], axis=0)
    t["t_slopei"] = np.ascontiguousarray(
        si.transpose(1, 0, 2).reshape(128, c.SLOTS_A * c.QT)).astype(np.float32).astype(bf)
    s0 = (core % c.NSRC) * c.T
    pos = np.arange(s0, s0 + c.T, dtype=np.float64)
    invf = 1.0 / (10000.0 ** (np.arange(0, c.DHB, 2, dtype=np.float64) / c.DHB))
    p = np.arange(128)
    ang = pos[None, :] * invf[p % (c.DHB // 2)][:, None]
    sgn = np.where((p % c.DHB) < (c.DHB // 2), -1.0, 1.0)
    t["t_ctab"] = np.cos(ang).astype(np.float32).astype(bf)
    t["t_stab"] = (np.sin(ang) * sgn[:, None]).astype(np.float32).astype(bf)
    P = np.zeros((128, 128), np.float32)
    r = np.arange(128)
    src = (r // c.DHB) * c.DHB + ((r + c.DHB // 2) % c.DHB)
    P[src, r] = 1.0
    t["t_pswap"] = P.astype(bf)
    return t


_CFG = Cfg()


def run(c: Cfg, x, weights):
    nc = build(c)
    in_maps = []
    for core in range(NCORE):
        b, s0 = core // c.NSRC, (core % c.NSRC) * c.T
        m = {"x_shard": np.ascontiguousarray(x[b, s0:s0 + c.T, :])}
        m.update(weights)
        m.update(host_tables(c, core))
        in_maps.append(m)
    res = run_bass_kernel_spmd(nc, in_maps, core_ids=list(range(NCORE)))
    out = np.empty((c.B, c.S, c.D), np.float32)
    for core in range(NCORE):
        b, s0 = core // c.NSRC, (core % c.NSRC) * c.T
        out[b, s0:s0 + c.T, :] = res.results[core]["out_shard"]
    return out


def kernel(L0_attn_a_x, L0_attn_a_Wqkv, L0_attn_a_bqkv, L0_attn_a_Wo, L0_attn_a_bo,
           L0_attn_b_Wq, L0_attn_b_Wk, L0_attn_b_Wv, L0_attn_b_Wo, L0_attn_b_b_out,
           L0_ffn_router_w, L0_ffn_w_gate, L0_ffn_w_up, L0_ffn_w_down,
           L0_ffn_sh_gate, L0_ffn_sh_up, L0_ffn_sh_down, L0_ffn_sh_gate_w):
    c = _CFG
    x = np.ascontiguousarray(np.asarray(L0_attn_a_x, np.float32))
    weights = {
        "Wqkv": L0_attn_a_Wqkv, "bqkv": L0_attn_a_bqkv, "Wo_a": L0_attn_a_Wo,
        "bo_a": L0_attn_a_bo, "Wq_b": L0_attn_b_Wq, "Wk_b": L0_attn_b_Wk,
        "Wv_b": L0_attn_b_Wv, "Wo_b": L0_attn_b_Wo, "bo_b": L0_attn_b_b_out,
        "router_w": L0_ffn_router_w, "w_gate": L0_ffn_w_gate, "w_up": L0_ffn_w_up,
        "w_down": L0_ffn_w_down, "sh_gate": L0_ffn_sh_gate, "sh_up": L0_ffn_sh_up,
        "sh_down": L0_ffn_sh_down, "sh_gate_w": L0_ffn_sh_gate_w,
    }
    weights = {k: np.ascontiguousarray(np.asarray(v, np.float32)) for k, v in weights.items()}
    return run(c, x, weights)


# version 13 (final)
# speedup vs baseline: 1.2142x; 1.0146x over previous
import numpy as np
from contextlib import ExitStack

import concourse.bass as bass
import concourse.mybir as mybir
from concourse.bass_utils import run_bass_kernel_spmd
from concourse.aot_env import allow_ici_all_to_all

F32 = mybir.dt.float32
BF16 = mybir.dt.bfloat16
AF = mybir.ActivationFunctionType
ALU = mybir.AluOpType
AX = mybir.AxisListType

NCORE = 8


class Cfg:
    def __init__(self, B=2, S=4096, D=2048, I_MOE=1408, I_SH=5632, E=16):
        self.B, self.S, self.D = B, S, D
        self.I_MOE, self.I_SH, self.E = I_MOE, I_SH, E
        self.N = B * S
        self.T = self.N // NCORE
        self.NT = self.T // 128
        self.CC = D // 128
        self.HA = D // 128
        self.SLOTS_A = B * self.HA // NCORE // B
        self.DHB = 64
        self.QH = D // self.DHB
        self.KVH = NCORE
        self.G = self.QH // self.KVH
        self.DKV = self.KVH * self.DHB
        self.QT = S // 128
        self.CT = 64
        self.CAP = self.T // 128 * self.CT
        self.IC = I_MOE // 128
        self.ICS = I_SH // 128
        self.NSRC = S // self.T
        self.IBLK = []
        o = 0
        while o < I_MOE:
            w = min(512, I_MOE - o)
            self.IBLK.append((o, w))
            o += w
        self.SSEG = []
        o = 0
        while o < I_SH:
            w = min(1536, I_SH - o)
            self.SSEG.append((o, w))
            o += w


def cdiv(a, b):
    return (a + b - 1) // b



class _TT:
    def __init__(self, eng):
        self.e = eng
        self.last = None

    def matmul(self, *a, **k):
        self.last = self.e.matmul(*a, **k)
        return self.last

    def transpose(self, *a, **k):
        self.last = self.e.transpose(*a, **k)
        return self.last

    def wait_ge(self, *a):
        return self.e.wait_ge(*a)

    def inc(self, sem):
        self.last.then_inc(sem, 1)


_UID = [0]


def _u(s):
    _UID[0] += 1
    return f"{s}_{_UID[0]}"


class VstgCt:
    def __init__(self):
        self.n = 0

    def __getitem__(self, i):
        return self.n

    def __setitem__(self, i, v):
        self.n = v


class Sem:
    def __init__(self, E_, nc, name):
        self.h = E_(nc.semaphore(name))
        self.n = 0


def build(c: Cfg):
    nc = bass.Bass("TRN2", target_bir_lowering=False)
    T, NT, CC, D, S = c.T, c.NT, c.CC, c.D, c.S
    TH = cdiv(T, 512)
    thw = [min(512, T - 512 * i) for i in range(TH)]
    SCALE_A = float(128.0 ** -0.25)
    SCALE_B = float(c.DHB ** -0.25)

    x_in = nc.dram_tensor("x_shard", [T, D], F32, kind="ExternalInput")
    out_t = nc.dram_tensor("out_shard", [T, D], F32, kind="ExternalOutput")
    wqkv = nc.dram_tensor("Wqkv", [D, 3 * D], F32, kind="ExternalInput")
    bqkv = nc.dram_tensor("bqkv", [3 * D], F32, kind="ExternalInput")
    wo_a = nc.dram_tensor("Wo_a", [D, D], F32, kind="ExternalInput")
    bo_a = nc.dram_tensor("bo_a", [D], F32, kind="ExternalInput")
    wq_b = nc.dram_tensor("Wq_b", [D, D], F32, kind="ExternalInput")
    wk_b = nc.dram_tensor("Wk_b", [D, c.DKV], F32, kind="ExternalInput")
    wv_b = nc.dram_tensor("Wv_b", [D, c.DKV], F32, kind="ExternalInput")
    wo_b = nc.dram_tensor("Wo_b", [D, D], F32, kind="ExternalInput")
    bo_b = nc.dram_tensor("bo_b", [D], F32, kind="ExternalInput")
    router_w = nc.dram_tensor("router_w", [c.E, D], F32, kind="ExternalInput")
    w_gate = nc.dram_tensor("w_gate", [c.E, D, c.I_MOE], F32, kind="ExternalInput")
    w_up = nc.dram_tensor("w_up", [c.E, D, c.I_MOE], F32, kind="ExternalInput")
    w_down = nc.dram_tensor("w_down", [c.E, c.I_MOE, D], F32, kind="ExternalInput")
    sh_gate = nc.dram_tensor("sh_gate", [D, c.I_SH], F32, kind="ExternalInput")
    sh_up = nc.dram_tensor("sh_up", [D, c.I_SH], F32, kind="ExternalInput")
    sh_down = nc.dram_tensor("sh_down", [c.I_SH, D], F32, kind="ExternalInput")
    sh_gate_w = nc.dram_tensor("sh_gate_w", [D, 1], F32, kind="ExternalInput")
    t_ident = nc.dram_tensor("t_ident", [128, 128], BF16, kind="ExternalInput")
    t_maskT = nc.dram_tensor("t_maskT", [128, 128], BF16, kind="ExternalInput")
    t_maskA = nc.dram_tensor("t_maskA", [128, 128], BF16, kind="ExternalInput")
    t_cvec = nc.dram_tensor("t_cvec", [c.SLOTS_A, 3, S], BF16, kind="ExternalInput")
    t_slopei = nc.dram_tensor("t_slopei", [128, c.SLOTS_A * c.QT], BF16, kind="ExternalInput")
    t_ctab = nc.dram_tensor("t_ctab", [128, T], BF16, kind="ExternalInput")
    t_stab = nc.dram_tensor("t_stab", [128, T], BF16, kind="ExternalInput")
    t_pswap = nc.dram_tensor("t_pswap", [128, 128], BF16, kind="ExternalInput")
    t_identf = nc.dram_tensor("t_identf", [2 * D // 128, 2 * D // 128], F32, kind="ExternalInput")
    t_utri = nc.dram_tensor("t_utri", [128, 128], BF16, kind="ExternalInput")
    t_ecrow = nc.dram_tensor("t_ecrow", [1, NT * 16], BF16, kind="ExternalInput")

    WCTA = c.SLOTS_A * 128
    WCTB = c.G * 64
    qaA_i = nc.dram_tensor("qaA_i", [c.SLOTS_A * 128 * NCORE, T], BF16)
    qaA_o = nc.dram_tensor("qaA_o", [c.SLOTS_A * 128 * NCORE, T], BF16)
    kaA_i = nc.dram_tensor("kaA_i", [c.SLOTS_A * 128 * NCORE, T], BF16)
    kaA_o = nc.dram_tensor("kaA_o", [c.SLOTS_A * 128 * NCORE, T], BF16)
    vaA_i = nc.dram_tensor("vaA_i", [NCORE * T, WCTA], BF16)
    vaA_o = nc.dram_tensor("vaA_o", [NCORE * T, WCTA], BF16)
    caA_i = nc.dram_tensor("caA_i", [NCORE * T, WCTA], BF16)
    caA_o = nc.dram_tensor("caA_o", [NCORE * T, WCTA], BF16)
    qaB_i = nc.dram_tensor("qaB_i", [WCTB * NCORE, T], BF16)
    qaB_o = nc.dram_tensor("qaB_o", [WCTB * NCORE, T], BF16)
    kaB_i = nc.dram_tensor("kaB_i", [64 * NCORE, T], BF16)
    kaB_o = nc.dram_tensor("kaB_o", [64 * NCORE, T], BF16)
    vaB_i = nc.dram_tensor("vaB_i", [NCORE * T, 64], BF16)
    vaB_o = nc.dram_tensor("vaB_o", [NCORE * T, 64], BF16)
    caB_i = nc.dram_tensor("caB_i", [NCORE * T, WCTB], BF16)
    caB_o = nc.dram_tensor("caB_o", [NCORE * T, WCTB], BF16)
    ECAP = c.E * c.CAP
    NTRASH = 4 * NT
    h_dram = nc.dram_tensor("h_dram", [T, D], BF16)
    hgath = nc.dram_tensor("hgath", [ECAP + NTRASH, D], BF16)
    comb_sl = nc.dram_tensor("comb_sl", [ECAP + NTRASH, c.E], F32)
    y_dram = nc.dram_tensor("y_dram", [ECAP + NTRASH, D], BF16)

    st_ = ExitStack()
    E_ = st_.enter_context

    max_act = max(c.IC * T, max(w // 128 for _, w in c.SSEG) * T)
    SCRB = max(CC * T, 3 * S, max_act + 8 * 512, 2 * S + (S // 128) * 64 + 64 + S)
    SILOFF = SCRB - 8 * 512
    KOFF = 2 * S + (S // 128) * 64
    x_sb = E_(nc.sbuf_tensor("x_sb", [128, NT * D], F32))
    scrA = E_(nc.sbuf_tensor("scrA", [128, CC * T], BF16))
    scrB = E_(nc.sbuf_tensor("scrB", [128, SCRB], BF16))
    w0 = E_(nc.sbuf_tensor("w0", [128, 8192], BF16))
    w1 = E_(nc.sbuf_tensor("w1", [128, 8192], BF16))
    wslots = [w0, w1]
    stg = [E_(nc.sbuf_tensor(f"stg{i}", [128, max(D, T)], BF16)) for i in range(2)]
    vstg = [E_(nc.sbuf_tensor("vstg0", [128, 512], BF16))] * 2
    psb = E_(nc.sbuf_tensor("psb", [128, 2048], BF16))
    ptsb = E_(nc.sbuf_tensor("ptsb", [128, 1024], BF16))
    NSIL = 8
    def silsl(i, wdt):
        return scrB[:, SILOFF + 512 * i:SILOFF + 512 * i + wdt]
    rowscr = E_(nc.sbuf_tensor("rowscr", [1, S], BF16))
    knm = E_(nc.sbuf_tensor("knm", [1, 1], F32))
    rcol = E_(nc.sbuf_tensor("rcol", [128, c.QT], F32))
    negR = E_(nc.sbuf_tensor("negR", [128, c.QT], F32))
    lrec = E_(nc.sbuf_tensor("lrec", [128, 1], F32))
    ln_stats = E_(nc.sbuf_tensor("ln_stats", [128, 6 * (D // 512)], F32))
    ln_mv = E_(nc.sbuf_tensor("ln_mv", [128, 2], F32))
    ln_negm = E_(nc.sbuf_tensor("ln_negm", [128, 1], F32))
    ln_std = E_(nc.sbuf_tensor("ln_std", [128, 1], F32))
    ln_rstd = E_(nc.sbuf_tensor("ln_rstd", [128, 1], F32))
    expv = E_(nc.sbuf_tensor("expv", [128, c.E], F32))
    nmx = E_(nc.sbuf_tensor("nmx", [128, 1], F32))
    mml = E_(nc.sbuf_tensor("mml", [128, 6], F32))
    ge01 = E_(nc.sbuf_tensor("ge01", [128, c.E], BF16))
    mskv = E_(nc.sbuf_tensor("mskv", [128, c.E], F32))
    gidx_f = E_(nc.sbuf_tensor("gidx_f", [128, c.E], F32))
    ecrow_sb = E_(nc.sbuf_tensor("ecrow_sb", [1, NT * 16], BF16))
    utri = E_(nc.sbuf_tensor("utri", [128, 128], BF16))
    cslot = E_(nc.sbuf_tensor("cslot", [128, ((c.CAP + 127) // 128) * c.E], F32))
    gmn = E_(nc.sbuf_tensor("gmn", [128, 2], F32))
    gmnu = E_(nc.sbuf_tensor("gmnu", [128, NT * 4], mybir.dt.uint32))
    comb = E_(nc.sbuf_tensor("comb", [128, NT * c.E], F32))
    sg_sb = E_(nc.sbuf_tensor("sg_sb", [128, NT], F32))
    sglog = E_(nc.sbuf_tensor("sglog", [128, NT], F32))
    EEP = (c.E + 2) // 2 * 2
    routerT = E_(nc.sbuf_tensor("routerT", [128, EEP * CC], BF16))
    bias_rows = E_(nc.sbuf_tensor("bias_rows", [65, 2 * D], BF16))
    bqkv_fm = E_(nc.sbuf_tensor("bqkv_fm", [128, 2 * CC], F32))
    tmpb = E_(nc.sbuf_tensor("tmpb", [65, 128], F32))
    bqkv_raw = tmpb
    identf = E_(nc.sbuf_tensor("identf", [2 * CC, 2 * CC], F32))
    ctab = E_(nc.sbuf_tensor("ctab", [128, T], BF16))
    stab = E_(nc.sbuf_tensor("stab", [128, T], BF16))
    ident = E_(nc.sbuf_tensor("ident", [128, 128], BF16))
    maskT = E_(nc.sbuf_tensor("maskT", [128, 128], BF16))
    maskA = E_(nc.sbuf_tensor("maskA", [128, 128], BF16))
    slopei = E_(nc.sbuf_tensor("slopei", [128, c.SLOTS_A * c.QT], BF16))
    pswap = E_(nc.sbuf_tensor("pswap", [128, 128], BF16))
    ones1 = E_(nc.sbuf_tensor("ones1", [1, 32], BF16))
    ones65 = E_(nc.sbuf_tensor("ones65", [65, 128], BF16))
    ones3 = E_(nc.sbuf_tensor("ones3", [3, 128], BF16))

    TQ = Sem(E_, nc, "TQ")
    VQ = Sem(E_, nc, "VQ")
    SQ = Sem(E_, nc, "SQ")
    PQ = Sem(E_, nc, "PQ")
    CCS = Sem(E_, nc, "CCS")
    SX = Sem(E_, nc, "SX")
    SCT = Sem(E_, nc, "SCT")
    GW = [Sem(E_, nc, f"GW{i}") for i in range(len(wslots))]
    SST = [Sem(E_, nc, f"SST{i}") for i in range(2)]
    _svt = Sem(E_, nc, "SVT0")
    SVT = [_svt, _svt]
    SQB = Sem(E_, nc, "SQB")
    SKB = Sem(E_, nc, "SKB")
    SVB = Sem(E_, nc, "SVB")
    SCX = Sem(E_, nc, "SCX")
    SCV = Sem(E_, nc, "SCV")
    SOUT = Sem(E_, nc, "SOUT")
    GB = Sem(E_, nc, "GB")
    GB2 = Sem(E_, nc, "GB2")
    SIX = Sem(E_, nc, "SIX")
    SIT = [Sem(E_, nc, "SIT0"), Sem(E_, nc, "SIT1")]
    SGA = Sem(E_, nc, "SGA")
    SCS = Sem(E_, nc, "SCS")
    SYL = [Sem(E_, nc, f"SYL{i}") for i in range(5)]

    CVOFF = 3 * S
    assert CVOFF + S <= SCRB

    block = E_(nc.Block())

    ps_free = {}
    wfree = [None, None]
    slot_rr = [0]
    stg_ct = [0, 0]
    vstg_ct = VstgCt()
    psb_free = [(TQ, 0), (TQ, 0)]
    ptsb_free = [(TQ, 0), (TQ, 0)]
    scrB_rel = [0]
    scrB_vrel = [0]
    cvec_rel = [0]
    six_rel = [0]
    sit_rel = [0, 0]
    sga_rel = [0]
    hg_rel = [0]
    scs_rel = [0]
    zfill_gate = [0]
    y_sga_gate = [0]
    ctx_region_rel = [0]

    SCBUF = []

    def sc(f):
        SCBUF.append(f)

    def sc_flush():
        if not SCBUF:
            return
        ops = SCBUF[:]
        SCBUF.clear()

        @block.scalar
        def _(s_):
            for f in ops:
                f(s_)

    def init_free(ps):
        ps_free[id(ps)] = (TQ, 0)

    def mark_free(ps, sem):
        ps_free[id(ps)] = (sem, sem.n)

    def next_slot():
        s = slot_rr[0]
        slot_rr[0] = (s + 1) % len(wslots)
        return s

    def wview(slot_i, a, b):
        return bass.AP(wslots[slot_i], 0, [[8192, 128], [b, a], [1, b]])

    def wload(slot_i, in_ap, a, b):
        sem = GW[slot_i]
        fr = wfree[slot_i]

        @block.gpsimd
        def _(g):
            if fr is not None:
                g.wait_ge(fr[0].h, fr[1])
            g.dma_start(out=wview(slot_i, a, b), in_=in_ap).then_inc(sem.h, 16)

        sem.n += 16
        return sem.n

    def release_slot(slot_i, sem):
        wfree[slot_i] = (sem, sem.n)

    @block.sync
    def _(sy):
        sy.dma_start(out=bass.AP(x_sb, 0, [[NT * D, 128], [D, NT], [1, D]]),
                     in_=x_in.ap().rearrange("(n p) d -> p n d", p=128)).then_inc(SX.h, 16)
        for dst, src in [
            (ident[:, :], t_ident.ap()),
            (maskT[:, :], t_maskT.ap()),
            (maskA[:, :], t_maskA.ap()),
            (slopei[:, :], t_slopei.ap()),
            (ctab[:, :], t_ctab.ap()),
            (stab[:, :], t_stab.ap()),
            (pswap[:, :], t_pswap.ap()),
            (identf[:, :], t_identf.ap()),
            (bqkv_raw[0:2 * CC, :], bqkv.ap().rearrange("(r p) -> r p", p=128)[0:2 * CC, :]),
            (utri[:, :], t_utri.ap()),
            (ecrow_sb[:, :], t_ecrow.ap()),
        ]:
            sy.dma_start(out=dst, in_=src).then_inc(SCT.h, 16)
    SX.n += 16
    SCT.n += 16 * 11

    bsrc = {0: bqkv.ap().rearrange("(t d) -> t d", t=3)[2:3, :],
            32: bo_a.ap().rearrange("(t d) -> t d", t=1),
            64: bo_b.ap().rearrange("(t d) -> t d", t=1)}
    lo_rel = [0]
    bq_tq = [0]

    @block.gpsimd
    def _(g):
        for r, ap_ in bsrc.items():
            g.dma_start(out=bias_rows[r:r + 1, 0:D], in_=ap_).then_inc(GB.h, 16)
        g.dma_start(out=stg[0][0:c.E, 0:D], in_=router_w.ap()).then_inc(GB.h, 16)
        g.dma_start(out=stg[0][c.E:c.E + 1, 0:D],
                    in_=sh_gate_w.ap().rearrange("d o -> o d")).then_inc(GB.h, 16)
        g.wait_ge(SCT.h, SCT.n)
    GB.n += 16 * 5

    with nc.psum_tensor(_u("ps_bq"), [128, 2 * CC], F32) as ps_bq:
        @block.tensor
        def _(t):
            t = _TT(t)
            t.wait_ge(SCT.h, SCT.n)
            t.matmul(ps_bq[:, :], bqkv_raw[0:2 * CC, :], identf[:, :], start=True, stop=True)
            t.inc(TQ.h)
        TQ.n += 1
        tq_bq = TQ.n
        bq_tq[0] = TQ.n

        @block.vector
        def _(v):
            v.wait_ge(TQ.h, tq_bq)
            v.tensor_copy(bqkv_fm[:, :], ps_bq[:, :]).then_inc(VQ.h, 1)
        VQ.n += 1
    bqkv_fm_v = VQ.n

    for ch in range(D // 128):
        @block.gpsimd
        def _(g):
            if lo_rel[0]:
                g.wait_ge(VQ.h, lo_rel[0])
            g.wait_ge(TQ.h, bq_tq[0])
            for r, ap_ in bsrc.items():
                g.dma_start(out=tmpb[r:r + 1, 0:128],
                            in_=ap_[:, 128 * ch:128 * (ch + 1)]).then_inc(GB2.h, 16)
        GB2.n += 16 * 3
        gb_now = GB2.n

        @block.vector
        def _(v):
            v.wait_ge(GB2.h, gb_now)
            v.wait_ge(GB.h, 16 * 5)
            for r in (0, 32, 64):
                v.wait_ge(VQ.h, VQ.n)
                v.tensor_tensor(tmpb[r:r + 1, 0:128], tmpb[r:r + 1, 0:128],
                                bias_rows[r:r + 1, 128 * ch:128 * (ch + 1)],
                                ALU.subtract).then_inc(VQ.h, 1)
                VQ.n += 1
                v.wait_ge(VQ.h, VQ.n)
                v.tensor_copy(bias_rows[r:r + 1, D + 128 * ch:D + 128 * ch + 128],
                              tmpb[r:r + 1, 0:128]).then_inc(VQ.h, 1)
                VQ.n += 1
        lo_rel[0] = VQ.n

    @block.vector
    def _(v):
        v.memset(ones1[:, :], 1.0)
        v.memset(ones65[:, :], 1.0)
        v.memset(psb[:, :], 0.0)
        v.memset(expv[:, :], 0.0)
        v.memset(ones3[:, :], 1.0).then_inc(VQ.h, 1)
    VQ.n += 1
    memset_v = VQ.n

    NROW = c.E * c.CAP + 4 * NT
    @block.sync
    def _(sy):
        sy.wait_ge(VQ.h, memset_v)
        for k0 in range(0, NROW, 128):
            w_ = min(128, NROW - k0)
            sy.dma_start(out=hgath.ap()[k0:k0 + w_, :],
                         in_=psb[0:w_, 0:D]).then_inc(SIX.h, 16)
            SIX.n += 16
        sy.dma_start(out=y_dram.ap()[c.E * c.CAP:NROW, :],
                     in_=psb[0:4 * NT, 0:D]).then_inc(SIX.h, 16)
        SIX.n += 16
        for k0 in range(0, NROW, 128):
            w_ = min(128, NROW - k0)
            sy.dma_start(out=comb_sl.ap()[k0:k0 + w_, :],
                         in_=expv[0:w_, :]).then_inc(SIX.h, 16)
            SIX.n += 16
    six_rel[0] = 0
    hg_rel[0] = SIX.n
    zfill_gate[0] = SIX.n
    psb_free[0] = psb_free[1] = (SIX, SIX.n)
    bias_split_v = max(bqkv_fm_v, lo_rel[0])

    EE = c.E + 1
    with nc.psum_tensor(_u("ps_rt"), [128, EEP * CC], BF16) as ps_rt:
        @block.tensor
        def _(t):
            t = _TT(t)
            t.wait_ge(GB.h, 16 * 5)
            t.wait_ge(SCT.h, SCT.n)
            t.wait_ge(VQ.h, VQ.n)
            for cc in range(CC):
                t.transpose(ps_rt[:, cc * EEP:cc * EEP + EE],
                            stg[0][0:EE, 128 * cc:128 * (cc + 1)],
                            ident[0:EE, 0:EE])
            t.inc(TQ.h)
        TQ.n += 1
        tq_rt = TQ.n

        @block.vector
        def _(v):
            v.wait_ge(TQ.h, tq_rt)
            v.tensor_copy(bass.AP(routerT, 0, [[EEP * CC, 128], [EEP, CC], [1, EE]]),
                          bass.AP(ps_rt, 0, [[EEP * CC, 128], [EEP, CC], [1, EE]])).then_inc(VQ.h, 1)
        VQ.n += 1
    startup_v = VQ.n

    ln_state = {"stg_trel": [0, 0], "pl_vrel": [0, 0]}

    def ln_phase(dump_h=False):
        nch = D // 512
        tq0 = TQ.n

        with nc.psum_tensor(_u("ps_ln0"), [128, 1024], BF16) as pl0, \
             nc.psum_tensor(_u("ps_ln1"), [128, 1024], BF16) as pl1:
            pls = [pl0, pl1]
            for tile in range(NT):
                sl = tile % 2

                @block.vector
                def _(v):
                    if tile == 0:
                        v.wait_ge(SX.h, 16)
                        v.wait_ge(TQ.h, tq0)
                        v.wait_ge(VQ.h, VQ.n)
                        v.wait_ge(SIX.h, six_rel[0])
                        v.wait_ge(SIT[0].h, sit_rel[0])
                        v.wait_ge(SIT[1].h, sit_rel[1])
                        v.wait_ge(SGA.h, sga_rel[0])
                    v.wait_ge(TQ.h, ln_state["stg_trel"][sl])
                    for ch in range(nch):
                        v.bn_stats(ln_stats[:, 6 * ch:6 * ch + 6],
                                   x_sb[:, tile * D + 512 * ch:tile * D + 512 * (ch + 1)]
                                   ).then_inc(VQ.h, 1)
                    v.wait_ge(VQ.h, VQ.n + nch)
                    v.bn_aggr(ln_mv[:, :], ln_stats[:, :]).then_inc(VQ.h, 1)
                    v.wait_ge(VQ.h, VQ.n + nch + 1)
                    v.tensor_scalar(ln_negm[:, :], ln_mv[:, 0:1], -1.0, None,
                                    ALU.mult).then_inc(VQ.h, 1)
                    v.tensor_scalar(ln_std[:, :], ln_mv[:, 1:2], 1e-5, None,
                                    ALU.add).then_inc(VQ.h, 1)
                VQ.n += nch + 3
                vq_stats = VQ.n

                def f(s_, vq_stats=vq_stats):
                    s_.wait_ge(VQ.h, vq_stats)
                    s_.activation(out=ln_rstd[:, :], in_=ln_std[:, :],
                                  func=AF.Sqrt).then_inc(SQ.h, 1)
                sc(f)
                SQ.n += 1
                sq_std = SQ.n

                @block.vector
                def _(v):
                    v.wait_ge(SQ.h, sq_std)
                    v.wait_ge(SST[sl].h, 16 * stg_ct[sl])
                    v.reciprocal(ln_std[:, :], ln_rstd[:, :]).then_inc(VQ.h, 1)
                    v.wait_ge(VQ.h, VQ.n + 1)
                    v.tensor_scalar(stg[sl][:, 0:D], x_sb[:, tile * D:(tile + 1) * D],
                                    ln_negm[:, :], ln_std[:, :], ALU.add,
                                    ALU.mult).then_inc(VQ.h, 1)
                VQ.n += 2
                vq_apply = VQ.n

                if dump_h:
                    @block.sync
                    def _(sy):
                        sy.wait_ge(VQ.h, vq_apply)
                        sy.dma_start(out=h_dram.ap()[128 * tile:128 * (tile + 1), :],
                                     in_=stg[sl][:, 0:D]).then_inc(SST[sl].h, 16)
                    SST[sl].n += 16
                    stg_ct[sl] += 1

                nb = CC // 8
                for bb in range(nb):
                    pi = (tile * nb + bb) % 2
                    pb = pls[pi]

                    @block.tensor
                    def _(t):
                        t = _TT(t)
                        t.wait_ge(VQ.h, max(vq_apply, ln_state["pl_vrel"][pi]))
                        for k in range(8):
                            cc = 8 * bb + k
                            t.transpose(pb[:, 128 * k:128 * (k + 1)],
                                        stg[sl][:, 128 * cc:128 * (cc + 1)], ident[:, :])
                        t.inc(TQ.h)
                    TQ.n += 1
                    tq_tr = TQ.n

                    @block.vector
                    def _(v):
                        v.wait_ge(TQ.h, tq_tr)
                        outap = bass.AP(scrA, 8 * bb * T + tile * 128,
                                        [[CC * T, 128], [T, 8], [1, 128]])
                        v.tensor_copy(outap,
                                      bass.AP(pb, 0, [[1024, 128], [128, 8], [1, 128]])
                                      ).then_inc(VQ.h, 1)
                    VQ.n += 1
                    ln_state["pl_vrel"][pi] = VQ.n
                ln_state["stg_trel"][sl] = TQ.n
        return VQ.n

    def attn_sublayer(isA):
        h_v = ln_phase()

        if isA:
            qa_i, ka_i, va_i = qaA_i, kaA_i, vaA_i
            qa_o, ka_o, va_o = qaA_o, kaA_o, vaA_o
            ca_i, ca_o = caA_i, caA_o
            wct = WCTA
        else:
            qa_i, ka_i, va_i = qaB_i, kaB_i, vaB_i
            qa_o, ka_o, va_o = qaB_o, kaB_o, vaB_o
            ca_i, ca_o = caB_i, caB_o
            wct = WCTB

        with nc.psum_tensor(_u("ps_p0"), [128, 512], F32) as pp0, \
             nc.psum_tensor(_u("ps_p1"), [128, 512], F32) as pp1, \
             nc.psum_tensor(_u("ps_p2"), [128, 512], F32) as pp2, \
             nc.psum_tensor(_u("ps_p3"), [128, 512], F32) as pp3, \
             nc.psum_tensor(_u("ps_w0"), [128, 512], F32) as pw0, \
             nc.psum_tensor(_u("ps_w1"), [128, 512], F32) as pw1:
            pps = [pp0, pp1, pp2, pp3]
            pws = [pw0, pw1]
            for p in pps + pws:
                init_free(p)

            def qk_proj(weight, col_off, ncols, out_dram, bias_off, scale, rope):
                for blk in range(ncols // 512):
                    slot_i = next_slot()
                    wtok = wload(slot_i,
                                 weight.ap()[:, col_off + 512 * blk:col_off + 512 * (blk + 1)]
                                 .rearrange("(cc p) f -> p cc f", p=128),
                                 CC, 512)
                    for fo in range(4):
                        fglob = blk * 4 + fo
                        sl = fglob % 2
                        pr = (pps[0], pps[1]) if fo % 2 == 0 else (pps[2], pps[3])

                        @block.tensor
                        def _(t):
                            t = _TT(t)
                            t.wait_ge(GW[slot_i].h, wtok)
                            t.wait_ge(VQ.h, max(h_v, bias_split_v))
                            for th in range(TH):
                                fr = ps_free[id(pr[th])]
                                t.wait_ge(fr[0].h, fr[1])
                            wap = wview(slot_i, CC, 512)
                            for cc in range(CC):
                                for th in range(TH):
                                    t.matmul(pr[th][:, 0:thw[th]],
                                             wap[:, cc, 128 * fo:128 * (fo + 1)],
                                             scrA[:, cc * T + 512 * th:cc * T + 512 * th + thw[th]],
                                             start=(cc == 0), stop=(cc == CC - 1))
                            t.inc(TQ.h)
                        TQ.n += 1
                        tq_mm = TQ.n

                        for th in range(TH):
                            ps = pr[th]

                            @block.vector
                            def _(v):
                                v.wait_ge(TQ.h, tq_mm)
                                v.wait_ge(SST[sl].h, 16 * stg_ct[sl])
                                if bias_off is not None:
                                    v.tensor_scalar(stg[sl][:, 512 * th:512 * th + thw[th]],
                                                    ps[:, 0:thw[th]],
                                                    bqkv_fm[:, bias_off + fglob:bias_off + fglob + 1],
                                                    scale, ALU.add, ALU.mult).then_inc(VQ.h, 1)
                                else:
                                    v.tensor_scalar(stg[sl][:, 512 * th:512 * th + thw[th]],
                                                    ps[:, 0:thw[th]], scale, None,
                                                    ALU.mult).then_inc(VQ.h, 1)
                            VQ.n += 1
                            mark_free(ps, VQ)
                        sq_evac = VQ.n
                        dma_wait_sem, dma_wait_tok = VQ, sq_evac

                        if rope:
                            vq_t0 = []
                            for th in range(TH):
                                pw = pws[th % 2]

                                @block.tensor
                                def _(t):
                                    t = _TT(t)
                                    t.wait_ge(VQ.h, sq_evac)
                                    fr = ps_free[id(pw)]
                                    t.wait_ge(fr[0].h, fr[1])
                                    t.matmul(pw[:, 0:thw[th]], pswap[:, :],
                                             stg[sl][:, 512 * th:512 * th + thw[th]],
                                             start=True, stop=True)
                                    t.inc(TQ.h)
                                TQ.n += 1
                                tq_sw = TQ.n

                                @block.vector
                                def _(v):
                                    v.wait_ge(TQ.h, tq_sw)
                                    v.wait_ge(VQ.h, VQ.n)
                                    v.scalar_tensor_tensor(
                                        ptsb[:, 512 * th:512 * th + thw[th]],
                                        pw[:, 0:thw[th]], 1.0,
                                        stab[:, 512 * th:512 * th + thw[th]],
                                        ALU.mult, ALU.mult).then_inc(VQ.h, 1)
                                VQ.n += 1
                                mark_free(pw, VQ)
                            vq_ropeT = VQ.n

                            @block.gpsimd
                            def _(g):
                                g.wait_ge(VQ.h, sq_evac)
                                g.wait_ge(VQ.h, VQ.n)
                                g.wait_ge(TQ.h, TQ.n)
                                g.tensor_mul(psb[:, 0:T], stg[sl][:, 0:T],
                                             ctab[:, 0:T]).then_inc(PQ.h, 1)
                            PQ.n += 1
                            pq_u = PQ.n
                            tq_swaps = TQ.n

                            @block.vector
                            def _(v):
                                v.wait_ge(PQ.h, pq_u)
                                v.wait_ge(VQ.h, vq_ropeT)
                                v.wait_ge(TQ.h, tq_swaps)
                                v.tensor_tensor(stg[sl][:, 0:T], psb[:, 0:T],
                                                ptsb[:, 0:T], ALU.add).then_inc(VQ.h, 1)
                            VQ.n += 1
                            dma_wait_sem, dma_wait_tok = VQ, VQ.n

                        @block.sync
                        def _(sy, sem=dma_wait_sem, tok=dma_wait_tok):
                            sy.wait_ge(sem.h, tok)
                            sy.dma_start(out=out_dram.ap()[128 * fglob:128 * (fglob + 1), :],
                                         in_=stg[sl][:, 0:T]).then_inc(SST[sl].h, 16)
                        SST[sl].n += 16
                        stg_ct[sl] += 1
                    release_slot(slot_i, TQ)

            if isA:
                qk_proj(wqkv, 0, D, qa_i, 0, SCALE_A, False)
                qk_proj(wqkv, D, D, ka_i, CC, SCALE_A, False)
            else:
                qk_proj(wq_b, 0, D, qa_i, None, SCALE_B, True)
                qk_proj(wk_b, 0, c.DKV, ka_i, None, SCALE_B, True)
                psb_free[0] = psb_free[1] = (VQ, VQ.n)
                ptsb_free[0] = ptsb_free[1] = (VQ, VQ.n)

            vcols = D if isA else c.DKV
            wpd = WCTA if isA else 64
            for blk in range(vcols // 512):
                slot_i = next_slot()
                wsrc = (wqkv.ap()[:, 2 * D + 512 * blk:2 * D + 512 * (blk + 1)] if isA
                        else wv_b.ap()[:, 512 * blk:512 * (blk + 1)])
                wtok = wload(slot_i, wsrc.rearrange("(cc p) f -> p cc f", p=128), CC, 512)
                for tp in range(cdiv(NT, 2)):
                    tiles = [t_ for t_ in (2 * tp, 2 * tp + 1) if t_ < NT]
                    pr = (pps[0], pps[1]) if tp % 2 == 0 else (pps[2], pps[3])

                    @block.tensor
                    def _(t):
                        t = _TT(t)
                        t.wait_ge(GW[slot_i].h, wtok)
                        t.wait_ge(VQ.h, max(h_v, bias_split_v))
                        for ti in range(len(tiles)):
                            fr = ps_free[id(pr[ti])]
                            t.wait_ge(fr[0].h, fr[1])
                        wap = wview(slot_i, CC, 512)
                        for cc in range(CC):
                            for ti, tile in enumerate(tiles):
                                t.matmul(pr[ti][:, 0:512],
                                         scrA[:, cc * T + tile * 128:cc * T + (tile + 1) * 128],
                                         wap[:, cc, 0:512],
                                         start=(cc == 0), stop=(cc == CC - 1) and not isA)
                        if isA:
                            for ti in range(len(tiles)):
                                t.matmul(pr[ti][:, 0:512], ones65[0:1, :],
                                         bias_rows[0:1, 512 * blk:512 * (blk + 1)],
                                         start=False, stop=False)
                                t.matmul(pr[ti][:, 0:512], ones65[0:1, :],
                                         bias_rows[0:1, D + 512 * blk:D + 512 * (blk + 1)],
                                         start=False, stop=True)
                        t.inc(TQ.h)
                    TQ.n += 1
                    tq_mm = TQ.n

                    for ti, tile in enumerate(tiles):
                        ps = pr[ti]
                        vsl = 0

                        @block.vector
                        def _(v):
                            v.wait_ge(TQ.h, tq_mm)
                            v.wait_ge(SVT[vsl].h, 16 * vstg_ct[vsl])
                            v.tensor_copy(vstg[vsl][:, 0:512], ps[:, 0:512]).then_inc(VQ.h, 1)
                        VQ.n += 1
                        mark_free(ps, VQ)
                        sq_ev = VQ.n
                        nd = 512 // wpd

                        @block.sync
                        def _(sy):
                            sy.wait_ge(VQ.h, sq_ev)
                            for dd in range(nd):
                                d = (512 * blk) // wpd + dd
                                sy.dma_start(
                                    out=va_i.ap()[T * d + 128 * tile:T * d + 128 * (tile + 1), :],
                                    in_=vstg[vsl][:, wpd * dd:wpd * (dd + 1)]).then_inc(SVT[vsl].h, 16)
                        SVT[vsl].n += 16 * nd
                        vstg_ct[vsl] += nd
                release_slot(slot_i, TQ)

        @block.gpsimd
        def _(g):
            for s in (SST[0], SST[1], SVT[0], SVT[1]):
                g.wait_ge(s.h, s.n)
            with allow_ici_all_to_all():
                for i_, o_ in [(qa_i, qa_o), (ka_i, ka_o), (va_i, va_o)]:
                    g.collective_compute("AllToAll", ALU.bypass,
                                         replica_groups=[list(range(NCORE))],
                                         ins=[i_.ap()], outs=[o_.ap()]).then_inc(CCS.h, 1)
        CCS.n += 3
        cc_qkv = CCS.n

        hd = 128 if isA else 64
        nqh = 1 if isA else c.G
        units = ([(b, j) for b in range(c.B) for j in range(c.SLOTS_A)] if isA
                 else [(b, 0) for b in range(c.B)])

        with nc.psum_tensor(_u("ps_sc"), [128, 2048], F32) as ss2, \
             nc.psum_tensor(_u("ps_t0"), [128, 512], BF16) as pt0, \
             nc.psum_tensor(_u("ps_t1"), [128, 512], BF16) as pt1, \
             nc.psum_tensor(_u("ps_c0"), [128, hd + 1], F32) as cx0, \
             nc.psum_tensor(_u("ps_c1"), [128, hd + 1], F32) as cx1:
            pts, cxs = [pt0, pt1], [cx0, cx1]
            for p in pts + cxs:
                init_free(p)
            ss_free = [(TQ, 0), (TQ, 0)]
            sctr = [0]
            tctr = [0]

            for ui, (b, j) in enumerate(units):
                @block.sync
                def _(sy):
                    sy.wait_ge(CCS.h, cc_qkv)
                    sy.wait_ge(TQ.h, max(scrB_rel[0], cvec_rel[0]))
                    sy.wait_ge(VQ.h, scrB_vrel[0])
                    for s_ in SYL:
                        sy.wait_ge(s_.h, s_.n)
                    ns = c.NSRC
                    if isA:
                        sy.dma_start(out=scrB[0:3, CVOFF:CVOFF + S],
                                     in_=t_cvec.ap()[j]).then_inc(SCV.h, 16)
                    if isA:
                        inq = bass.AP(qa_o, (c.SLOTS_A * ns * b + j) * 128 * T,
                                      [[T, 128], [c.SLOTS_A * 128 * T, ns], [1, T]])
                        sy.dma_start(out=bass.AP(scrB, 0, [[SCRB, 128], [T, ns], [1, T]]),
                                     in_=inq).then_inc(SQB.h, 16)
                        ink = bass.AP(ka_o, (c.SLOTS_A * ns * b + j) * 128 * T,
                                      [[T, 128], [c.SLOTS_A * 128 * T, ns], [1, T]])
                        sy.dma_start(out=bass.AP(scrB, S, [[SCRB, 128], [T, ns], [1, T]]),
                                     in_=ink).then_inc(SKB.h, 16)
                        inv = bass.AP(va_o, (b * S) * WCTA + j * 128,
                                      [[WCTA, 128], [128 * WCTA, S // 128], [1, 128]])
                        sy.dma_start(out=bass.AP(scrB, 2 * S, [[SCRB, 128], [128, S // 128], [1, 128]]),
                                     in_=inv).then_inc(SVB.h, 16)
                    else:
                        for g2 in range(c.G // 2):
                            inq = bass.AP(qa_o, (WCTB * ns * b + g2 * 128) * T,
                                          [[T, 128], [WCTB * T, ns], [1, T]])
                            sy.dma_start(out=bass.AP(scrB, g2 * S, [[SCRB, 128], [T, ns], [1, T]]),
                                         in_=inq).then_inc(SQB.h, 16)
                        ink = bass.AP(ka_o, (64 * ns * b) * T,
                                      [[T, 64], [64 * T, ns], [1, T]])
                        sy.dma_start(out=bass.AP(scrB, KOFF, [[SCRB, 64], [T, ns], [1, T]]),
                                     in_=ink).then_inc(SKB.h, 16)
                        sy.dma_start(out=bass.AP(scrB, 64 * SCRB + KOFF, [[SCRB, 64], [T, ns], [1, T]]),
                                     in_=ink).then_inc(SKB.h, 16)
                        inv = bass.AP(va_o, (b * S) * 64,
                                      [[64, 128], [128 * 64, S // 128], [1, 64]])
                        sy.dma_start(out=bass.AP(scrB, 2 * S, [[SCRB, 128], [64, S // 128], [1, 64]]),
                                     in_=inv).then_inc(SVB.h, 16)
                if isA:
                    SCV.n += 16
                SQB.n += 16 * (1 if isA else c.G // 2)
                SKB.n += 16 if isA else 32
                SVB.n += 16
                qtok, ktok, vtok = SQB.n, SKB.n, SVB.n
                cvtok = SCV.n
                def norm_row(row_sb, src_fn, pbase, pcnt, ld_sem, ld_tok, kscale):
                    for ch in range(S // 512):
                        def f(s_, ch=ch, ld_sem=ld_sem, ld_tok=ld_tok, tqn=TQ.n,
                              pbase=pbase, pcnt=pcnt, src_fn=src_fn,
                              pfr=psb_free[0]):
                            s_.wait_ge(ld_sem.h, ld_tok)
                            s_.wait_ge(TQ.h, tqn)
                            if pfr[1]:
                                s_.wait_ge(pfr[0].h, pfr[1])
                            s_.activation(out=psb[pbase:pbase + pcnt, 0:512],
                                          in_=src_fn(ch), func=AF.Square).then_inc(SQ.h, 1)
                        sc(f)
                        SQ.n += 1
                        sq_sq = SQ.n

                        @block.tensor
                        def _(t):
                            t = _TT(t)
                            t.wait_ge(SQ.h, sq_sq)
                            t.wait_ge(VQ.h, VQ.n)
                            fr = ss_free[0]
                            t.wait_ge(fr[0].h, fr[1])
                            t.matmul(ss2[0:1, 0:512], maskT[pbase:pbase + pcnt, 127:128],
                                     psb[pbase:pbase + pcnt, 0:512],
                                     start=True, stop=True)
                            t.inc(TQ.h)
                        TQ.n += 1
                        psb_free[0] = (TQ, TQ.n)

                        @block.vector
                        def _(v):
                            v.wait_ge(TQ.h, TQ.n)
                            v.wait_ge(VQ.h, VQ.n)
                            if kscale:
                                v.tensor_scalar(row_sb[0:1, 512 * ch:512 * (ch + 1)],
                                                ss2[0:1, 0:512], knm[0:1, 0:1], None,
                                                ALU.mult).then_inc(VQ.h, 1)
                            else:
                                v.tensor_copy(row_sb[0:1, 512 * ch:512 * (ch + 1)],
                                              ss2[0:1, 0:512]).then_inc(VQ.h, 1)
                        VQ.n += 1
                        ss_free[0] = (VQ, VQ.n)

                if isA:
                    norm_row(rowscr, lambda ch: scrB[:, S + 512 * ch:S + 512 * (ch + 1)],
                             0, 128, SKB, ktok, False)
                else:
                    norm_row(rowscr, lambda ch: scrB[0:64, KOFF + 512 * ch:KOFF + 512 * (ch + 1)],
                             0, 64, SKB, ktok, False)

                @block.vector
                def _(v):
                    v.wait_ge(VQ.h, VQ.n)
                    v.reduce_max(knm[:, :], rowscr[0:1, 0:S], axis=AX.X).then_inc(VQ.h, 1)
                VQ.n += 1

                for qh in range(nqh):
                    if isA:
                        norm_row(rowscr, lambda ch: scrB[:, 512 * ch:512 * (ch + 1)],
                                 0, 128, SQB, qtok, True)
                    else:
                        g2, qb = qh // 2, 64 * (qh % 2)
                        norm_row(rowscr,
                                 lambda ch, g2=g2, qb=qb: scrB[qb:qb + 64,
                                                               g2 * S + 512 * ch:g2 * S + 512 * (ch + 1)],
                                 qb, 64, SQB, qtok, True)
                    vq_qnk = VQ.n

                    ptR = pts[0]

                    @block.tensor
                    def _(t):
                        t = _TT(t)
                        t.wait_ge(VQ.h, vq_qnk)
                        fr = ps_free[id(ptR)]
                        t.wait_ge(fr[0].h, fr[1])
                        for qt in range(c.QT):
                            t.transpose(ptR[:, 2 * qt:2 * qt + 1],
                                        rowscr[0:1, 128 * qt:128 * (qt + 1)],
                                        ones1[0:1, 0:1])
                        t.inc(TQ.h)
                    TQ.n += 1
                    tq_R = TQ.n

                    def f(s_, tq_R=tq_R, vqn=VQ.n, ptR=ptR):
                        s_.wait_ge(TQ.h, tq_R)
                        s_.wait_ge(VQ.h, vqn)
                        s_.activation(out=rcol[:, :],
                                      in_=bass.AP(ptR, 0, [[512, 128], [2, c.QT]]),
                                      func=AF.Sqrt).then_inc(SQ.h, 1)
                    sc(f)
                    SQ.n += 1
                    mark_free(ptR, SQ)
                    sq_R = SQ.n

                    @block.vector
                    def _(v):
                        v.wait_ge(SQ.h, sq_R)
                        v.wait_ge(TQ.h, TQ.n)
                        if isA:
                            v.scalar_tensor_tensor(negR[:, :], rcol[:, :], -1.0,
                                                   slopei[:, j * c.QT:(j + 1) * c.QT],
                                                   ALU.mult, ALU.subtract).then_inc(VQ.h, 1)
                        else:
                            v.tensor_scalar(negR[:, :], rcol[:, :], -1.0, None,
                                            ALU.mult).then_inc(VQ.h, 1)
                    VQ.n += 1
                    vq_negR = VQ.n

                    blocks = []
                    for qt in range(c.QT):
                        nk = (qt + 1) * 128
                        nb = cdiv(nk, 1024)
                        for kb in range(nb):
                            blocks.append({"qt": qt, "kb": kb,
                                           "w": min(1024, nk - 1024 * kb),
                                           "diag": kb == nb - 1})
                    NB = len(blocks)

                    def emit_score(B):
                        sl = sctr[0] % 2
                        sctr[0] += 1
                        B["sl"] = sl
                        soff = 1024 * sl
                        B["soff"] = soff
                        qt, kb, w = B["qt"], B["kb"], B["w"]

                        @block.tensor
                        def _(t):
                            t = _TT(t)
                            fr = ss_free[sl]
                            t.wait_ge(fr[0].h, fr[1])
                            t.wait_ge(VQ.h, vq_negR)
                            t.wait_ge(SVB.h, vtok)
                            if isA:
                                t.wait_ge(SCV.h, cvtok)
                            nh2 = cdiv(w, 512)
                            for h2 in range(nh2):
                                wh = min(512, w - 512 * h2)
                                kb512 = 1024 * kb + 512 * h2
                                if isA:
                                    t.matmul(ss2[:, soff + 512 * h2:soff + 512 * h2 + wh],
                                             scrB[:, 128 * qt:128 * (qt + 1)],
                                             scrB[:, S + kb512:S + kb512 + wh],
                                             start=True, stop=False)
                                else:
                                    g2, qb = qh // 2, 64 * (qh % 2)
                                    t.matmul(ss2[:, soff + 512 * h2:soff + 512 * h2 + wh],
                                             scrB[qb:qb + 64,
                                                  g2 * S + 128 * qt:g2 * S + 128 * (qt + 1)],
                                             scrB[qb:qb + 64, KOFF + kb512:KOFF + kb512 + wh],
                                             start=True, stop=not isA)
                            if isA:
                                for h2 in range(nh2):
                                    wh = min(512, w - 512 * h2)
                                    kb512 = 1024 * kb + 512 * h2
                                    t.matmul(ss2[:, soff + 512 * h2:soff + 512 * h2 + wh],
                                             ones3[:, :],
                                             scrB[0:3, CVOFF + kb512:CVOFF + kb512 + wh],
                                             start=False, stop=True)
                            t.inc(TQ.h)
                        TQ.n += 1
                        tq_s = TQ.n

                        vq_msk = 0
                        if isA and B["diag"]:
                            @block.vector
                            def _(v):
                                v.wait_ge(TQ.h, tq_s)
                                v.tensor_tensor(ss2[:, soff + w - 128:soff + w],
                                                ss2[:, soff + w - 128:soff + w],
                                                maskA[:, :], ALU.add).then_inc(VQ.h, 1)
                            VQ.n += 1
                            vq_msk = VQ.n

                        def f(s_, tq_s=tq_s, vq_msk=vq_msk, fr=psb_free[sl],
                              soff=soff, w=w, qt=qt):
                            s_.wait_ge(TQ.h, tq_s)
                            if vq_msk:
                                s_.wait_ge(VQ.h, vq_msk)
                            if fr[1]:
                                s_.wait_ge(fr[0].h, fr[1])
                            s_.activation(out=psb[:, soff:soff + w],
                                          in_=ss2[:, soff:soff + w], func=AF.Exp,
                                          bias=negR[:, qt:qt + 1], scale=1.0
                                          ).then_inc(SQ.h, 1)
                        sc(f)
                        SQ.n += 1
                        ss_free[sl] = (SQ, SQ.n)
                        B["sq_exp"] = SQ.n

                    def emit_trev(B):
                        qt, kb, w, soff, sl = B["qt"], B["kb"], B["w"], B["soff"], B["sl"]
                        B["groups"] = []
                        for g5 in range(cdiv(w, 512)):
                            wg = min(512, w - 512 * g5)
                            ptl = tctr[0] % 2
                            ptp = pts[ptl]
                            tctr[0] += 1

                            @block.tensor
                            def _(t):
                                t = _TT(t)
                                t.wait_ge(SQ.h, B["sq_exp"])
                                fr = ps_free[id(ptp)]
                                t.wait_ge(fr[0].h, fr[1])
                                for k in range(wg // 128):
                                    t.transpose(ptp[:, 128 * k:128 * (k + 1)],
                                                psb[:, soff + 512 * g5 + 128 * k:
                                                    soff + 512 * g5 + 128 * (k + 1)],
                                                ident[:, :])
                                t.inc(TQ.h)
                            TQ.n += 1
                            if g5 == cdiv(w, 512) - 1:
                                psb_free[sl] = (TQ, TQ.n)
                            tq_tr = TQ.n

                            @block.vector
                            def _(v):
                                v.wait_ge(TQ.h, tq_tr)
                                fr = ptsb_free[ptl]
                                if fr[1]:
                                    v.wait_ge(fr[0].h, fr[1])
                                v.tensor_copy(ptsb[:, 512 * ptl:512 * ptl + wg],
                                              ptp[:, 0:wg]).then_inc(VQ.h, 1)
                                ksub_hi = (1024 * kb + 512 * g5 + wg) // 128 - 1
                                if ksub_hi == qt and not isA:
                                    v.wait_ge(VQ.h, VQ.n + 1)
                                    dcol = 512 * ptl + wg - 128
                                    v.tensor_tensor(ptsb[:, dcol:dcol + 128],
                                                    ptsb[:, dcol:dcol + 128],
                                                    maskT[:, :], ALU.mult).then_inc(VQ.h, 1)
                            nvops = 2 if ((1024 * kb + 512 * g5 + wg) // 128 - 1 == qt
                                          and not isA) else 1
                            VQ.n += nvops
                            mark_free(ptp, VQ)
                            B["groups"].append((g5, wg, ptl, VQ.n))

                    def emit_ctx(B):
                        qt, kb = B["qt"], B["kb"]
                        cxp = cxs[qt % 2]
                        first = kb == 0
                        for (g5, wg, ptl, vq_pt) in B["groups"]:
                            @block.tensor
                            def _(t, first=first):
                                t = _TT(t)
                                t.wait_ge(VQ.h, vq_pt)
                                if first:
                                    fr = ps_free[id(cxp)]
                                    t.wait_ge(fr[0].h, fr[1])
                                for k in range(wg // 128):
                                    ksub = 8 * kb + 4 * g5 + k
                                    lhs = ptsb[:, 512 * ptl + 128 * k:512 * ptl + 128 * (k + 1)]
                                    t.matmul(cxp[:, 0:hd], lhs,
                                             scrB[:, 2 * S + hd * ksub:2 * S + hd * (ksub + 1)],
                                             start=first and k == 0, stop=False)
                                    t.matmul(cxp[:, hd:hd + 1], lhs, maskT[:, 127:128],
                                             start=False, stop=(ksub == qt))
                                t.inc(TQ.h)
                            TQ.n += 1
                            ptsb_free[ptl] = (TQ, TQ.n)
                            first = False
                        if B["diag"]:
                            emit_final(qt, cxp)

                    def emit_final(qt, cxp):
                        tq_ctx = TQ.n

                        @block.vector
                        def _(v):
                            v.wait_ge(TQ.h, tq_ctx)
                            if ctx_region_rel[0]:
                                v.wait_ge(SCX.h, ctx_region_rel[0])
                            v.reciprocal(lrec[:, :], cxp[:, hd:hd + 1]).then_inc(VQ.h, 1)
                            v.wait_ge(VQ.h, VQ.n + 1)
                            col = qt * wct + (j * 128 if isA else 64 * qh)
                            v.tensor_scalar(scrA[:, col:col + hd], cxp[:, 0:hd],
                                            lrec[:, :], None, ALU.mult).then_inc(VQ.h, 1)
                        VQ.n += 2
                        mark_free(cxp, VQ)

                    for i in range(NB + 2):
                        if i >= 2:
                            emit_ctx(blocks[i - 2])
                        if i < NB:
                            emit_score(blocks[i])
                        if i >= 1 and i - 1 < NB:
                            emit_trev(blocks[i - 1])

                last_of_batch = (j == c.SLOTS_A - 1) if isA else True
                if last_of_batch:
                    vq_unit = VQ.n

                    @block.sync
                    def _(sy):
                        sy.wait_ge(VQ.h, vq_unit)
                        outap = bass.AP(ca_i, (c.NSRC * b * T) * wct,
                                        [[wct, 128], [128 * wct, S // 128], [1, wct]])
                        inap = bass.AP(scrA, 0, [[CC * T, 128], [wct, S // 128], [1, wct]])
                        sy.dma_start(out=outap, in_=inap).then_inc(SCX.h, 16)
                    SCX.n += 16
                    ctx_region_rel[0] = SCX.n
                scrB_rel[0] = TQ.n
                cvec_rel[0] = TQ.n
            sc_flush()

        @block.gpsimd
        def _(g):
            g.wait_ge(SCX.h, SCX.n)
            with allow_ici_all_to_all():
                g.collective_compute("AllToAll", ALU.bypass,
                                     replica_groups=[list(range(NCORE))],
                                     ins=[ca_i.ap()], outs=[ca_o.ap()]).then_inc(CCS.h, 1)
        CCS.n += 1
        cc_ctx = CCS.n

        @block.sync
        def _(sy):
            sy.wait_ge(CCS.h, cc_ctx)
            sy.wait_ge(TQ.h, scrB_rel[0])
            sy.wait_ge(VQ.h, scrB_vrel[0])
            for tile in range(NT):
                inap = bass.AP(ca_o, tile * 128 * wct,
                               [[wct, 128], [T * wct, NCORE], [1, wct]])
                outap = bass.AP(scrB, tile * D,
                                [[SCRB, 128], [wct, NCORE], [1, wct]])
                sy.dma_start(out=outap, in_=inap).then_inc(SCX.h, 16)
        SCX.n += 16 * NT
        ctx_ld = SCX.n

        with nc.psum_tensor(_u("ps_ct0"), [128, 1024], BF16) as pc0, \
             nc.psum_tensor(_u("ps_ct1"), [128, 1024], BF16) as pc1:
            pcs = [pc0, pc1]
            rel = [0, 0]
            for tile in range(NT):
                for bb in range(CC // 8):
                    pi = (tile * (CC // 8) + bb) % 2
                    pb = pcs[pi]

                    @block.tensor
                    def _(t):
                        t = _TT(t)
                        t.wait_ge(SCX.h, ctx_ld)
                        t.wait_ge(VQ.h, VQ.n)
                        t.wait_ge(SQ.h, SQ.n)
                        if rel[pi]:
                            t.wait_ge(VQ.h, rel[pi])
                        for k in range(8):
                            cc = 8 * bb + k
                            t.transpose(pb[:, 128 * k:128 * (k + 1)],
                                        scrB[:, tile * D + 128 * cc:tile * D + 128 * (cc + 1)],
                                        ident[:, :])
                        t.inc(TQ.h)
                    TQ.n += 1
                    tq_tr = TQ.n

                    @block.vector
                    def _(v):
                        v.wait_ge(TQ.h, tq_tr)
                        outap = bass.AP(scrA, 8 * bb * T + tile * 128,
                                        [[CC * T, 128], [T, 8], [1, 128]])
                        v.tensor_copy(outap,
                                      bass.AP(pb, 0, [[1024, 128], [128, 8], [1, 128]])
                                      ).then_inc(VQ.h, 1)
                    VQ.n += 1
                    rel[pi] = VQ.n
        ctxfm_v = VQ.n
        scrB_rel[0] = TQ.n

        wo = wo_a if isA else wo_b
        brow = 32 if isA else 64
        with nc.psum_tensor(_u("ps_o0"), [128, 512], F32) as po0, \
             nc.psum_tensor(_u("ps_o1"), [128, 512], F32) as po1, \
             nc.psum_tensor(_u("ps_o2"), [128, 512], F32) as po2, \
             nc.psum_tensor(_u("ps_o3"), [128, 512], F32) as po3:
            pos_ = [po0, po1, po2, po3]
            for p in pos_:
                init_free(p)
            for blk in range(D // 512):
                slot_i = next_slot()
                wtok = wload(slot_i,
                             wo.ap()[:, 512 * blk:512 * (blk + 1)].rearrange("(cc p) f -> p cc f", p=128),
                             CC, 512)
                for tp in range(cdiv(NT, 2)):
                    tiles = [t_ for t_ in (2 * tp, 2 * tp + 1) if t_ < NT]
                    pr = (pos_[0], pos_[1]) if tp % 2 == 0 else (pos_[2], pos_[3])

                    @block.tensor
                    def _(t):
                        t = _TT(t)
                        t.wait_ge(GW[slot_i].h, wtok)
                        t.wait_ge(VQ.h, ctxfm_v)
                        for ti in range(len(tiles)):
                            fr = ps_free[id(pr[ti])]
                            t.wait_ge(fr[0].h, fr[1])
                        wap = wview(slot_i, CC, 512)
                        for cc in range(CC):
                            for ti, tile in enumerate(tiles):
                                t.matmul(pr[ti][:, 0:512],
                                         scrA[:, cc * T + tile * 128:cc * T + (tile + 1) * 128],
                                         wap[:, cc, 0:512], start=(cc == 0), stop=False)
                        for ti in range(len(tiles)):
                            t.matmul(pr[ti][:, 0:512], ones65[brow:brow + 1, :],
                                     bias_rows[brow:brow + 1, 512 * blk:512 * (blk + 1)],
                                     start=False, stop=False)
                            t.matmul(pr[ti][:, 0:512], ones65[brow:brow + 1, :],
                                     bias_rows[brow:brow + 1, D + 512 * blk:D + 512 * (blk + 1)],
                                     start=False, stop=True)
                        t.inc(TQ.h)
                    TQ.n += 1
                    tq_mm = TQ.n

                    for ti, tile in enumerate(tiles):
                        ps = pr[ti]

                        @block.vector
                        def _(v):
                            v.wait_ge(TQ.h, tq_mm)
                            v.tensor_tensor(x_sb[:, tile * D + 512 * blk:tile * D + 512 * (blk + 1)],
                                            x_sb[:, tile * D + 512 * blk:tile * D + 512 * (blk + 1)],
                                            ps[:, 0:512], ALU.add).then_inc(VQ.h, 1)
                        VQ.n += 1
                        mark_free(ps, VQ)
                release_slot(slot_i, TQ)

    def ffn_sublayer():
        h_v = ln_phase(dump_h=True)

        ECAPL = c.E * c.CAP
        pf_trel = [0]
        sit_base = [SIT[0].n, SIT[1].n]
        with nc.psum_tensor(_u("ps_r0"), [128, 512], F32) as pr0, \
             nc.psum_tensor(_u("ps_r1"), [128, 512], F32) as pr1, \
             nc.psum_tensor(_u("ps_pf"), [128, 32], F32) as ppf:
            prs = [pr0, pr1]
            for p in prs + [ppf]:
                init_free(p)
            for tile in range(NT):
                ps = prs[tile % 2]

                @block.tensor
                def _(t):
                    t = _TT(t)
                    t.wait_ge(VQ.h, max(h_v, startup_v))
                    fr = ps_free[id(ps)]
                    t.wait_ge(fr[0].h, fr[1])
                    EE = c.E + 1
                    for cc in range(CC):
                        t.matmul(ps[:, 0:EE],
                                 scrA[:, cc * T + tile * 128:cc * T + (tile + 1) * 128],
                                 routerT[:, cc * EEP:cc * EEP + EE],
                                 start=(cc == 0), stop=(cc == CC - 1))
                    t.inc(TQ.h)
                TQ.n += 1
                tq_log = TQ.n

                @block.vector
                def _(v):
                    v.wait_ge(TQ.h, tq_log)
                    v.reduce_max(nmx[:, :], ps[:, 0:c.E], axis=AX.X).then_inc(VQ.h, 1)
                    v.wait_ge(VQ.h, VQ.n + 1)
                    v.tensor_scalar(nmx[:, :], nmx[:, :], -1.0, None, ALU.mult).then_inc(VQ.h, 1)
                VQ.n += 2
                vq_mx = VQ.n

                def f(s_, vq_mx=vq_mx, ps=ps, zg=zfill_gate[0]):
                    s_.wait_ge(VQ.h, vq_mx)
                    if zg:
                        s_.wait_ge(SIX.h, zg)
                    s_.activation(out=expv[:, :], in_=ps[:, 0:c.E], func=AF.Exp,
                                  bias=nmx[:, :], scale=1.0).then_inc(SQ.h, 1)
                sc(f)
                SQ.n += 1
                sq_e = SQ.n

                @block.vector
                def _(v):
                    v.wait_ge(SQ.h, sq_e)
                    v.tensor_copy(sglog[:, tile:tile + 1], ps[:, c.E:c.E + 1]).then_inc(VQ.h, 1)
                VQ.n += 1
                mark_free(ps, VQ)

                @block.vector
                def _(v):
                    v.wait_ge(SQ.h, sq_e)
                    v.wait_ge(TQ.h, pf_trel[0])
                    n0 = VQ.n
                    v.tensor_copy(mskv[:, :], expv[:, :]).then_inc(VQ.h, 1)
                    for k in range(4):
                        v.wait_ge(VQ.h, n0 + 1 + 3 * k)
                        v.reduce_max(mml[:, k:k + 1], mskv[:, :], axis=AX.X).then_inc(VQ.h, 1)
                        if k < 3:
                            v.wait_ge(VQ.h, n0 + 2 + 3 * k)
                            v.tensor_scalar(ge01[:, :], mskv[:, :], mml[:, k:k + 1], -1e9,
                                            ALU.is_ge, ALU.mult).then_inc(VQ.h, 1)
                            v.wait_ge(VQ.h, n0 + 3 + 3 * k)
                            v.tensor_tensor(mskv[:, :], mskv[:, :], ge01[:, :],
                                            ALU.add).then_inc(VQ.h, 1)
                    v.wait_ge(VQ.h, n0 + 11)
                    v.tensor_tensor(mml[:, 4:5], mml[:, 0:1], mml[:, 1:2], ALU.add).then_inc(VQ.h, 1)
                    v.wait_ge(VQ.h, n0 + 12)
                    v.tensor_tensor(mml[:, 4:5], mml[:, 4:5], mml[:, 2:3], ALU.add).then_inc(VQ.h, 1)
                    v.wait_ge(VQ.h, n0 + 13)
                    v.tensor_tensor(mml[:, 4:5], mml[:, 4:5], mml[:, 3:4], ALU.add).then_inc(VQ.h, 1)
                    v.wait_ge(VQ.h, n0 + 14)
                    v.reciprocal(mml[:, 5:6], mml[:, 4:5]).then_inc(VQ.h, 1)
                    v.tensor_scalar(ge01[:, :], expv[:, :], mml[:, 3:4], None,
                                    ALU.is_ge).then_inc(VQ.h, 1)
                    v.wait_ge(VQ.h, n0 + 16)
                    v.tensor_tensor(mskv[:, :], expv[:, :], ge01[:, :], ALU.mult).then_inc(VQ.h, 1)
                    v.wait_ge(VQ.h, n0 + 17)
                    v.tensor_scalar(comb[:, tile * c.E:(tile + 1) * c.E], mskv[:, :],
                                    mml[:, 5:6], None, ALU.mult).then_inc(VQ.h, 1)
                VQ.n += 18
                vq_ge = VQ.n - 2

                frp = ps_free[id(ppf)]

                @block.tensor
                def _(t):
                    t = _TT(t)
                    t.wait_ge(VQ.h, vq_ge)
                    t.wait_ge(frp[0].h, frp[1])
                    t.matmul(ppf[:, 0:16], utri[:, :], ge01[:, :],
                             start=True, stop=True)
                    t.matmul(ppf[:, 16:32], utri[:, :], ge01[:, :],
                             start=True, stop=False)
                    t.matmul(ppf[:, 16:32], ones65[0:1, :],
                             ecrow_sb[0:1, 16 * tile:16 * tile + 16],
                             start=False, stop=True)
                    t.inc(TQ.h)
                TQ.n += 1
                pf_trel[0] = TQ.n
                pf_tq = TQ.n

                @block.vector
                def _(v):
                    n1 = VQ.n
                    v.wait_ge(TQ.h, pf_tq)
                    v.tensor_scalar(gidx_f[:, :], ppf[:, 0:16], c.CT - 0.5, 1e7,
                                    ALU.is_ge, ALU.mult).then_inc(VQ.h, 1)
                    v.wait_ge(VQ.h, n1 + 1)
                    v.tensor_tensor(gidx_f[:, :], gidx_f[:, :], ppf[:, 16:32],
                                    ALU.add).then_inc(VQ.h, 1)
                    v.wait_ge(VQ.h, n1 + 2)
                    v.scalar_tensor_tensor(gidx_f[:, :], ge01[:, :], -1e7,
                                           gidx_f[:, :], ALU.mult,
                                           ALU.add).then_inc(VQ.h, 1)
                    v.wait_ge(VQ.h, n1 + 3)
                    v.tensor_scalar(gidx_f[:, :], gidx_f[:, :], 1e7, -1.0,
                                    ALU.add, ALU.mult).then_inc(VQ.h, 1)
                    for r in range(4):
                        v.wait_ge(VQ.h, n1 + 5 * r + 4)
                        v.reduce_max(gmn[:, 0:1], gidx_f[:, :],
                                     axis=AX.X).then_inc(VQ.h, 1)
                        v.wait_ge(VQ.h, n1 + 5 * r + 5)
                        v.tensor_scalar(mskv[:, :], gidx_f[:, :], gmn[:, 0:1],
                                        -3e7, ALU.is_ge,
                                        ALU.mult).then_inc(VQ.h, 1)
                        v.wait_ge(VQ.h, n1 + 5 * r + 6)
                        v.tensor_tensor(gidx_f[:, :], gidx_f[:, :], mskv[:, :],
                                        ALU.add).then_inc(VQ.h, 1)
                        v.wait_ge(VQ.h, n1 + 5 * r + 5)
                        v.tensor_scalar(gmn[:, 1:2], gmn[:, 0:1], -1.0,
                                        float(ECAPL + 4 * tile + r), ALU.mult,
                                        ALU.min).then_inc(VQ.h, 1)
                        v.wait_ge(VQ.h, n1 + 5 * r + 8)
                        v.tensor_copy(gmnu[:, 4 * tile + r:4 * tile + r + 1],
                                      gmn[:, 1:2]).then_inc(VQ.h, 1)
                VQ.n += 24
                mark_free(ppf, VQ)
                vq_idx = VQ.n

                sl = tile % 2
                if True:
                    @block.sync
                    def _(sy):
                        sy.wait_ge(TQ.h, ln_state["stg_trel"][sl])
                        sy.wait_ge(SST[sl].h, 16 * stg_ct[sl])
                        if tile >= 2:
                            sy.wait_ge(SIT[sl].h,
                                       sit_base[sl] + 8 * 16 * (tile // 2))
                        sy.dma_start(out=stg[sl][:, 0:D],
                                     in_=h_dram.ap()[128 * tile:128 * (tile + 1), :]
                                     ).then_inc(SST[sl].h, 16)
                    SST[sl].n += 16
                    stg_ct[sl] += 1
                sst_tok = 16 * stg_ct[sl]

                @block.gpsimd
                def _(g):
                    g.wait_ge(VQ.h, vq_idx)
                    g.wait_ge(SST[sl].h, sst_tok)
                    g.wait_ge(SST[1 - sl].h, 16 * stg_ct[1 - sl])
                    g.wait_ge(SIX.h, hg_rel[0])
                    g.wait_ge(SCS.h, scs_rel[0])
                    for r in range(4):
                        g.indirect_dma_start(
                            hgath.ap(),
                            bass.IndirectOffsetOnAxis(
                                ap=gmnu[:, 4 * tile + r:4 * tile + r + 1], axis=0),
                            stg[sl][:, 0:D], None).then_inc(SIT[sl].h, 16)
                        g.indirect_dma_start(
                            comb_sl.ap(),
                            bass.IndirectOffsetOnAxis(
                                ap=gmnu[:, 4 * tile + r:4 * tile + r + 1], axis=0),
                            comb[:, tile * c.E:(tile + 1) * c.E],
                            None).then_inc(SIT[sl].h, 16)
                SIT[sl].n += 8 * 16
            sc_flush()
        vq_sg = VQ.n

        def f(s_, vq_sg=vq_sg):
            s_.wait_ge(VQ.h, vq_sg)
            s_.activation(out=sg_sb[:, :], in_=sglog[:, :],
                          func=AF.Sigmoid).then_inc(SQ.h, 1)
        sc(f)
        SQ.n += 1

        comb_v = VQ.n
        ffn_entry_s = SQ.n

        with nc.psum_tensor(_u("ps_m0"), [128, 512], F32) as pm0, \
             nc.psum_tensor(_u("ps_m1"), [128, 512], F32) as pm1, \
             nc.psum_tensor(_u("ps_m2"), [128, 512], F32) as pm2, \
             nc.psum_tensor(_u("ps_m3"), [128, 512], F32) as pm3, \
             nc.psum_tensor(_u("ps_d0"), [128, 512], F32) as pd0, \
             nc.psum_tensor(_u("ps_d1"), [128, 512], F32) as pd1, \
             nc.psum_tensor(_u("ps_t0"), [128, 1024], BF16) as pt0, \
             nc.psum_tensor(_u("ps_t1"), [128, 1024], BF16) as pt1:
            pms, pds = [pm0, pm1, pm2, pm3], [pd0, pd1]
            pts = [pt0, pt1]
            for p in pms + pds:
                init_free(p)
            sil_free = {i: (VQ, 0) for i in range(NSIL)}

            def gu_phase(gsrc, usrc, iblks, rebase):
                pending = []
                gct, uct = [0], [0]
                iclct = [0]
                for (ioff, iw) in iblks:
                    nic = iw // 128
                    for which in range(2):
                        slot_i = next_slot()
                        src_ = (gsrc if which == 0 else usrc)(ioff, iw)
                        wtok = wload(slot_i, src_, CC, iw)
                        for icl in range(nic):
                            ic = ioff // 128 + icl
                            pr = (pms[0], pms[1]) if iclct[0] % 2 == 0 else (pms[2], pms[3])
                            iclct[0] += 1

                            @block.tensor
                            def _(t):
                                t = _TT(t)
                                t.wait_ge(GW[slot_i].h, wtok)
                                t.wait_ge(VQ.h, max(h_v, comb_v))
                                t.wait_ge(SQ.h, ffn_entry_s)
                                for th in range(TH):
                                    fr = ps_free[id(pr[th])]
                                    t.wait_ge(fr[0].h, fr[1])
                                wap = wview(slot_i, CC, iw)
                                for cc in range(CC):
                                    for th in range(TH):
                                        t.matmul(pr[th][:, 0:thw[th]],
                                                 wap[:, cc, 128 * icl:128 * (icl + 1)],
                                                 scrA[:, cc * T + 512 * th:cc * T + 512 * th + thw[th]],
                                                 start=(cc == 0), stop=(cc == CC - 1))
                                t.inc(TQ.h)
                            TQ.n += 1
                            tq_g = TQ.n

                            for th in range(TH):
                                ps = pr[th]
                                if which == 0:
                                    sl = gct[0] % NSIL

                                    def f(s_, tqw=max(tq_g, scrB_rel[0]),
                                          fr=sil_free[sl], sl=sl, th=th, ps=ps):
                                        s_.wait_ge(TQ.h, tqw)
                                        if fr[1]:
                                            s_.wait_ge(fr[0].h, fr[1])
                                        s_.activation(out=silsl(sl, thw[th]),
                                                      in_=ps[:, 0:thw[th]],
                                                      func=AF.Silu).then_inc(SQ.h, 1)
                                    sc(f)
                                    SQ.n += 1
                                    mark_free(ps, SQ)
                                    pending.append((ic, th, sl, SQ.n))
                                    gct[0] += 1
                                else:
                                    ic0, th0, sl, sq_sil = pending.pop(0)
                                    col = (ic0 - rebase) * T + 512 * th0

                                    @block.vector
                                    def _(v):
                                        v.wait_ge(TQ.h, max(tq_g, scrB_rel[0]))
                                        v.wait_ge(SQ.h, sq_sil)
                                        v.scalar_tensor_tensor(
                                            scrB[:, col:col + thw[th0]],
                                            ps[:, 0:thw[th0]], 1.0, silsl(sl, thw[th0]),
                                            ALU.mult, ALU.mult).then_inc(VQ.h, 1)
                                    VQ.n += 1
                                    mark_free(ps, VQ)
                                    sil_free[sl] = (VQ, VQ.n)
                                    uct[0] += 1
                        release_slot(slot_i, TQ)
                return VQ.n

            def down_phase(dsrc, ncc, rebase, scal_fn):
                for fc in range(D // 512):
                    slot_i = next_slot()
                    wtok = wload(slot_i, dsrc(fc), ncc, 512)
                    for tp in range(cdiv(NT, 2)):
                        tiles = [t_ for t_ in (2 * tp, 2 * tp + 1) if t_ < NT]
                        pr = (pds[0], pds[1]) if tp % 2 == 0 else (pms[0], pms[1])

                        @block.tensor
                        def _(t):
                            t = _TT(t)
                            t.wait_ge(GW[slot_i].h, wtok)
                            t.wait_ge(VQ.h, act_v[0])
                            for ti in range(len(tiles)):
                                fr = ps_free[id(pr[ti])]
                                t.wait_ge(fr[0].h, fr[1])
                            wap = wview(slot_i, ncc, 512)
                            for icl in range(ncc):
                                for ti, tile in enumerate(tiles):
                                    t.matmul(pr[ti][:, 0:512],
                                             scrB[:, icl * T + tile * 128:icl * T + (tile + 1) * 128],
                                             wap[:, icl, 0:512],
                                             start=(icl == 0), stop=(icl == ncc - 1))
                            t.inc(TQ.h)
                        TQ.n += 1
                        tq_d = TQ.n

                        for ti, tile in enumerate(tiles):
                            ps = pr[ti]
                            scal = scal_fn(tile)

                            @block.vector
                            def _(v):
                                v.wait_ge(TQ.h, tq_d)
                                v.wait_ge(VQ.h, comb_v)
                                v.scalar_tensor_tensor(
                                    x_sb[:, tile * D + 512 * fc:tile * D + 512 * (fc + 1)],
                                    ps[:, 0:512], scal,
                                    x_sb[:, tile * D + 512 * fc:tile * D + 512 * (fc + 1)],
                                    ALU.mult, ALU.add).then_inc(VQ.h, 1)
                            VQ.n += 1
                            mark_free(ps, VQ)
                    release_slot(slot_i, TQ)

            act_v = [0]
            CAPL = c.CAP
            CAPT = cdiv(CAPL, 128)
            CW = min(128, CAPL)
            HGOFF = c.IC * CAPL
            NSIL2 = 5
            SIL2OFF = SCRB - NSIL2 * 512
            assert HGOFF + c.E * CAPL <= SIL2OFF, (HGOFF, c.E * CAPL, SIL2OFF)
            sil2_free = {i: (SYL[i], SYL[i].n) for i in range(NSIL2)}
            sil2_ct = [0]
            hgA_trel = [TQ.n, TQ.n]
            hgfm_rel = [0]
            act_trel = [0]
            cslot_vrel = [0]
            pt_vrel = [0, 0]
            sit_full = [SIT[0].n, SIT[1].n]
            for e in range(c.E):
                hg_tok = {}

                def hg_load(k):
                    sl = k % 2
                    cw = min(128, CAPL - 128 * k)

                    @block.sync
                    def _(sy):
                        if e == 0 and k < 2:
                            sy.wait_ge(SIT[0].h, sit_full[0])
                            sy.wait_ge(SIT[1].h, sit_full[1])
                        sy.wait_ge(TQ.h, hgA_trel[sl])
                        sy.wait_ge(SST[sl].h, 16 * stg_ct[sl])
                        sy.dma_start(out=stg[sl][0:cw, 0:D],
                                     in_=hgath.ap()[e * CAPL + 128 * k:
                                                    e * CAPL + 128 * k + cw, :]
                                     ).then_inc(SST[sl].h, 16)
                    SST[sl].n += 16
                    stg_ct[sl] += 1
                    hg_tok[k] = 16 * stg_ct[sl]

                for k in range(min(2, CAPT)):
                    hg_load(k)
                for k in range(CAPT):
                    sl = k % 2
                    cw = min(128, CAPL - 128 * k)
                    for bb in range(CC // 8):
                        pi = (k * (CC // 8) + bb) % 2
                        pt = pts[pi]

                        @block.tensor
                        def _(t):
                            t = _TT(t)
                            t.wait_ge(SST[sl].h, hg_tok[k])
                            t.wait_ge(VQ.h, pt_vrel[pi])
                            for j in range(8):
                                cc = 8 * bb + j
                                t.transpose(pt[:, 128 * j:128 * j + cw],
                                            stg[sl][0:cw, 128 * cc:128 * (cc + 1)],
                                            ident[0:cw, 0:cw])
                            t.inc(TQ.h)
                        TQ.n += 1
                        tq_tr = TQ.n

                        @block.vector
                        def _(v):
                            v.wait_ge(TQ.h, max(tq_tr, hgfm_rel[0]))
                            outap = bass.AP(scrB, HGOFF + 8 * bb * CAPL + 128 * k,
                                            [[SCRB, 128], [CAPL, 8], [1, cw]])
                            v.tensor_copy(outap,
                                          bass.AP(pt, 0, [[1024, 128], [128, 8], [1, cw]])
                                          ).then_inc(VQ.h, 1)
                        VQ.n += 1
                        pt_vrel[pi] = VQ.n
                    hgA_trel[sl] = TQ.n
                    if k + 2 < CAPT:
                        hg_load(k + 2)
                hgfm_v = VQ.n

                @block.sync
                def _(sy):
                    sy.wait_ge(VQ.h, cslot_vrel[0])
                    sy.dma_start(out=bass.AP(cslot, 0,
                                             [[CAPT * c.E, CW], [c.E, CAPT], [1, c.E]]),
                                 in_=comb_sl.ap()[e * CAPL:(e + 1) * CAPL, :]
                                 .rearrange("(s p) f -> p s f", p=CW)
                                 ).then_inc(SCS.h, 16)
                SCS.n += 16
                cslot_tok = SCS.n

                pending = []
                iclct = [0]
                for (ioff, iw) in c.IBLK:
                    nic = iw // 128
                    for which in range(2):
                        slot_i = next_slot()
                        wsrc = (w_gate if which == 0 else w_up)
                        wtok = wload(slot_i,
                                     wsrc.ap()[e, :, ioff:ioff + iw]
                                     .rearrange("(cc p) i -> p cc i", p=128),
                                     CC, iw)
                        for icl in range(nic):
                            ic = ioff // 128 + icl
                            pr = pms[iclct[0] % 4]
                            iclct[0] += 1

                            @block.tensor
                            def _(t):
                                t = _TT(t)
                                t.wait_ge(GW[slot_i].h, wtok)
                                t.wait_ge(VQ.h, hgfm_v)
                                fr = ps_free[id(pr)]
                                t.wait_ge(fr[0].h, fr[1])
                                wap = wview(slot_i, CC, iw)
                                for cc in range(CC):
                                    t.matmul(pr[:, 0:CAPL],
                                             wap[:, cc, 128 * icl:128 * (icl + 1)],
                                             scrB[:, HGOFF + cc * CAPL:
                                                  HGOFF + (cc + 1) * CAPL],
                                             start=(cc == 0), stop=(cc == CC - 1))
                                t.inc(TQ.h)
                            TQ.n += 1
                            tq_g = TQ.n

                            if which == 0:
                                sl2 = sil2_ct[0] % NSIL2
                                sil2_ct[0] += 1

                                def f(s_, tq_g=tq_g, fr2=sil2_free[sl2],
                                      sl2=sl2, pr=pr):
                                    s_.wait_ge(TQ.h, tq_g)
                                    if fr2[1]:
                                        s_.wait_ge(fr2[0].h, fr2[1])
                                    s_.activation(
                                        out=scrB[:, SIL2OFF + 512 * sl2:
                                                 SIL2OFF + 512 * sl2 + CAPL],
                                        in_=pr[:, 0:CAPL],
                                        func=AF.Silu).then_inc(SQ.h, 1)
                                sc(f)
                                SQ.n += 1
                                mark_free(pr, SQ)
                                pending.append((ic, sl2, SQ.n))
                            else:
                                ic0, sl2, sq_sil = pending.pop(0)

                                @block.vector
                                def _(v):
                                    v.wait_ge(TQ.h, max(tq_g, act_trel[0]))
                                    v.wait_ge(SQ.h, sq_sil)
                                    v.scalar_tensor_tensor(
                                        scrB[:, ic0 * CAPL:(ic0 + 1) * CAPL],
                                        pr[:, 0:CAPL], 1.0,
                                        scrB[:, SIL2OFF + 512 * sl2:
                                             SIL2OFF + 512 * sl2 + CAPL],
                                        ALU.mult, ALU.mult).then_inc(VQ.h, 1)
                                VQ.n += 1
                                mark_free(pr, VQ)
                                sil2_free[sl2] = (VQ, VQ.n)
                        release_slot(slot_i, TQ)
                hgfm_rel[0] = TQ.n
                vq_act = VQ.n

                for fc in range(D // 512):
                    slot_i = next_slot()
                    wtok = wload(slot_i,
                                 w_down.ap()[e, :, 512 * fc:512 * (fc + 1)]
                                 .rearrange("(ic p) f -> p ic f", p=128),
                                 c.IC, 512)
                    for st in range(CAPT):
                        cw = min(128, CAPL - 128 * st)
                        pr = pds[(fc * CAPT + st) % 2]

                        @block.tensor
                        def _(t):
                            t = _TT(t)
                            t.wait_ge(GW[slot_i].h, wtok)
                            t.wait_ge(VQ.h, vq_act)
                            fr = ps_free[id(pr)]
                            t.wait_ge(fr[0].h, fr[1])
                            wap = wview(slot_i, c.IC, 512)
                            for icl in range(c.IC):
                                t.matmul(pr[0:cw, 0:512],
                                         scrB[:, icl * CAPL + 128 * st:
                                              icl * CAPL + 128 * st + cw],
                                         wap[:, icl, 0:512],
                                         start=(icl == 0), stop=(icl == c.IC - 1))
                            t.inc(TQ.h)
                        TQ.n += 1
                        tq_d = TQ.n

                        sl2 = sil2_ct[0] % NSIL2
                        sil2_ct[0] += 1
                        fr2 = sil2_free[sl2]

                        @block.vector
                        def _(v):
                            v.wait_ge(TQ.h, tq_d)
                            v.wait_ge(SCS.h, cslot_tok)
                            if fr2[1]:
                                v.wait_ge(fr2[0].h, fr2[1])
                            v.tensor_scalar(
                                scrB[0:cw, SIL2OFF + 512 * sl2:
                                     SIL2OFF + 512 * sl2 + 512],
                                pr[0:cw, 0:512],
                                cslot[0:cw, st * c.E + e:st * c.E + e + 1], None,
                                ALU.mult).then_inc(VQ.h, 1)
                        VQ.n += 1
                        mark_free(pr, VQ)
                        vq_evac = VQ.n

                        @block.sync
                        def _(sy):
                            sy.wait_ge(VQ.h, vq_evac)
                            sy.wait_ge(SGA.h, y_sga_gate[0])
                            sy.dma_start(
                                out=y_dram.ap()[e * CAPL + 128 * st:
                                                e * CAPL + 128 * st + cw,
                                                512 * fc:512 * (fc + 1)],
                                in_=scrB[0:cw, SIL2OFF + 512 * sl2:
                                         SIL2OFF + 512 * sl2 + 512]
                                ).then_inc(SYL[sl2].h, 16)
                        SYL[sl2].n += 16
                        sil2_free[sl2] = (SYL[sl2], SYL[sl2].n)
                    release_slot(slot_i, TQ)
                act_trel[0] = TQ.n
                cslot_vrel[0] = VQ.n
            y_done = [(SYL[i], SYL[i].n) for i in range(NSIL2)]
            hg_rel[0] = SIX.n
            scs_rel[0] = SCS.n
            scrB_rel[0] = TQ.n
            scrB_vrel[0] = VQ.n
            for i in range(NSIL):
                if i >= 3:
                    sil_free[i] = y_done[i - 3]

            for (soff, sw) in c.SSEG:
                iblk_seg = []
                o = soff
                while o < soff + sw:
                    w_ = min(512, soff + sw - o)
                    iblk_seg.append((o, w_))
                    o += w_
                act_v[0] = gu_phase(
                    lambda ioff, iw: sh_gate.ap()[:, ioff:ioff + iw]
                    .rearrange("(cc p) i -> p cc i", p=128),
                    lambda ioff, iw: sh_up.ap()[:, ioff:ioff + iw]
                    .rearrange("(cc p) i -> p cc i", p=128),
                    iblk_seg, soff // 128)
                down_phase(
                    lambda fc, soff=soff, sw=sw: sh_down.ap()[soff:soff + sw, 512 * fc:512 * (fc + 1)]
                    .rearrange("(ic p) f -> p ic f", p=128),
                    sw // 128, soff // 128,
                    lambda tile: sg_sb[:, tile:tile + 1])
                scrB_rel[0] = TQ.n
                scrB_vrel[0] = VQ.n
            sc_flush()

            vq_shared = VQ.n
            sga_base = SGA.n
            for r in range(4):
                for tile in range(NT):
                    @block.gpsimd
                    def _(g):
                        if tile == 0:
                            if r == 0:
                                g.wait_ge(VQ.h, vq_shared)
                                for sem_, tok_ in y_done:
                                    g.wait_ge(sem_.h, tok_)
                            else:
                                g.wait_ge(SGA.h, sga_base + 16 * NT * r)
                        g.indirect_dma_start(
                            x_sb[:, tile * D:(tile + 1) * D], None,
                            y_dram.ap(),
                            bass.IndirectOffsetOnAxis(
                                ap=gmnu[:, 4 * tile + r:4 * tile + r + 1], axis=0),
                            compute_op=ALU.add, oob_is_err=True).then_inc(SGA.h, 16)
                    SGA.n += 16
            sga_rel[0] = SGA.n
            y_sga_gate[0] = SGA.n
            six_rel[0] = SIX.n
            sit_rel[0] = SIT[0].n
            sit_rel[1] = SIT[1].n

    attn_sublayer(True)
    ffn_sublayer()
    attn_sublayer(False)
    ffn_sublayer()

    sc_flush()

    @block.sync
    def _(sy):
        sy.wait_ge(VQ.h, VQ.n)
        sy.wait_ge(SGA.h, SGA.n)
        sy.wait_ge(SIX.h, SIX.n)
        sy.wait_ge(SIT[0].h, SIT[0].n)
        sy.wait_ge(SIT[1].h, SIT[1].n)
        sy.dma_start(out=out_t.ap().rearrange("(n p) d -> p n d", p=128),
                     in_=bass.AP(x_sb, 0, [[NT * D, 128], [D, NT], [1, D]])
                     ).then_inc(SOUT.h, 16)
        sy.wait_ge(SOUT.h, 16)

    st_.close()
    return nc


def host_tables(c: Cfg, core):
    import ml_dtypes
    bf = ml_dtypes.bfloat16
    t = {}
    t["t_ident"] = np.eye(128, dtype=np.float32).astype(bf)
    t["t_identf"] = np.eye(2 * c.D // 128, dtype=np.float32)
    kk2 = np.arange(128)
    t["t_utri"] = (kk2[:, None] < kk2[None, :]).astype(np.float32).astype(bf)
    NTl = c.T // 128
    ec = np.zeros((1, NTl * 16), np.float64)
    for tl in range(NTl):
        for e in range(16):
            ec[0, 16 * tl + e] = e * c.CAP + tl * c.CT
    t["t_ecrow"] = ec.astype(bf)
    assert np.all(ec.astype(bf).astype(np.float64) == ec), "ecrow must be bf16-exact"
    k_ = np.arange(128)[:, None]
    q_ = np.arange(128)[None, :]
    t["t_maskT"] = (k_ <= q_).astype(np.float32).astype(bf)
    t["t_maskA"] = np.where(q_.T >= k_.T, 0.0, -1e9).astype(np.float32).astype(bf)
    slopes = np.array([1.0 / 2 ** (k / 2) for k in range(c.HA)], np.float64) \
        if c.HA > 8 else np.array([1.0 / 2 ** k for k in range(c.HA)], np.float64)
    my_heads = [c.SLOTS_A * core + jj for jj in range(c.SLOTS_A)]
    jj_ = np.arange(c.S)
    cj = np.stack([slopes[h] * jj_.astype(np.float64) for h in my_heads])
    c0 = cj.astype(bf).astype(np.float64)
    c1 = (cj - c0).astype(bf).astype(np.float64)
    c2 = (cj - c0 - c1).astype(bf)
    t["t_cvec"] = np.stack([c0.astype(bf), c1.astype(bf), c2], axis=1)
    qt = np.arange(c.QT)
    pp = np.arange(128)
    si = np.stack([slopes[h] * (128 * qt[None, :] + pp[:, None]).astype(np.float64)
                   for h in my_heads], axis=0)
    t["t_slopei"] = np.ascontiguousarray(
        si.transpose(1, 0, 2).reshape(128, c.SLOTS_A * c.QT)).astype(np.float32).astype(bf)
    s0 = (core % c.NSRC) * c.T
    pos = np.arange(s0, s0 + c.T, dtype=np.float64)
    invf = 1.0 / (10000.0 ** (np.arange(0, c.DHB, 2, dtype=np.float64) / c.DHB))
    p = np.arange(128)
    ang = pos[None, :] * invf[p % (c.DHB // 2)][:, None]
    sgn = np.where((p % c.DHB) < (c.DHB // 2), -1.0, 1.0)
    t["t_ctab"] = np.cos(ang).astype(np.float32).astype(bf)
    t["t_stab"] = (np.sin(ang) * sgn[:, None]).astype(np.float32).astype(bf)
    P = np.zeros((128, 128), np.float32)
    r = np.arange(128)
    src = (r // c.DHB) * c.DHB + ((r + c.DHB // 2) % c.DHB)
    P[src, r] = 1.0
    t["t_pswap"] = P.astype(bf)
    return t


_CFG = Cfg()


def run(c: Cfg, x, weights):
    nc = build(c)
    in_maps = []
    for core in range(NCORE):
        b, s0 = core // c.NSRC, (core % c.NSRC) * c.T
        m = {"x_shard": np.ascontiguousarray(x[b, s0:s0 + c.T, :])}
        m.update(weights)
        m.update(host_tables(c, core))
        in_maps.append(m)
    res = run_bass_kernel_spmd(nc, in_maps, core_ids=list(range(NCORE)))
    out = np.empty((c.B, c.S, c.D), np.float32)
    for core in range(NCORE):
        b, s0 = core // c.NSRC, (core % c.NSRC) * c.T
        out[b, s0:s0 + c.T, :] = res.results[core]["out_shard"]
    return out


def kernel(L0_attn_a_x, L0_attn_a_Wqkv, L0_attn_a_bqkv, L0_attn_a_Wo, L0_attn_a_bo,
           L0_attn_b_Wq, L0_attn_b_Wk, L0_attn_b_Wv, L0_attn_b_Wo, L0_attn_b_b_out,
           L0_ffn_router_w, L0_ffn_w_gate, L0_ffn_w_up, L0_ffn_w_down,
           L0_ffn_sh_gate, L0_ffn_sh_up, L0_ffn_sh_down, L0_ffn_sh_gate_w):
    c = _CFG
    x = np.ascontiguousarray(np.asarray(L0_attn_a_x, np.float32))
    weights = {
        "Wqkv": L0_attn_a_Wqkv, "bqkv": L0_attn_a_bqkv, "Wo_a": L0_attn_a_Wo,
        "bo_a": L0_attn_a_bo, "Wq_b": L0_attn_b_Wq, "Wk_b": L0_attn_b_Wk,
        "Wv_b": L0_attn_b_Wv, "Wo_b": L0_attn_b_Wo, "bo_b": L0_attn_b_b_out,
        "router_w": L0_ffn_router_w, "w_gate": L0_ffn_w_gate, "w_up": L0_ffn_w_up,
        "w_down": L0_ffn_w_down, "sh_gate": L0_ffn_sh_gate, "sh_up": L0_ffn_sh_up,
        "sh_down": L0_ffn_sh_down, "sh_gate_w": L0_ffn_sh_gate_w,
    }
    weights = {k: np.ascontiguousarray(np.asarray(v, np.float32)) for k, v in weights.items()}
    return run(c, x, weights)
